# version 21
# baseline (speedup 1.0000x reference)
import contextlib
import numpy as np
import ml_dtypes
import concourse.bass as bass
import concourse.mybir as mybir
from concourse.bass_utils import run_bass_kernel_spmd

F32 = mybir.dt.float32
BF16 = mybir.dt.bfloat16
AF = mybir.ActivationFunctionType
ALU = mybir.AluOpType
AX = mybir.AxisListType

D = 1024
H = 16
DH = 64
W = 256
KC = 8
NEG = -1.0e9
EPS = 1e-5

W_Q, W_K, W_V, W_O, W_COS, W_SIN = 0, 1, 2, 3, 4, 5
_USE_CYCLE = [W_O, W_K, W_V, W_Q, W_COS, W_SIN]


def use_widx(use_id):
    if use_id < 3:
        return [W_K, W_V, W_Q][use_id]
    return _USE_CYCLE[(use_id - 3) % 6]


class Ctr:

    def __init__(self, sem, name):
        self.sem = sem
        self.name = name
        self.v = 0

    def inc(self, instr, n=1):
        instr.then_inc(self.sem, n)
        self.v += n
        return self.v


class KB:
    def __init__(self):
        self.ev = {}
        self.guards = {}
        self.waited = {}

    def wait(self, eng, ctr, val):
        if val <= 0:
            return
        key = (id(eng), ctr.name)
        if self.waited.get(key, 0) >= val:
            return
        eng.wait_ge(ctr.sem, val)
        self.waited[key] = val

    def bank_wait(self, eng, banks):
        for b in banks:
            for (c, v) in self.guards.get(b, []):
                self.wait(eng, c, v)

    def bank_set(self, banks, entries):
        for b in banks:
            self.guards[b] = list(entries)


def build_nc(seqs=2, nblk=8):
    S = nblk * W
    T = seqs * S
    nc = bass.Bass("TRN2", target_bir_lowering=False)

    x_d = nc.dram_tensor("x", [T, D], F32, kind="ExternalInput").ap()
    y_d = nc.dram_tensor("y", [T, D], F32, kind="ExternalOutput").ap()
    wsrc = [nc.dram_tensor(n, [D, D], F32, kind="ExternalInput").ap()
            for n in ["wqT", "wkT", "wvT", "woT"]]
    tha = nc.dram_tensor("tha", [D, D], F32, kind="ExternalInput").ap()
    thb = nc.dram_tensor("thb", [D, D], F32, kind="ExternalInput").ap()
    pha = nc.dram_tensor("pha", [D, D], F32, kind="ExternalInput").ap()
    phb = nc.dram_tensor("phb", [D, D], F32, kind="ExternalInput").ap()
    maskc_d = nc.dram_tensor("maskc", [128, 1152], BF16, kind="ExternalInput").ap()
    eyebf_d = nc.dram_tensor("eyebf", [128, 128], BF16, kind="ExternalInput").ap()
    eyef_d = nc.dram_tensor("eyef", [128, 128], F32, kind="ExternalInput").ap()
    consts_d = nc.dram_tensor("consts", [128, 8], F32, kind="ExternalInput").ap()
    wsc = nc.dram_tensor("wsc", [6, 2, 128, KC * D], BF16).ap()

    kb = KB()
    ev = kb.ev

    with contextlib.ExitStack() as ctx:
        E = ctx.enter_context
        sb = nc.sbuf_tensor
        x_sb = E(sb("x_sb", [128, 2048], F32))
        x_sb2 = E(sb("x_sb2", [128, 2048], F32))
        fmh2 = E(sb("fmh2", [128, 2048], BF16))
        fml2 = E(sb("fml2", [128, 2048], BF16))
        sc1 = E(sb("sc1", [128, 2048], F32))
        sc2 = E(sb("sc2", [128, 2048], F32))
        fmh = E(sb("fmh", [128, 2048], BF16))
        fml = E(sb("fml", [128, 2048], BF16))
        q_sb = E(sb("q_sb", [128, 2048], F32))
        k_sb = E(sb("k_sb", [128, 4096], F32))
        v_sb = E(sb("v_sb", [128, 4096], F32))
        attn_sb = E(sb("attn_sb", [128, 1536], F32))
        attnT_sb = E(sb("attnT_sb", [128, 1152], F32))
        ao_sb = E(sb("ao_sb", [128, 2048], F32))
        stats = E(sb("stats", [128, 256], F32))
        maskc = E(sb("maskc_s", [128, 1152], BF16))
        MOFF = {0: 0, 1: 384, 2: 768, 3: 896}
        eyebf = E(sb("eyebf_s", [128, 128], BF16))
        eyef = E(sb("eyef_s", [128, 128], F32))
        cst = E(sb("cst_s", [128, 8], F32))
        pb = [E(nc.psum_tensor(f"pb{i}", [128, 512], F32)) for i in range(8)]

        qsl = [Ctr(E(nc.semaphore("qs0")), "qs0"),
               Ctr(E(nc.semaphore("qs1")), "qs1")]
        qc = Ctr(E(nc.semaphore("qc")), "qc")
        qw = [Ctr(E(nc.semaphore("qw0")), "qw0"),
              Ctr(E(nc.semaphore("qw1")), "qw1")]
        qv = [Ctr(E(nc.semaphore("qv0")), "qv0"),
              Ctr(E(nc.semaphore("qv1")), "qv1")]
        qal = [Ctr(E(nc.semaphore("qa0")), "qa0"),
               Ctr(E(nc.semaphore("qa1")), "qa1")]
        st = Ctr(E(nc.semaphore("st")), "st")
        mm = Ctr(E(nc.semaphore("mm")), "mm")
        ve = Ctr(E(nc.semaphore("ve")), "ve")
        ac = Ctr(E(nc.semaphore("ac")), "ac")
        block = E(nc.Block())

        @block.sync
        def _(sync):
            qc.inc(sync.dma_start(out=maskc[:], in_=maskc_d), 16)
            qc.inc(sync.dma_start(out=eyebf[:], in_=eyebf_d), 16)
            qc.inc(sync.dma_start(out=eyef[:], in_=eyef_d), 16)
            qc.inc(sync.dma_start(out=cst[:], in_=consts_d), 16)
        ev["cst"] = qc.v

        qp = [[Ctr(E(nc.semaphore(f"qp{i}_{p}")), f"qp{i}_{p}") for p in range(2)]
              for i in range(6)]
        slot_h, slot_l = [], []

        def alloc_slots():
            for i in range(3):
                slot_h.append(E(sb(f"slot{i}h", [128, KC * D], BF16)))
                slot_l.append(E(sb(f"slot{i}l", [128, KC * D], BF16)))

        def emit_prep():
            CW = 2 * D
            with contextlib.ExitStack() as pctx:
                PE_ = pctx.enter_context
                pa = [PE_(sb(f"pa{i}", [128, CW], F32)) for i in range(2)]
                pbf = [PE_(sb(f"pbf{i}", [128, CW], F32)) for i in range(2)]
                pf = [PE_(sb(f"pf{i}", [128, CW], F32)) for i in range(2)]
                ph = [PE_(sb(f"ph{i}", [128, CW], BF16)) for i in range(2)]
                pl = [PE_(sb(f"pl{i}", [128, CW], BF16)) for i in range(2)]

                for wpos, widx in enumerate([W_K, W_V, W_Q, W_O, W_COS, W_SIN]):
                    for ci, c in enumerate(range(0, KC, 2)):
                        gi = wpos * 4 + ci
                        p = ci % 2
                        trig = widx >= 4
                        if not trig:
                            srcs = [(pa[p], wsrc[widx])]
                        else:
                            srcs = [(pa[p], tha if widx == 4 else pha),
                                    (pbf[p], thb if widx == 4 else phb)]

                        @block.sync
                        def _(sync, p=p, c=c, gi=gi, widx=widx, srcs=srcs, trig=trig):
                            kb.wait(sync, ve, ev.get(f"plog_{gi - 2}", 0))
                            kb.wait(sync, ac, ev.get(f"phig_{gi - 2}", 0))
                            if trig:
                                kb.wait(sync, ac, ev.get(f"pactg_{gi - 2}", 0))
                            for dst, srcd in srcs:
                                src2 = srcd[c * 128:(c + 2) * 128, :].rearrange(
                                    "(a p) d -> p a d", p=128)
                                dst2 = dst[:, :].rearrange("p (a d) -> p a d", d=D)
                                qsl[p].inc(sync.dma_start(out=dst2, in_=src2), 16)
                            ev[f"pld_{widx}_{c}"] = qsl[p].v

                        if trig:
                            @block.vector
                            def _(vector, p=p, c=c, gi=gi, widx=widx):
                                kb.wait(vector, qsl[p], ev[f"pld_{widx}_{c}"])
                                kb.wait(vector, ac, ev.get(f"pactg_{gi - 2}", 0))
                                if widx == 4:
                                    r = float(np.float64(np.pi / 2)
                                              - np.float64(np.float32(np.pi / 2)))
                                    ev[f"padd_{widx}_{c}"] = ve.inc(
                                        vector.scalar_tensor_tensor(
                                            pf[p][:], pa[p][:], r, pbf[p][:],
                                            op0=ALU.add, op1=ALU.add))
                                else:
                                    ev[f"padd_{widx}_{c}"] = ve.inc(vector.tensor_add(
                                        pf[p][:], pa[p][:], pbf[p][:]))

                            @block.scalar
                            def _(scalar, p=p, c=c, gi=gi, widx=widx):
                                kb.wait(scalar, ve, ev[f"padd_{widx}_{c}"])
                                kb.wait(scalar, qc, ev["cst"])
                                if widx == 4:
                                    ins = scalar.activation(pa[p][:], pf[p][:], AF.Sin,
                                                            bias=cst[:, 1:2], scale=1.0)
                                else:
                                    ins = scalar.activation(pa[p][:], pf[p][:], AF.Sin)
                                ev[f"pact_{widx}_{c}"] = ac.inc(ins)
                                ev[f"pactg_{gi}"] = ev[f"pact_{widx}_{c}"]

                        @block.scalar
                        def _(scalar, p=p, c=c, gi=gi, widx=widx, trig=trig):
                            if trig:
                                kb.wait(scalar, ac, ev[f"pact_{widx}_{c}"])
                            else:
                                kb.wait(scalar, qsl[p], ev[f"pld_{widx}_{c}"])
                            pg = ev.get(f"pstg_{gi - 2}")
                            if pg is not None:
                                kb.wait(scalar, pg[0], pg[1])
                            ev[f"phi_{widx}_{c}"] = ac.inc(
                                scalar.activation(ph[p][:], pa[p][:], AF.Copy))
                            ev[f"phig_{gi}"] = ev[f"phi_{widx}_{c}"]

                        @block.vector
                        def _(vector, p=p, c=c, gi=gi, widx=widx):
                            kb.wait(vector, ac, ev[f"phi_{widx}_{c}"])
                            ev[f"plo_{widx}_{c}"] = ve.inc(
                                vector.scalar_tensor_tensor(
                                    pl[p][:], pa[p][:], 1.0, ph[p][:],
                                    op0=ALU.mult, op1=ALU.subtract))
                            ev[f"plog_{gi}"] = ev[f"plo_{widx}_{c}"]

                        @block.gpsimd
                        def _(gpsimd, p=p, c=c, gi=gi, widx=widx):
                            kb.wait(gpsimd, ac, ev[f"phi_{widx}_{c}"])
                            kb.wait(gpsimd, ve, ev[f"plo_{widx}_{c}"])
                            qp[widx][p].inc(gpsimd.dma_start(
                                out=wsc[widx, 0, :, c * D:(c + 2) * D], in_=ph[p][:]), 16)
                            qp[widx][p].inc(gpsimd.dma_start(
                                out=wsc[widx, 1, :, c * D:(c + 2) * D], in_=pl[p][:]), 16)
                            ev[f"pstg_{gi}"] = (qp[widx][p], qp[widx][p].v)
                    ev[f"wsc_ready_{widx}"] = (qp[widx][0].v, qp[widx][1].v)

        total_uses = 3 + 6 * seqs * nblk
        loads_emitted = [0]

        def load_slot(use_id):
            si = use_id % 3
            qq = qw[use_id % 2]
            qq2 = qv[use_id % 2]
            widx = use_widx(use_id)

            @block.sync
            def _(sync):
                for wi in range(6):
                    for pp in range(2):
                        kb.wait(sync, qp[wi][pp], ev[f"wsc_ready_{wi}"][pp])
                if use_id - 2 >= 0:
                    kb.wait(sync, qq, ev[f"use_ld_{use_id - 2}"])
                if use_id - 3 >= 0:
                    kb.wait(sync, mm, ev[f"use_mm_{use_id - 3}"])
                qq.inc(sync.dma_start(out=slot_h[si][:], in_=wsc[widx, 0]), 16)
                ev[f"use_ld_{use_id}"] = qq.v

            @block.gpsimd
            def _(gpsimd):
                for wi in range(6):
                    for pp in range(2):
                        kb.wait(gpsimd, qp[wi][pp], ev[f"wsc_ready_{wi}"][pp])
                if use_id - 2 >= 0:
                    kb.wait(gpsimd, qv[use_id % 2], ev[f"use_lo_{use_id - 2}"])
                if use_id - 3 >= 0:
                    kb.wait(gpsimd, mm, ev[f"use_mm_{use_id - 3}"])
                qq2.inc(gpsimd.dma_start(out=slot_l[si][:], in_=wsc[widx, 1]), 16)
                ev[f"use_lo_{use_id}"] = qq2.v

        def pump_loads(upto):
            while loads_emitted[0] < min(upto, total_uses):
                load_slot(loads_emitted[0])
                loads_emitted[0] += 1

        def emit_ln(g, src_ap_fn, out_ap_fn, tag, extra_waits=(),
                    norm_waits=()):
            P = (g % 2) * 128
            bns0, bns1 = P + 72, P + 84
            agg = P + 96
            stdc, rstdc, nmrc, vepc = P + 100, P + 102, P + 104, P + 106

            @block.vector
            def _(vector):
                for c_, v_ in extra_waits:
                    kb.wait(vector, c_, v_)
                vector.bn_stats(stats[:, bns0:bns0 + 6], src_ap_fn(0, 0))
                vector.bn_stats(stats[:, bns0 + 6:bns0 + 12], src_ap_fn(0, 1))
                vector.bn_stats(stats[:, bns1:bns1 + 6], src_ap_fn(1, 0))
                ev[f"{tag}_bns"] = ve.inc(
                    vector.bn_stats(stats[:, bns1 + 6:bns1 + 12], src_ap_fn(1, 1)))
                kb.wait(vector, ve, ev[f"{tag}_bns"])
                vector.bn_aggr(stats[:, agg:agg + 2], stats[:, bns0:bns0 + 12])
                ev[f"{tag}_agg"] = ve.inc(
                    vector.bn_aggr(stats[:, agg + 2:agg + 4], stats[:, bns1:bns1 + 12]))
                kb.wait(vector, ve, ev[f"{tag}_agg"])
                vector.tensor_scalar_add(stats[:, vepc:vepc + 1],
                                         stats[:, agg + 1:agg + 2], EPS)
                ev[f"{tag}_vep"] = ve.inc(vector.tensor_scalar_add(
                    stats[:, vepc + 1:vepc + 2], stats[:, agg + 3:agg + 4], EPS))

            @block.scalar
            def _(scalar):
                kb.wait(scalar, ve, ev[f"{tag}_vep"])
                ev[f"{tag}_std"] = ac.inc(scalar.activation(
                    stats[:, stdc:stdc + 2], stats[:, vepc:vepc + 2], AF.Sqrt))

            @block.vector
            def _(vector):
                kb.wait(vector, ac, ev[f"{tag}_std"])
                ev[f"{tag}_rstd"] = ve.inc(vector.reciprocal(
                    stats[:, rstdc:rstdc + 2], stats[:, stdc:stdc + 2]))
                kb.wait(vector, ve, ev[f"{tag}_rstd"])
                vector.scalar_tensor_tensor(
                    stats[:, nmrc:nmrc + 1], stats[:, agg:agg + 1], -1.0,
                    stats[:, rstdc:rstdc + 1], op0=ALU.mult, op1=ALU.mult)
                ev[f"{tag}_nmr"] = ve.inc(vector.scalar_tensor_tensor(
                    stats[:, nmrc + 1:nmrc + 2], stats[:, agg + 2:agg + 3], -1.0,
                    stats[:, rstdc + 1:rstdc + 2], op0=ALU.mult, op1=ALU.mult))
                kb.wait(vector, ve, ev[f"{tag}_nmr"])
                for c_, v_ in norm_waits:
                    kb.wait(vector, c_, v_)
                ins = None
                for tt in range(2):
                    for half in range(2):
                        ins = vector.tensor_scalar(
                            out_ap_fn(tt, half), src_ap_fn(tt, half),
                            stats[:, rstdc + tt:rstdc + tt + 1],
                            stats[:, nmrc + tt:nmrc + tt + 1],
                            op0=ALU.mult, op1=ALU.add)
                ev[f"{tag}_norm"] = ve.inc(ins)
            return ev[f"{tag}_norm"]

        def emit_tsplit_tt(g, ht, src_sb, tag, waits=(), banks=(0, 1),
                           dsth=None, dstl=None):
            for bi, b in enumerate(banks):
                @block.tensor
                def _(tensor, bi=bi, b=b):
                    kb.wait(tensor, qc, ev["cst"])
                    for c_, v_ in waits:
                        kb.wait(tensor, c_, v_)
                    kb.bank_wait(tensor, [b])
                    ins = None
                    for i in range(4):
                        kc = bi * 4 + i
                        ins = tensor.transpose(
                            pb[b][:, i * 128:(i + 1) * 128],
                            src_sb[:, ht * 1024 + kc * 128:ht * 1024 + (kc + 1) * 128],
                            eyef[:])
                    ev[f"{tag}_tp{bi}"] = mm.inc(ins)

            dsth = fmh if dsth is None else dsth
            dstl = fml if dstl is None else dstl

            def _fm_dst(t, bi):
                v3 = t[:, :].rearrange("p (kc d) -> p kc d", d=256)
                return v3[:, bi * 4:(bi + 1) * 4, ht * 128:(ht + 1) * 128]

            @block.scalar
            def _(scalar):
                for bi, b in enumerate(banks):
                    kb.wait(scalar, mm, ev[f"{tag}_tp{bi}"])
                    ev[f"{tag}_hi{bi}"] = ac.inc(scalar.activation(
                        _fm_dst(dsth, bi),
                        pb[b][:, 0:512].rearrange("p (kc d) -> p kc d", d=128),
                        AF.Copy))

            @block.vector
            def _(vector):
                for bi, b in enumerate(banks):
                    kb.wait(vector, ac, ev[f"{tag}_hi{bi}"])
                    ev[f"{tag}_lo{bi}"] = ve.inc(vector.scalar_tensor_tensor(
                        _fm_dst(dstl, bi),
                        pb[b][:, 0:512].rearrange("p (kc d) -> p kc d", d=128), 1.0,
                        _fm_dst(dsth, bi), op0=ALU.mult, op1=ALU.subtract))
            kb.bank_set(list(banks), [(ve, ve.v)])
            ev[f"{tag}_hi"] = ev[f"{tag}_hi1"]
            ev[f"{tag}_lo"] = ev[f"{tag}_lo1"]
            return f"{tag}_hi", f"{tag}_lo"

        def emit_tsplit(g, src_sb, tag, waits=(), banks=(0, 1, 2, 3),
                        hi_waits=(), dsth=None, dstl=None):
            for kcp in range(0, KC, 2):
                b = banks[kcp // 2]
                tiles = [(kcp, 0), (kcp, 1), (kcp + 1, 0), (kcp + 1, 1)]

                @block.tensor
                def _(tensor, kcp=kcp, b=b, tiles=tiles):
                    kb.wait(tensor, qc, ev["cst"])
                    for c_, v_ in waits:
                        kb.wait(tensor, c_, v_)
                    kb.bank_wait(tensor, [b])
                    ins = None
                    for i, (kc, tt) in enumerate(tiles):
                        ins = tensor.transpose(
                            pb[b][:, i * 128:(i + 1) * 128],
                            src_sb[:, tt * 1024 + kc * 128:tt * 1024 + (kc + 1) * 128],
                            eyef[:])
                    ev[f"{tag}_tp{kcp}"] = mm.inc(ins)

            @block.scalar
            def _(scalar):
                for c_, v_ in hi_waits:
                    kb.wait(scalar, c_, v_)
                dh_ = fmh if dsth is None else dsth
                for kcp in range(0, KC, 2):
                    kb.wait(scalar, mm, ev[f"{tag}_tp{kcp}"])
                    ev[f"{tag}_hi{kcp}"] = ac.inc(scalar.activation(
                        dh_[:, kcp * 256:(kcp + 2) * 256],
                        pb[banks[kcp // 2]][:], AF.Copy))

            @block.vector
            def _(vector):
                dh_ = fmh if dsth is None else dsth
                dl_ = fml if dstl is None else dstl
                for kcp in range(0, KC, 2):
                    kb.wait(vector, ac, ev[f"{tag}_hi{kcp}"])
                    ev[f"{tag}_lo{kcp}"] = ve.inc(vector.scalar_tensor_tensor(
                        dl_[:, kcp * 256:(kcp + 2) * 256], pb[banks[kcp // 2]][:], 1.0,
                        dh_[:, kcp * 256:(kcp + 2) * 256],
                        op0=ALU.mult, op1=ALU.subtract))
            kb.bank_set(list(banks), [(ve, ve.v)])
            ev[f"{tag}_hi"] = ev[f"{tag}_hi{KC - 2}"]
            ev[f"{tag}_lo"] = ev[f"{tag}_lo{KC - 2}"]
            return f"{tag}_hi", f"{tag}_lo"

        def emit_proj_fm(g, use_id, dst_sb, dst_col0, tag, hi_ev, lo_ev, evac_eng,
                         evac_waits=(), srch=None, srcl=None):
            si = use_id % 3
            for m in range(KC):
                b = m % 4

                @block.tensor
                def _(tensor, m=m, b=b):
                    kb.wait(tensor, qw[use_id % 2], ev[f"use_ld_{use_id}"])
                    kb.wait(tensor, qv[use_id % 2], ev[f"use_lo_{use_id}"])
                    kb.wait(tensor, ac, ev[hi_ev])
                    kb.wait(tensor, ve, ev[lo_ev])
                    kb.bank_wait(tensor, [b])
                    sh_ = fmh if srch is None else srch
                    sl_ = fml if srcl is None else srcl
                    n_mm = 0
                    ins = None
                    for kc in range(KC):
                        wh = slot_h[si][:, kc * D + m * 128:kc * D + (m + 1) * 128]
                        wl = slot_l[si][:, kc * D + m * 128:kc * D + (m + 1) * 128]
                        ah = sh_[:, kc * 256:(kc + 1) * 256]
                        al = sl_[:, kc * 256:(kc + 1) * 256]
                        for lhsT, rhs in [(wh, ah), (wh, al), (wl, ah)]:
                            ins = tensor.matmul(pb[b][:, 0:256], lhsT, rhs,
                                                start=(n_mm == 0), stop=(n_mm == 23))
                            n_mm += 1
                    ev[f"{tag}_mm{m}"] = mm.inc(ins)

                eng_sec = block.vector if evac_eng == "v" else block.scalar
                ctr = ve if evac_eng == "v" else ac

                @eng_sec
                def _(eng, m=m, b=b):
                    for c_, v_ in evac_waits:
                        kb.wait(eng, c_, v_)
                    kb.wait(eng, mm, ev[f"{tag}_mm{m}"])
                    dst = dst_sb[:, dst_col0 + m * 256:dst_col0 + (m + 1) * 256]
                    if evac_eng == "v":
                        ins = eng.tensor_copy(dst, pb[b][:, 0:256])
                    else:
                        ins = eng.activation(dst, pb[b][:, 0:256], AF.Copy)
                    ev[f"{tag}_ev{m}"] = ctr.inc(ins)
                kb.bank_set([b], [(ctr, ctr.v)])
            ev[f"use_mm_{use_id}"] = ev[f"{tag}_mm{KC - 1}"]
            ev[f"{tag}_done"] = ev[f"{tag}_ev{KC - 1}"]

        def emit_proj_tm(g, use_id, tag, hi_ev, lo_ev, banks, extra_waits=(),
                         srch=None, srcl=None):
            si = use_id % 3
            for ti, b in enumerate(banks):
                tt, nh = ti // 2, ti % 2

                @block.tensor
                def _(tensor, tt=tt, nh=nh, b=b):
                    kb.wait(tensor, qw[use_id % 2], ev[f"use_ld_{use_id}"])
                    kb.wait(tensor, qv[use_id % 2], ev[f"use_lo_{use_id}"])
                    for c_, v_ in extra_waits:
                        kb.wait(tensor, c_, v_)
                    kb.wait(tensor, ac, ev[hi_ev])
                    kb.wait(tensor, ve, ev[lo_ev])
                    kb.bank_wait(tensor, [b])
                    sh_ = fmh if srch is None else srch
                    sl_ = fml if srcl is None else srcl
                    n_mm = 0
                    ins = None
                    for kc in range(KC):
                        ah = sh_[:, kc * 256 + tt * 128:kc * 256 + (tt + 1) * 128]
                        al = sl_[:, kc * 256 + tt * 128:kc * 256 + (tt + 1) * 128]
                        wh = slot_h[si][:, kc * D + nh * 512:kc * D + (nh + 1) * 512]
                        wl = slot_l[si][:, kc * D + nh * 512:kc * D + (nh + 1) * 512]
                        for lhsT, rhs in [(ah, wh), (ah, wl), (al, wh)]:
                            ins = tensor.matmul(pb[b][:], lhsT, rhs,
                                                start=(n_mm == 0), stop=(n_mm == 23))
                            n_mm += 1
                    ev[f"{tag}_mm{ti}"] = mm.inc(ins)
            ev[f"use_mm_{use_id}"] = ev[f"{tag}_mm3"]

        nuse = 0
        pending_fm = {}
        pending_tail = [None]

        def _src_x(tt, half):
            return x_sb[:, tt * 1024 + half * 512:tt * 1024 + (half + 1) * 512]

        def _out_sc1(tt, half):
            return sc1[:, tt * 1024 + half * 512:tt * 1024 + (half + 1) * 512]

        def emit_ln1(gg):
            nw = [(mm, ev[f"n3t_{gg - 1}_tp{KC - 2}"])] if gg >= 1 else []
            ev[f"ln1v_{gg}"] = emit_ln(gg, _src_x, _out_sc1, f"ln1_{gg}",
                                       extra_waits=[(qa, ev[f"xload_{gg}"])],
                                       norm_waits=nw)

        def emit_n1t(gg):
            hw_ = [(mm, ev[f"im_{gg - 1}_mm3"])] if gg >= 1 else []
            h_ev, l_ev = emit_tsplit(gg, sc1, f"n1t_{gg}",
                                     waits=[(ve, ev[f"ln1v_{gg}"])],
                                     banks=(4, 5, 6, 7), hi_waits=hw_)
            pending_fm[gg] = (h_ev, l_ev)

        def emit_ln1(gg, g_waits=()):
            def _src(tt, half):
                xbuf = x_sb if gg % 2 == 0 else x_sb2
                return xbuf[:, tt * 1024 + half * 512:tt * 1024 + (half + 1) * 512]

            def _out(tt, half):
                return ao_sb[:, tt * 1024 + half * 512:tt * 1024 + (half + 1) * 512]

            ev[f"ln1v_{gg}"] = emit_ln(gg, _src, _out, f"ln1_{gg}",
                                       extra_waits=[(qal[gg % 2], ev[f"xload_{gg}"])],
                                       norm_waits=g_waits)

        def emit_n1t(gg):
            h_ev, l_ev = emit_tsplit(gg, ao_sb, f"n1t_{gg}",
                                     waits=[(ve, ev[f"ln1v_{gg}"])],
                                     banks=(4, 5, 6, 7))
            pending_fm[gg] = (h_ev, l_ev)

        def emit_kvq(gg, base_use):
            bufg = (gg % nblk) % 2
            h_ev, l_ev = pending_fm.pop(gg)
            emit_proj_fm(gg, base_use, k_sb, bufg * 2048, f"kp_{gg}",
                         h_ev, l_ev, "v")
            pump_loads(base_use + 4)
            emit_proj_tm(gg, base_use + 1, f"vp_{gg}", h_ev, l_ev, [0, 1, 2, 3])
            for ti in range(4):
                tt, nh = ti // 2, ti % 2

                @block.vector
                def _(vector, ti=ti, tt=tt, nh=nh):
                    kb.wait(vector, mm, ev[f"vp_{gg}_mm{ti}"])
                    ev[f"vp_{gg}_ev{ti}"] = ve.inc(vector.tensor_copy(
                        v_sb[:, bufg * 2048 + tt * 1024 + nh * 512:
                             bufg * 2048 + tt * 1024 + (nh + 1) * 512],
                        pb[ti][:]))
                kb.bank_set([ti], [(ve, ve.v)])
            pump_loads(base_use + 5)
            emit_proj_fm(gg, base_use + 2, q_sb, 0, f"qp_{gg}", h_ev, l_ev, "v")
            pump_loads(base_use + 6)

        @block.gpsimd
        def _(gpsimd):
            qal[0].inc(gpsimd.dma_start(out=x_sb[:, 0:1024], in_=x_d[0:128, :]), 16)
            qal[0].inc(gpsimd.dma_start(out=x_sb[:, 1024:2048],
                                        in_=x_d[128:256, :]), 16)
            ev["xload_0"] = qal[0].v
            if seqs * nblk > 1:
                qal[1].inc(gpsimd.dma_start(out=x_sb2[:, 0:1024],
                                            in_=x_d[W + 0:W + 128, :]), 16)
                qal[1].inc(gpsimd.dma_start(out=x_sb2[:, 1024:2048],
                                            in_=x_d[W + 128:W + 256, :]), 16)
                ev["xload_1"] = qal[1].v

        emit_ln1(0)
        emit_n1t(0)
        emit_prep()
        alloc_slots()
        pump_loads(3)
        emit_kvq(0, 0)

        for g in range(seqs * nblk):
            seq, j = g // nblk, g % nblk
            row0 = seq * nblk * W + j * W
            P = (g % 2) * 128
            buf = j % 2
            pbuf = 1 - buf
            last = (g == seqs * nblk - 1)
            xb_g = x_sb if g % 2 == 0 else x_sb2

            def cs_of(qsub):
                if j == 0:
                    return [2] if qsub == 0 else [2, 3]
                return [0, 1, 2] if qsub == 0 else [1, 2, 3]

            mvar = 2 if j == 0 else 0

            SL = 384

            def score_mms(u, qsub, h):
                sb_bank = 4 + (u % 2)
                cs = cs_of(qsub)
                c0, c1 = cs[0], cs[-1]
                ncols = (c1 - c0 + 1) * 128
                m = qsub + mvar
                pr = 64 * (h % 2)
                lq = q_sb[pr:pr + 64,
                          (h // 2) * 256 + qsub * 128:
                          (h // 2) * 256 + (qsub + 1) * 128]
                mms = []
                pcs = [c for c in cs if c < 2]
                ccs = [c for c in cs if c >= 2]
                if pcs:
                    rk = k_sb[pr:pr + 64,
                              pbuf * 2048 + (h // 2) * 256 + pcs[0] * 128:
                              pbuf * 2048 + (h // 2) * 256 + (pcs[-1] + 1) * 128]
                    mms.append(((pcs[0] * 128, (pcs[-1] + 1) * 128), lq, rk))
                if ccs:
                    rk = k_sb[pr:pr + 64,
                              buf * 2048 + (h // 2) * 256 + (ccs[0] - 2) * 128:
                              buf * 2048 + (h // 2) * 256 + (ccs[-1] - 1) * 128]
                    mms.append(((ccs[0] * 128, (ccs[-1] + 1) * 128), lq, rk))
                return sb_bank, (c0 * 128, (c1 + 1) * 128, MOFF[m], ncols), mms

            def pair_scores_t(u):
                specs = [score_mms(uu, *units[uu]) for uu in (u, u + 1)]

                @block.tensor
                def _(tensor, u=u, specs=specs):
                    kb.wait(tensor, ve, ev[f"qp_{g}_done"])
                    kb.bank_wait(tensor, [specs[0][0], specs[1][0]])
                    for bank, (a, b, mo, ncols), _mms in specs:
                        tensor.matmul(pb[bank][:, a:b], eyebf[:],
                                      maskc[:, mo:mo + ncols],
                                      start=True, stop=False)
                    nmax = max(len(sp[2]) for sp in specs)
                    last = [len(specs[0][2]) - 1, len(specs[1][2]) - 1]
                    ins = None
                    for i in range(nmax):
                        for si, (bank, _mk, mms) in enumerate(specs):
                            if i < len(mms):
                                (a, b), lq, rk = mms[i]
                                ins = tensor.matmul(pb[bank][:, a:b], lq, rk,
                                                    start=False,
                                                    stop=(i == last[si]))
                    ev[f"sc_{g}_{u}"] = mm.inc(ins)
                    ev[f"sc_{g}_{u + 1}"] = ev[f"sc_{g}_{u}"]

            def unit_scores_va(u, qsub, h):
                sb_bank = 4 + (u % 2)
                up = u % 4
                cs = cs_of(qsub)
                c0, c1 = cs[0], cs[-1]

            def pair_va(u):
                specs = []
                for uu in (u, u + 1):
                    qsub, h = units[uu]
                    cs = cs_of(qsub)
                    specs.append((uu, uu % 2, uu % 4, qsub, h, cs[0], cs[-1]))

                @block.vector
                def _(vector, u=u, specs=specs):
                    kb.wait(vector, mm, ev[f"sc_{g}_{u}"])
                    for uu, ub, up, qsub, h, c0, c1 in specs:
                        if uu - 4 >= 0:
                            kb.wait(vector, mm, ev[f"tp_{g}_{uu - 4}"])
                        elif g >= 1:
                            kb.wait(vector, mm, ev[f"tp_{g - 1}_{28 + uu}"])
                        kept = pb[4 + ub][:, c0 * 128:(c1 + 1) * 128]
                        ev[f"mx_{g}_{uu}"] = ve.inc(vector.reduce_max(
                            stats[:, P + up:P + up + 1], kept, axis=AX.X))
                        kb.wait(vector, ve, ev[f"mx_{g}_{uu}"])
                        ev[f"ss_{g}_{uu}"] = ve.inc(vector.tensor_scalar(
                            attn_sb[:, up * SL:up * SL + (c1 - c0 + 1) * 128],
                            kept, stats[:, P + up:P + up + 1], 0.125,
                            op0=ALU.subtract, op1=ALU.mult))
                kb.bank_set([4], [(ve, ev[f"ss_{g}_{u}"])])
                kb.bank_set([5], [(ve, ev[f"ss_{g}_{u + 1}"])])

                @block.scalar
                def _(scalar, u=u, specs=specs):
                    for uu, ub, up, qsub, h, c0, c1 in specs:
                        kb.wait(scalar, ve, ev[f"ss_{g}_{uu}"])
                        ev[f"ex_{g}_{uu}"] = ac.inc(scalar.activation(
                            attn_sb[:, up * SL:up * SL + (c1 - c0 + 1) * 128],
                            attn_sb[:, up * SL:up * SL + (c1 - c0 + 1) * 128],
                            AF.Exp,
                            accum_out=stats[:, P + 8 + qsub * 16 + h:
                                            P + 9 + qsub * 16 + h]))

            def unit_tp(u, qsub, h):
                upA = u % 4
                upT = u % 3
                tp_bank = 6 + (u % 2)
                cs = cs_of(qsub)
                c0 = cs[0]

                @block.tensor
                def _(tensor, u=u, upA=upA, tp_bank=tp_bank, cs=cs, c0=c0):
                    kb.wait(tensor, ac, ev[f"ex_{g}_{u}"])
                    kb.bank_wait(tensor, [tp_bank])
                    ins = None
                    for c in cs:
                        ins = tensor.transpose(
                            pb[tp_bank][:, (c - c0) * 128:(c - c0 + 1) * 128],
                            attn_sb[:, upA * SL + (c - c0) * 128:
                                    upA * SL + (c - c0 + 1) * 128],
                            eyef[:])
                    ev[f"tp_{g}_{u}"] = mm.inc(ins)

                @block.vector
                def _(vector, u=u, upT=upT, tp_bank=tp_bank, cs=cs):
                    kb.wait(vector, mm, ev[f"tp_{g}_{u}"])
                    nc_ = len(cs) * 128
                    ev[f"te_{g}_{u}"] = ve.inc(vector.tensor_copy(
                        attnT_sb[:, upT * SL:upT * SL + nc_],
                        pb[tp_bank][:, 0:nc_]))
                kb.bank_set([tp_bank], [(ve, ve.v)])

            def unit_av(u, qsub, h):
                up = u % 3
                avb = [0, 1] if qsub == 0 else [2, 3]
                b_av = avb[h // 8]
                cs = cs_of(qsub)
                c0 = cs[0]

                @block.tensor
                def _(tensor, u=u, up=up, qsub=qsub, h=h, b_av=b_av, cs=cs, c0=c0):
                    kb.wait(tensor, ve, ev[f"te_{g}_{u}"])
                    kb.wait(tensor, ve, ev[f"vp_{g}_ev3"])
                    if h % 8 == 0:
                        kb.bank_wait(tensor, [b_av])
                    first_in_bank = (h % 8 == 0)
                    last_in_bank = (h % 8 == 7)
                    ins = None
                    for ci, c in enumerate(cs):
                        lhsT = attnT_sb[:, up * SL + (c - c0) * 128:
                                        up * SL + (c - c0 + 1) * 128]
                        vb, vtt = (pbuf, c) if c < 2 else (buf, c - 2)
                        rhs = v_sb[:, vb * 2048 + vtt * 1024 + h * DH:
                                   vb * 2048 + vtt * 1024 + (h + 1) * DH]
                        ins = tensor.matmul(
                            pb[b_av][:, (h % 8) * DH:(h % 8 + 1) * DH], lhsT, rhs,
                            start=(first_in_bank and ci == 0),
                            stop=(last_in_bank and ci == len(cs) - 1))
                    ev[f"av_{g}_{u}"] = mm.inc(ins)

            units = [(qsub, h) for qsub in range(2) for h in range(H)]
            nu = len(units)
            for u in range(nu + 5):
                if 0 <= u - 5 < nu:
                    unit_av(u - 5, *units[u - 5])
                if 0 <= u - 3 < nu:
                    unit_tp(u - 3, *units[u - 3])
                if u < nu and u % 2 == 0:
                    pair_scores_t(u)
                    pair_va(u)
                if pending_tail[0] is not None:
                    while pending_tail[0] and pending_tail[0][0][0] == u:
                        pending_tail[0].pop(0)[1]()
                    if not pending_tail[0]:
                        pending_tail[0] = None

            for qsub in range(2):
                avb = [0, 1] if qsub == 0 else [2, 3]
                u_last = qsub * 16 + 15

                @block.vector
                def _(vector, qsub=qsub, u_last=u_last):
                    kb.wait(vector, ac, ev[f"ex_{g}_{u_last}"])
                    kb.wait(vector, mm, ev[f"av_{g}_{u_last}"])
                    ev[f"rc_{g}_{qsub}"] = ve.inc(vector.reciprocal(
                        stats[:, P + 40 + qsub * 16:P + 56 + qsub * 16],
                        stats[:, P + 8 + qsub * 16:P + 24 + qsub * 16]))
                    kb.wait(vector, ve, ev[f"rc_{g}_{qsub}"])
                    ins = None
                    for h in range(H):
                        b_av = avb[h // 8]
                        rcol = P + 40 + qsub * 16 + h
                        ins = vector.tensor_scalar_mul(
                            ao_sb[:, qsub * 1024 + h * DH:qsub * 1024 + (h + 1) * DH],
                            pb[b_av][:, (h % 8) * DH:(h % 8 + 1) * DH],
                            stats[:, rcol:rcol + 1])
                    ev[f"ao_{g}_{qsub}"] = ve.inc(ins)
                kb.bank_set(avb, [(ve, ev[f"ao_{g}_{qsub}"])])

            emit_tsplit_tt(g, 0, ao_sb, f"aot0_{g}",
                           waits=[(ve, ev[f"ao_{g}_0"])], banks=(0, 1),
                           dsth=fmh2, dstl=fml2)
            h_ev2, l_ev2 = emit_tsplit_tt(g, 1, ao_sb, f"aot1_{g}",
                                          waits=[(ve, ev[f"ao_{g}_1"])], banks=(2, 3),
                                          dsth=fmh2, dstl=fml2)


            if not last:
                emit_ln1(g + 1, g_waits=[(mm, ev[f"aot1_{g}_tp1"])])

            o_use = 3 + 6 * g
            emit_proj_tm(g, o_use, f"op_{g}", h_ev2, l_ev2, [0, 1, 2, 3],
                         extra_waits=[(ac, ev[f"aot0_{g}_hi"]),
                                      (ve, ev[f"aot0_{g}_lo"])],
                         srch=fmh2, srcl=fml2)
            pump_loads(o_use + 4)

            if not last:
                emit_n1t(g + 1)

            @block.vector
            def _(vector):
                ins = None
                for ti in range(4):
                    kb.wait(vector, mm, ev[f"op_{g}_mm{ti}"])
                    ins = vector.tensor_copy(
                        sc2[:, ti * 512:(ti + 1) * 512], pb[ti][:])
                ev[f"oev_{g}"] = ve.inc(ins)
            kb.bank_set([0, 1, 2, 3], [(ve, ev[f"oev_{g}"])])

            def _sc2_ap(tt, half):
                return sc2[:, tt * 1024 + half * 512:tt * 1024 + (half + 1) * 512]

            ln2v = emit_ln(g, _sc2_ap, _sc2_ap, f"ln2_{g}",
                           extra_waits=[(ve, ev[f"oev_{g}"])])

            @block.vector
            def _(vector):
                kb.wait(vector, ve, ln2v)
                ins = None
                for tt in range(2):
                    ins = vector.tensor_add(
                        xb_g[:, tt * 1024:(tt + 1) * 1024],
                        xb_g[:, tt * 1024:(tt + 1) * 1024],
                        sc2[:, tt * 1024:(tt + 1) * 1024])
                ev[f"x2_{g}"] = ve.inc(ins)

            def _src_x2(tt, half):
                return xb_g[:, tt * 1024 + half * 512:tt * 1024 + (half + 1) * 512]

            def _out_sc1(tt, half):
                return sc1[:, tt * 1024 + half * 512:tt * 1024 + (half + 1) * 512]

            ln3v = emit_ln(g, _src_x2, _out_sc1, f"ln3_{g}",
                           extra_waits=[(ve, ev[f"x2_{g}"])],
                           norm_waits=([(st, ev[f"store_{g - 1}"])]
                                       if g >= 1 else []))

            if not last:
                emit_kvq(g + 1, 4 + 6 * g)
            else:
                for uid in range(4 + 6 * g, 7 + 6 * g):
                    ev[f"use_mm_{uid}"] = mm.v
                pump_loads(10 + 6 * g)

            h_ev3, l_ev3 = emit_tsplit(g, sc1, f"n3t_{g}", waits=[(ve, ln3v)],
                                       banks=(4, 5, 6, 7),
                                       hi_waits=[(mm, ev[f"op_{g}_mm3"])],
                                       dsth=fmh2, dstl=fml2)

            cos_use = 7 + 6 * g
            emit_proj_tm(g, cos_use, f"re_{g}", h_ev3, l_ev3, [0, 1, 2, 3],
                         srch=fmh2, srcl=fml2)
            pump_loads(cos_use + 4)

            @block.scalar
            def _(scalar):
                ins = None
                for ti in range(4):
                    kb.wait(scalar, mm, ev[f"re_{g}_mm{ti}"])
                    if not last:
                        kb.wait(scalar, mm, ev[f"n1t_{g + 1}_tp{KC - 2}"])
                    ins = scalar.activation(ao_sb[:, ti * 512:(ti + 1) * 512],
                                            pb[ti][:], AF.Square)
                ev[f"sqre_{g}"] = ac.inc(ins)
            kb.bank_set([0, 1, 2, 3], [(ac, ev[f"sqre_{g}"])])

            emit_proj_tm(g, cos_use + 1, f"im_{g}", h_ev3, l_ev3, [0, 1, 2, 3],
                         srch=fmh2, srcl=fml2)
            pump_loads(cos_use + 5)

            @block.scalar
            def _(scalar):
                ins = None
                for ti in range(4):
                    kb.wait(scalar, mm, ev[f"im_{g}_mm{ti}"])
                    ins = scalar.activation(sc2[:, ti * 512:(ti + 1) * 512],
                                            pb[ti][:], AF.Square)
                ev[f"sq_{g}"] = ac.inc(ins)
            kb.bank_set([0, 1, 2, 3], [(ac, ev[f"sq_{g}"])])

            Pt = (g % 2) * 128
            bns0, bns1 = Pt + 72, Pt + 84
            agg = Pt + 96
            stdc, rstdc, nmrc, vepc = Pt + 100, Pt + 102, Pt + 104, Pt + 106

            def tail_a(g=g):
                @block.vector
                def _(vector):
                    kb.wait(vector, ac, ev[f"sq_{g}"])
                    ev[f"m2_{g}"] = ve.inc(
                        vector.tensor_add(sc1[:], ao_sb[:], sc2[:]))

                @block.scalar
                def _(scalar):
                    kb.wait(scalar, ve, ev[f"m2_{g}"])
                    ev[f"mag_{g}"] = ac.inc(
                        scalar.activation(ao_sb[:], sc1[:], AF.Sqrt))

            def tail_b(g=g):
                @block.vector
                def _(vector):
                    kb.wait(vector, ac, ev[f"mag_{g}"])
                    vector.bn_stats(stats[:, bns0:bns0 + 6], ao_sb[:, 0:512])
                    vector.bn_stats(stats[:, bns0 + 6:bns0 + 12],
                                    ao_sb[:, 512:1024])
                    vector.bn_stats(stats[:, bns1:bns1 + 6], ao_sb[:, 1024:1536])
                    ev[f"ln4_{g}_bns"] = ve.inc(vector.bn_stats(
                        stats[:, bns1 + 6:bns1 + 12], ao_sb[:, 1536:2048]))
                    kb.wait(vector, ve, ev[f"ln4_{g}_bns"])
                    vector.bn_aggr(stats[:, agg:agg + 2], stats[:, bns0:bns0 + 12])
                    ev[f"ln4_{g}_agg"] = ve.inc(vector.bn_aggr(
                        stats[:, agg + 2:agg + 4], stats[:, bns1:bns1 + 12]))
                    kb.wait(vector, ve, ev[f"ln4_{g}_agg"])
                    vector.tensor_scalar_add(stats[:, vepc:vepc + 1],
                                             stats[:, agg + 1:agg + 2], EPS)
                    ev[f"ln4_{g}_vep"] = ve.inc(vector.tensor_scalar_add(
                        stats[:, vepc + 1:vepc + 2], stats[:, agg + 3:agg + 4],
                        EPS))

                @block.scalar
                def _(scalar):
                    kb.wait(scalar, ve, ev[f"ln4_{g}_vep"])
                    ev[f"ln4_{g}_std"] = ac.inc(scalar.activation(
                        stats[:, stdc:stdc + 2], stats[:, vepc:vepc + 2],
                        AF.Sqrt))

            def tail_c1(g=g):
                @block.vector
                def _(vector):
                    kb.wait(vector, ac, ev[f"ln4_{g}_std"])
                    ev[f"ln4_{g}_rstd"] = ve.inc(vector.reciprocal(
                        stats[:, rstdc:rstdc + 2], stats[:, stdc:stdc + 2]))
                    kb.wait(vector, ve, ev[f"ln4_{g}_rstd"])
                    vector.scalar_tensor_tensor(
                        stats[:, nmrc:nmrc + 1], stats[:, agg:agg + 1], -1.0,
                        stats[:, rstdc:rstdc + 1], op0=ALU.mult, op1=ALU.mult)
                    ev[f"ln4_{g}_nmr"] = ve.inc(vector.scalar_tensor_tensor(
                        stats[:, nmrc + 1:nmrc + 2], stats[:, agg + 2:agg + 3],
                        -1.0, stats[:, rstdc + 1:rstdc + 2],
                        op0=ALU.mult, op1=ALU.mult))
                    kb.wait(vector, ve, ev[f"ln4_{g}_nmr"])
                    ins = None
                    for tt in range(2):
                        for half in range(2):
                            ins = vector.tensor_scalar(
                                sc2[:, tt * 1024 + half * 512:
                                    tt * 1024 + (half + 1) * 512],
                                ao_sb[:, tt * 1024 + half * 512:
                                      tt * 1024 + (half + 1) * 512],
                                stats[:, rstdc + tt:rstdc + tt + 1],
                                stats[:, nmrc + tt:nmrc + tt + 1],
                                op0=ALU.mult, op1=ALU.add)
                    ev[f"ln4_{g}_norm"] = ve.inc(ins)

            def tail_c2(g=g, row0=row0, xb_g=xb_g):
                @block.vector
                def _(vector):
                    kb.wait(vector, ve, ev[f"ln4_{g}_norm"])
                    ins = None
                    for tt in range(2):
                        ins = vector.tensor_add(
                            sc1[:, tt * 1024:(tt + 1) * 1024],
                            xb_g[:, tt * 1024:(tt + 1) * 1024],
                            sc2[:, tt * 1024:(tt + 1) * 1024])
                    ev[f"outtt_{g}"] = ve.inc(ins)

                @block.gpsimd
                def _(gpsimd):
                    kb.wait(gpsimd, ve, ev[f"outtt_{g}"])
                    for tt in range(2):
                        st.inc(gpsimd.dma_start(
                            out=y_d[row0 + tt * 128:row0 + (tt + 1) * 128, :],
                            in_=sc1[:, tt * 1024:(tt + 1) * 1024]), 16)
                    ev[f"store_{g}"] = st.v

                if g + 2 < seqs * nblk:
                    nseq, nj = (g + 2) // nblk, (g + 2) % nblk
                    nrow0 = nseq * nblk * W + nj * W

                    @block.gpsimd
                    def _(gpsimd):
                        kb.wait(gpsimd, ve, ev[f"outtt_{g}"])
                        for tt in range(2):
                            qal[g % 2].inc(gpsimd.dma_start(
                                out=xb_g[:, tt * 1024:(tt + 1) * 1024],
                                in_=x_d[nrow0 + tt * 128:
                                        nrow0 + (tt + 1) * 128, :]), 16)
                        ev[f"xload_{g + 2}"] = qal[g % 2].v

            pending_tail[0] = [(2, tail_a), (6, tail_b), (10, tail_c1),
                               (12, tail_c2)]

        if pending_tail[0] is not None:
            for _, _fn in pending_tail[0]:
                _fn()
            pending_tail[0] = None

        @block.scalar
        def _(scalar):
            scalar.wait_ge(st.sem, st.v)

    return nc


def _host_consts():
    qi = np.arange(128)[:, None]
    ki = np.arange(512)[None, :]
    spans = {0: (0, 384), 1: (128, 512), 2: (256, 384), 3: (256, 512)}
    cols = []
    for variant in range(4):
        qsub, is_j0 = variant % 2, variant // 2
        gq = qsub * 128 + qi
        band = (ki >= gq + 1) & (ki <= gq + W)
        if is_j0:
            band = band & (ki >= 256)
        a, b = spans[variant]
        cols.append(np.where(band, 0.0, NEG)[:, a:b])
    maskc = np.concatenate(cols, axis=1).astype(ml_dtypes.bfloat16)
    assert maskc.shape == (128, 1152)
    eyebf = np.eye(128).astype(ml_dtypes.bfloat16)
    eyef = np.eye(128, dtype=np.float32)
    consts = np.zeros((128, 8), dtype=np.float32)
    consts[:, 0] = EPS
    consts[:, 1] = np.pi / 2
    consts[:, 3] = 1.0
    return maskc, eyebf, eyef, consts


def make_in_map(x_shard, wq, wk, wv, wo, tha, thb, pha, phb):
    maskc, eyebf, eyef, consts = _host_consts()
    return {
        "x": np.ascontiguousarray(x_shard.reshape(-1, D)),
        "wqT": np.ascontiguousarray(wq.T),
        "wkT": np.ascontiguousarray(wk.T),
        "wvT": np.ascontiguousarray(wv.T),
        "woT": np.ascontiguousarray(wo.T),
        "tha": np.ascontiguousarray(tha),
        "thb": np.ascontiguousarray(thb),
        "pha": np.ascontiguousarray(pha),
        "phb": np.ascontiguousarray(phb),
        "maskc": maskc, "eyebf": eyebf, "eyef": eyef, "consts": consts,
    }


def kernel(L0_attn_x, L0_attn_wq, L0_attn_wk, L0_attn_wv, L0_attn_wo,
           L0_ffn_theta, L0_ffn_phi, L0_ffn_theta_noise, L0_ffn_phi_noise):
    x = np.asarray(L0_attn_x, dtype=np.float32)
    wq = np.asarray(L0_attn_wq, dtype=np.float32)
    wk = np.asarray(L0_attn_wk, dtype=np.float32)
    wv = np.asarray(L0_attn_wv, dtype=np.float32)
    wo = np.asarray(L0_attn_wo, dtype=np.float32)
    tha = np.asarray(L0_ffn_theta, dtype=np.float32)
    thb = np.asarray(L0_ffn_theta_noise, dtype=np.float32)
    pha = np.asarray(L0_ffn_phi, dtype=np.float32)
    phb = np.asarray(L0_ffn_phi_noise, dtype=np.float32)

    n_cores, seqs, nblk = 8, 2, 8
    nc = build_nc(seqs=seqs, nblk=nblk)
    in_maps = [make_in_map(x[c * seqs:(c + 1) * seqs], wq, wk, wv, wo,
                           tha, thb, pha, phb)
               for c in range(n_cores)]
    res = run_bass_kernel_spmd(nc, in_maps, core_ids=list(range(n_cores)))
    y = np.concatenate(
        [np.asarray(r["y"]).reshape(seqs, nblk * W, D) for r in res.results], axis=0)
    return y


# version 22
# speedup vs baseline: 1.0063x; 1.0063x over previous
import contextlib
import numpy as np
import ml_dtypes
import concourse.bass as bass
import concourse.mybir as mybir
from concourse.bass_utils import run_bass_kernel_spmd

F32 = mybir.dt.float32
BF16 = mybir.dt.bfloat16
AF = mybir.ActivationFunctionType
ALU = mybir.AluOpType
AX = mybir.AxisListType

D = 1024
H = 16
DH = 64
W = 256
KC = 8
NEG = -1.0e9
EPS = 1e-5

W_Q, W_K, W_V, W_O, W_COS, W_SIN = 0, 1, 2, 3, 4, 5
_USE_CYCLE = [W_O, W_K, W_V, W_Q, W_COS, W_SIN]


def use_widx(use_id):
    if use_id < 3:
        return [W_K, W_V, W_Q][use_id]
    return _USE_CYCLE[(use_id - 3) % 6]


class Ctr:

    def __init__(self, sem, name):
        self.sem = sem
        self.name = name
        self.v = 0

    def inc(self, instr, n=1):
        instr.then_inc(self.sem, n)
        self.v += n
        return self.v


class KB:
    def __init__(self):
        self.ev = {}
        self.guards = {}
        self.waited = {}

    def wait(self, eng, ctr, val):
        if val <= 0:
            return
        key = (id(eng), ctr.name)
        if self.waited.get(key, 0) >= val:
            return
        eng.wait_ge(ctr.sem, val)
        self.waited[key] = val

    def bank_wait(self, eng, banks):
        for b in banks:
            for (c, v) in self.guards.get(b, []):
                self.wait(eng, c, v)

    def bank_set(self, banks, entries):
        for b in banks:
            self.guards[b] = list(entries)


def build_nc(seqs=2, nblk=8):
    S = nblk * W
    T = seqs * S
    nc = bass.Bass("TRN2", target_bir_lowering=False)

    x_d = nc.dram_tensor("x", [T, D], F32, kind="ExternalInput").ap()
    y_d = nc.dram_tensor("y", [T, D], F32, kind="ExternalOutput").ap()
    wsrc = [nc.dram_tensor(n, [D, D], F32, kind="ExternalInput").ap()
            for n in ["wqT", "wkT", "wvT", "woT"]]
    tha = nc.dram_tensor("tha", [D, D], F32, kind="ExternalInput").ap()
    thb = nc.dram_tensor("thb", [D, D], F32, kind="ExternalInput").ap()
    pha = nc.dram_tensor("pha", [D, D], F32, kind="ExternalInput").ap()
    phb = nc.dram_tensor("phb", [D, D], F32, kind="ExternalInput").ap()
    maskc_d = nc.dram_tensor("maskc", [128, 1152], BF16, kind="ExternalInput").ap()
    eyebf_d = nc.dram_tensor("eyebf", [128, 128], BF16, kind="ExternalInput").ap()
    eyef_d = nc.dram_tensor("eyef", [128, 128], F32, kind="ExternalInput").ap()
    consts_d = nc.dram_tensor("consts", [128, 8], F32, kind="ExternalInput").ap()
    wsc = nc.dram_tensor("wsc", [6, 2, 128, KC * D], BF16).ap()

    kb = KB()
    ev = kb.ev

    with contextlib.ExitStack() as ctx:
        E = ctx.enter_context
        sb = nc.sbuf_tensor
        x_sb = E(sb("x_sb", [128, 2048], F32))
        x_sb2 = E(sb("x_sb2", [128, 2048], F32))
        fmh2 = E(sb("fmh2", [128, 2048], BF16))
        fml2 = E(sb("fml2", [128, 2048], BF16))
        sc1 = E(sb("sc1", [128, 2048], F32))
        sc2 = E(sb("sc2", [128, 2048], F32))
        fmh = E(sb("fmh", [128, 2048], BF16))
        fml = E(sb("fml", [128, 2048], BF16))
        q_sb = E(sb("q_sb", [128, 2048], F32))
        k_sb = E(sb("k_sb", [128, 4096], F32))
        v_sb = E(sb("v_sb", [128, 4096], F32))
        attn_sb = E(sb("attn_sb", [128, 1536], F32))
        attnT_sb = E(sb("attnT_sb", [128, 1152], F32))
        ao_sb = E(sb("ao_sb", [128, 2048], F32))
        stats = E(sb("stats", [128, 256], F32))
        maskc = E(sb("maskc_s", [128, 1152], BF16))
        MOFF = {0: 0, 1: 384, 2: 768, 3: 896}
        eyebf = E(sb("eyebf_s", [128, 128], BF16))
        eyef = E(sb("eyef_s", [128, 128], F32))
        cst = E(sb("cst_s", [128, 8], F32))
        pb = [E(nc.psum_tensor(f"pb{i}", [128, 512], F32)) for i in range(8)]

        qsl = [Ctr(E(nc.semaphore("qs0")), "qs0"),
               Ctr(E(nc.semaphore("qs1")), "qs1")]
        qc = Ctr(E(nc.semaphore("qc")), "qc")
        qw = [Ctr(E(nc.semaphore("qw0")), "qw0"),
              Ctr(E(nc.semaphore("qw1")), "qw1")]
        qv = [Ctr(E(nc.semaphore("qv0")), "qv0"),
              Ctr(E(nc.semaphore("qv1")), "qv1")]
        qal = [Ctr(E(nc.semaphore("qa0")), "qa0"),
               Ctr(E(nc.semaphore("qa1")), "qa1")]
        st = Ctr(E(nc.semaphore("st")), "st")
        mm = Ctr(E(nc.semaphore("mm")), "mm")
        ve = Ctr(E(nc.semaphore("ve")), "ve")
        ac = Ctr(E(nc.semaphore("ac")), "ac")
        block = E(nc.Block())

        @block.sync
        def _(sync):
            qc.inc(sync.dma_start(out=maskc[:], in_=maskc_d), 16)
            qc.inc(sync.dma_start(out=eyebf[:], in_=eyebf_d), 16)
            qc.inc(sync.dma_start(out=eyef[:], in_=eyef_d), 16)
            qc.inc(sync.dma_start(out=cst[:], in_=consts_d), 16)
        ev["cst"] = qc.v

        qp = [[Ctr(E(nc.semaphore(f"qp{i}_{p}")), f"qp{i}_{p}") for p in range(2)]
              for i in range(6)]
        slot_h, slot_l = [], []

        def alloc_slots():
            for i in range(3):
                slot_h.append(E(sb(f"slot{i}h", [128, KC * D], BF16)))
                slot_l.append(E(sb(f"slot{i}l", [128, KC * D], BF16)))

        def emit_prep():
            CW = 2 * D
            with contextlib.ExitStack() as pctx:
                PE_ = pctx.enter_context
                pa = [PE_(sb(f"pa{i}", [128, CW], F32)) for i in range(2)]
                pbf = [PE_(sb(f"pbf{i}", [128, CW], F32)) for i in range(2)]
                pf = [PE_(sb(f"pf{i}", [128, CW], F32)) for i in range(2)]
                ph = [PE_(sb(f"ph{i}", [128, CW], BF16)) for i in range(2)]
                pl = [PE_(sb(f"pl{i}", [128, CW], BF16)) for i in range(2)]

                for wpos, widx in enumerate([W_K, W_V, W_Q, W_O, W_COS, W_SIN]):
                    for ci, c in enumerate(range(0, KC, 2)):
                        gi = wpos * 4 + ci
                        p = ci % 2
                        trig = widx >= 4
                        if not trig:
                            srcs = [(pa[p], wsrc[widx])]
                        else:
                            srcs = [(pa[p], tha if widx == 4 else pha),
                                    (pbf[p], thb if widx == 4 else phb)]

                        @block.sync
                        def _(sync, p=p, c=c, gi=gi, widx=widx, srcs=srcs, trig=trig):
                            kb.wait(sync, ve, ev.get(f"plog_{gi - 2}", 0))
                            kb.wait(sync, ac, ev.get(f"phig_{gi - 2}", 0))
                            if trig:
                                kb.wait(sync, ac, ev.get(f"pactg_{gi - 2}", 0))
                            for dst, srcd in srcs:
                                src2 = srcd[c * 128:(c + 2) * 128, :].rearrange(
                                    "(a p) d -> p a d", p=128)
                                dst2 = dst[:, :].rearrange("p (a d) -> p a d", d=D)
                                qsl[p].inc(sync.dma_start(out=dst2, in_=src2), 16)
                            ev[f"pld_{widx}_{c}"] = qsl[p].v

                        if trig:
                            @block.vector
                            def _(vector, p=p, c=c, gi=gi, widx=widx):
                                kb.wait(vector, qsl[p], ev[f"pld_{widx}_{c}"])
                                kb.wait(vector, ac, ev.get(f"pactg_{gi - 2}", 0))
                                if widx == 4:
                                    r = float(np.float64(np.pi / 2)
                                              - np.float64(np.float32(np.pi / 2)))
                                    ev[f"padd_{widx}_{c}"] = ve.inc(
                                        vector.scalar_tensor_tensor(
                                            pf[p][:], pa[p][:], r, pbf[p][:],
                                            op0=ALU.add, op1=ALU.add))
                                else:
                                    ev[f"padd_{widx}_{c}"] = ve.inc(vector.tensor_add(
                                        pf[p][:], pa[p][:], pbf[p][:]))

                            @block.scalar
                            def _(scalar, p=p, c=c, gi=gi, widx=widx):
                                kb.wait(scalar, ve, ev[f"padd_{widx}_{c}"])
                                kb.wait(scalar, qc, ev["cst"])
                                if widx == 4:
                                    ins = scalar.activation(pa[p][:], pf[p][:], AF.Sin,
                                                            bias=cst[:, 1:2], scale=1.0)
                                else:
                                    ins = scalar.activation(pa[p][:], pf[p][:], AF.Sin)
                                ev[f"pact_{widx}_{c}"] = ac.inc(ins)
                                ev[f"pactg_{gi}"] = ev[f"pact_{widx}_{c}"]

                        @block.scalar
                        def _(scalar, p=p, c=c, gi=gi, widx=widx, trig=trig):
                            if trig:
                                kb.wait(scalar, ac, ev[f"pact_{widx}_{c}"])
                            else:
                                kb.wait(scalar, qsl[p], ev[f"pld_{widx}_{c}"])
                            pg = ev.get(f"pstg_{gi - 2}")
                            if pg is not None:
                                kb.wait(scalar, pg[0], pg[1])
                            ev[f"phi_{widx}_{c}"] = ac.inc(
                                scalar.activation(ph[p][:], pa[p][:], AF.Copy))
                            ev[f"phig_{gi}"] = ev[f"phi_{widx}_{c}"]

                        @block.vector
                        def _(vector, p=p, c=c, gi=gi, widx=widx):
                            kb.wait(vector, ac, ev[f"phi_{widx}_{c}"])
                            ev[f"plo_{widx}_{c}"] = ve.inc(
                                vector.scalar_tensor_tensor(
                                    pl[p][:], pa[p][:], 1.0, ph[p][:],
                                    op0=ALU.mult, op1=ALU.subtract))
                            ev[f"plog_{gi}"] = ev[f"plo_{widx}_{c}"]

                        @block.gpsimd
                        def _(gpsimd, p=p, c=c, gi=gi, widx=widx):
                            kb.wait(gpsimd, ac, ev[f"phi_{widx}_{c}"])
                            kb.wait(gpsimd, ve, ev[f"plo_{widx}_{c}"])
                            qp[widx][p].inc(gpsimd.dma_start(
                                out=wsc[widx, 0, :, c * D:(c + 2) * D], in_=ph[p][:]), 16)
                            qp[widx][p].inc(gpsimd.dma_start(
                                out=wsc[widx, 1, :, c * D:(c + 2) * D], in_=pl[p][:]), 16)
                            ev[f"pstg_{gi}"] = (qp[widx][p], qp[widx][p].v)
                    ev[f"wsc_ready_{widx}"] = (qp[widx][0].v, qp[widx][1].v)

        total_uses = 3 + 6 * seqs * nblk
        loads_emitted = [0]

        def load_slot(use_id):
            si = use_id % 3
            qq = qw[use_id % 2]
            qq2 = qv[use_id % 2]
            widx = use_widx(use_id)

            @block.sync
            def _(sync):
                for wi in range(6):
                    for pp in range(2):
                        kb.wait(sync, qp[wi][pp], ev[f"wsc_ready_{wi}"][pp])
                if use_id - 2 >= 0:
                    kb.wait(sync, qq, ev[f"use_ld_{use_id - 2}"])
                if use_id - 3 >= 0:
                    kb.wait(sync, mm, ev[f"use_mm_{use_id - 3}"])
                qq.inc(sync.dma_start(out=slot_h[si][:], in_=wsc[widx, 0]), 16)
                ev[f"use_ld_{use_id}"] = qq.v

            @block.gpsimd
            def _(gpsimd):
                for wi in range(6):
                    for pp in range(2):
                        kb.wait(gpsimd, qp[wi][pp], ev[f"wsc_ready_{wi}"][pp])
                if use_id - 2 >= 0:
                    kb.wait(gpsimd, qv[use_id % 2], ev[f"use_lo_{use_id - 2}"])
                if use_id - 3 >= 0:
                    kb.wait(gpsimd, mm, ev[f"use_mm_{use_id - 3}"])
                qq2.inc(gpsimd.dma_start(out=slot_l[si][:], in_=wsc[widx, 1]), 16)
                ev[f"use_lo_{use_id}"] = qq2.v

        def pump_loads(upto):
            while loads_emitted[0] < min(upto, total_uses):
                load_slot(loads_emitted[0])
                loads_emitted[0] += 1

        def emit_ln(g, src_ap_fn, out_ap_fn, tag, extra_waits=(),
                    norm_waits=()):
            P = (g % 2) * 128
            bns0, bns1 = P + 72, P + 84
            agg = P + 96
            stdc, rstdc, nmrc, vepc = P + 100, P + 102, P + 104, P + 106

            @block.vector
            def _(vector):
                for c_, v_ in extra_waits:
                    kb.wait(vector, c_, v_)
                vector.bn_stats(stats[:, bns0:bns0 + 6], src_ap_fn(0, 0))
                vector.bn_stats(stats[:, bns0 + 6:bns0 + 12], src_ap_fn(0, 1))
                vector.bn_stats(stats[:, bns1:bns1 + 6], src_ap_fn(1, 0))
                ev[f"{tag}_bns"] = ve.inc(
                    vector.bn_stats(stats[:, bns1 + 6:bns1 + 12], src_ap_fn(1, 1)))
                kb.wait(vector, ve, ev[f"{tag}_bns"])
                vector.bn_aggr(stats[:, agg:agg + 2], stats[:, bns0:bns0 + 12])
                ev[f"{tag}_agg"] = ve.inc(
                    vector.bn_aggr(stats[:, agg + 2:agg + 4], stats[:, bns1:bns1 + 12]))
                kb.wait(vector, ve, ev[f"{tag}_agg"])
                vector.tensor_scalar_add(stats[:, vepc:vepc + 1],
                                         stats[:, agg + 1:agg + 2], EPS)
                ev[f"{tag}_vep"] = ve.inc(vector.tensor_scalar_add(
                    stats[:, vepc + 1:vepc + 2], stats[:, agg + 3:agg + 4], EPS))

            @block.scalar
            def _(scalar):
                kb.wait(scalar, ve, ev[f"{tag}_vep"])
                ev[f"{tag}_std"] = ac.inc(scalar.activation(
                    stats[:, stdc:stdc + 2], stats[:, vepc:vepc + 2], AF.Sqrt))

            @block.vector
            def _(vector):
                kb.wait(vector, ac, ev[f"{tag}_std"])
                ev[f"{tag}_rstd"] = ve.inc(vector.reciprocal(
                    stats[:, rstdc:rstdc + 2], stats[:, stdc:stdc + 2]))
                kb.wait(vector, ve, ev[f"{tag}_rstd"])
                vector.scalar_tensor_tensor(
                    stats[:, nmrc:nmrc + 1], stats[:, agg:agg + 1], -1.0,
                    stats[:, rstdc:rstdc + 1], op0=ALU.mult, op1=ALU.mult)
                ev[f"{tag}_nmr"] = ve.inc(vector.scalar_tensor_tensor(
                    stats[:, nmrc + 1:nmrc + 2], stats[:, agg + 2:agg + 3], -1.0,
                    stats[:, rstdc + 1:rstdc + 2], op0=ALU.mult, op1=ALU.mult))
                kb.wait(vector, ve, ev[f"{tag}_nmr"])
                for c_, v_ in norm_waits:
                    kb.wait(vector, c_, v_)
                ins = None
                for tt in range(2):
                    for half in range(2):
                        ins = vector.tensor_scalar(
                            out_ap_fn(tt, half), src_ap_fn(tt, half),
                            stats[:, rstdc + tt:rstdc + tt + 1],
                            stats[:, nmrc + tt:nmrc + tt + 1],
                            op0=ALU.mult, op1=ALU.add)
                ev[f"{tag}_norm"] = ve.inc(ins)
            return ev[f"{tag}_norm"]

        def emit_tsplit_tt(g, ht, src_sb, tag, waits=(), banks=(0, 1),
                           dsth=None, dstl=None):
            for bi, b in enumerate(banks):
                @block.tensor
                def _(tensor, bi=bi, b=b):
                    kb.wait(tensor, qc, ev["cst"])
                    for c_, v_ in waits:
                        kb.wait(tensor, c_, v_)
                    kb.bank_wait(tensor, [b])
                    ins = None
                    for i in range(4):
                        kc = bi * 4 + i
                        ins = tensor.transpose(
                            pb[b][:, i * 128:(i + 1) * 128],
                            src_sb[:, ht * 1024 + kc * 128:ht * 1024 + (kc + 1) * 128],
                            eyef[:])
                    ev[f"{tag}_tp{bi}"] = mm.inc(ins)

            dsth = fmh if dsth is None else dsth
            dstl = fml if dstl is None else dstl

            def _fm_dst(t, bi):
                v3 = t[:, :].rearrange("p (kc d) -> p kc d", d=256)
                return v3[:, bi * 4:(bi + 1) * 4, ht * 128:(ht + 1) * 128]

            @block.scalar
            def _(scalar):
                for bi, b in enumerate(banks):
                    kb.wait(scalar, mm, ev[f"{tag}_tp{bi}"])
                    ev[f"{tag}_hi{bi}"] = ac.inc(scalar.activation(
                        _fm_dst(dsth, bi),
                        pb[b][:, 0:512].rearrange("p (kc d) -> p kc d", d=128),
                        AF.Copy))

            @block.vector
            def _(vector):
                for bi, b in enumerate(banks):
                    kb.wait(vector, ac, ev[f"{tag}_hi{bi}"])
                    ev[f"{tag}_lo{bi}"] = ve.inc(vector.scalar_tensor_tensor(
                        _fm_dst(dstl, bi),
                        pb[b][:, 0:512].rearrange("p (kc d) -> p kc d", d=128), 1.0,
                        _fm_dst(dsth, bi), op0=ALU.mult, op1=ALU.subtract))
            kb.bank_set(list(banks), [(ve, ve.v)])
            ev[f"{tag}_hi"] = ev[f"{tag}_hi1"]
            ev[f"{tag}_lo"] = ev[f"{tag}_lo1"]
            return f"{tag}_hi", f"{tag}_lo"

        def emit_tsplit(g, src_sb, tag, waits=(), banks=(0, 1, 2, 3),
                        hi_waits=(), dsth=None, dstl=None):
            for kcp in range(0, KC, 2):
                b = banks[kcp // 2]
                tiles = [(kcp, 0), (kcp, 1), (kcp + 1, 0), (kcp + 1, 1)]

                @block.tensor
                def _(tensor, kcp=kcp, b=b, tiles=tiles):
                    kb.wait(tensor, qc, ev["cst"])
                    for c_, v_ in waits:
                        kb.wait(tensor, c_, v_)
                    kb.bank_wait(tensor, [b])
                    ins = None
                    for i, (kc, tt) in enumerate(tiles):
                        ins = tensor.transpose(
                            pb[b][:, i * 128:(i + 1) * 128],
                            src_sb[:, tt * 1024 + kc * 128:tt * 1024 + (kc + 1) * 128],
                            eyef[:])
                    ev[f"{tag}_tp{kcp}"] = mm.inc(ins)

            @block.scalar
            def _(scalar):
                for c_, v_ in hi_waits:
                    kb.wait(scalar, c_, v_)
                dh_ = fmh if dsth is None else dsth
                for kcp in range(0, KC, 2):
                    kb.wait(scalar, mm, ev[f"{tag}_tp{kcp}"])
                    ev[f"{tag}_hi{kcp}"] = ac.inc(scalar.activation(
                        dh_[:, kcp * 256:(kcp + 2) * 256],
                        pb[banks[kcp // 2]][:], AF.Copy))

            @block.vector
            def _(vector):
                dh_ = fmh if dsth is None else dsth
                dl_ = fml if dstl is None else dstl
                for kcp in range(0, KC, 2):
                    kb.wait(vector, ac, ev[f"{tag}_hi{kcp}"])
                    ev[f"{tag}_lo{kcp}"] = ve.inc(vector.scalar_tensor_tensor(
                        dl_[:, kcp * 256:(kcp + 2) * 256], pb[banks[kcp // 2]][:], 1.0,
                        dh_[:, kcp * 256:(kcp + 2) * 256],
                        op0=ALU.mult, op1=ALU.subtract))
            kb.bank_set(list(banks), [(ve, ve.v)])
            ev[f"{tag}_hi"] = ev[f"{tag}_hi{KC - 2}"]
            ev[f"{tag}_lo"] = ev[f"{tag}_lo{KC - 2}"]
            return f"{tag}_hi", f"{tag}_lo"

        def emit_proj_fm(g, use_id, dst_sb, dst_col0, tag, hi_ev, lo_ev, evac_eng,
                         evac_waits=(), srch=None, srcl=None):
            si = use_id % 3
            for m in range(KC):
                b = m % 4

                @block.tensor
                def _(tensor, m=m, b=b):
                    kb.wait(tensor, qw[use_id % 2], ev[f"use_ld_{use_id}"])
                    kb.wait(tensor, qv[use_id % 2], ev[f"use_lo_{use_id}"])
                    kb.wait(tensor, ac, ev[hi_ev])
                    kb.wait(tensor, ve, ev[lo_ev])
                    kb.bank_wait(tensor, [b])
                    sh_ = fmh if srch is None else srch
                    sl_ = fml if srcl is None else srcl
                    n_mm = 0
                    ins = None
                    for kc in range(KC):
                        wh = slot_h[si][:, kc * D + m * 128:kc * D + (m + 1) * 128]
                        wl = slot_l[si][:, kc * D + m * 128:kc * D + (m + 1) * 128]
                        ah = sh_[:, kc * 256:(kc + 1) * 256]
                        al = sl_[:, kc * 256:(kc + 1) * 256]
                        for lhsT, rhs in [(wh, ah), (wh, al), (wl, ah)]:
                            ins = tensor.matmul(pb[b][:, 0:256], lhsT, rhs,
                                                start=(n_mm == 0), stop=(n_mm == 23))
                            n_mm += 1
                    ev[f"{tag}_mm{m}"] = mm.inc(ins)

                eng_sec = block.vector if evac_eng == "v" else block.scalar
                ctr = ve if evac_eng == "v" else ac

                @eng_sec
                def _(eng, m=m, b=b):
                    for c_, v_ in evac_waits:
                        kb.wait(eng, c_, v_)
                    kb.wait(eng, mm, ev[f"{tag}_mm{m}"])
                    dst = dst_sb[:, dst_col0 + m * 256:dst_col0 + (m + 1) * 256]
                    if evac_eng == "v":
                        ins = eng.tensor_copy(dst, pb[b][:, 0:256])
                    else:
                        ins = eng.activation(dst, pb[b][:, 0:256], AF.Copy)
                    ev[f"{tag}_ev{m}"] = ctr.inc(ins)
                kb.bank_set([b], [(ctr, ctr.v)])
            ev[f"use_mm_{use_id}"] = ev[f"{tag}_mm{KC - 1}"]
            ev[f"{tag}_done"] = ev[f"{tag}_ev{KC - 1}"]

        def emit_proj_tm(g, use_id, tag, hi_ev, lo_ev, banks, extra_waits=(),
                         srch=None, srcl=None):
            si = use_id % 3
            for ti, b in enumerate(banks):
                tt, nh = ti // 2, ti % 2

                @block.tensor
                def _(tensor, tt=tt, nh=nh, b=b):
                    kb.wait(tensor, qw[use_id % 2], ev[f"use_ld_{use_id}"])
                    kb.wait(tensor, qv[use_id % 2], ev[f"use_lo_{use_id}"])
                    for c_, v_ in extra_waits:
                        kb.wait(tensor, c_, v_)
                    kb.wait(tensor, ac, ev[hi_ev])
                    kb.wait(tensor, ve, ev[lo_ev])
                    kb.bank_wait(tensor, [b])
                    sh_ = fmh if srch is None else srch
                    sl_ = fml if srcl is None else srcl
                    n_mm = 0
                    ins = None
                    for kc in range(KC):
                        ah = sh_[:, kc * 256 + tt * 128:kc * 256 + (tt + 1) * 128]
                        al = sl_[:, kc * 256 + tt * 128:kc * 256 + (tt + 1) * 128]
                        wh = slot_h[si][:, kc * D + nh * 512:kc * D + (nh + 1) * 512]
                        wl = slot_l[si][:, kc * D + nh * 512:kc * D + (nh + 1) * 512]
                        for lhsT, rhs in [(ah, wh), (ah, wl), (al, wh)]:
                            ins = tensor.matmul(pb[b][:], lhsT, rhs,
                                                start=(n_mm == 0), stop=(n_mm == 23))
                            n_mm += 1
                    ev[f"{tag}_mm{ti}"] = mm.inc(ins)
            ev[f"use_mm_{use_id}"] = ev[f"{tag}_mm3"]

        nuse = 0
        pending_fm = {}
        pending_tail = [None]

        def _src_x(tt, half):
            return x_sb[:, tt * 1024 + half * 512:tt * 1024 + (half + 1) * 512]

        def _out_sc1(tt, half):
            return sc1[:, tt * 1024 + half * 512:tt * 1024 + (half + 1) * 512]

        def emit_ln1(gg):
            nw = [(mm, ev[f"n3t_{gg - 1}_tp{KC - 2}"])] if gg >= 1 else []
            ev[f"ln1v_{gg}"] = emit_ln(gg, _src_x, _out_sc1, f"ln1_{gg}",
                                       extra_waits=[(qa, ev[f"xload_{gg}"])],
                                       norm_waits=nw)

        def emit_n1t(gg):
            hw_ = [(mm, ev[f"im_{gg - 1}_mm3"])] if gg >= 1 else []
            h_ev, l_ev = emit_tsplit(gg, sc1, f"n1t_{gg}",
                                     waits=[(ve, ev[f"ln1v_{gg}"])],
                                     banks=(4, 5, 6, 7), hi_waits=hw_)
            pending_fm[gg] = (h_ev, l_ev)

        def emit_ln1(gg, g_waits=()):
            def _src(tt, half):
                xbuf = x_sb if gg % 2 == 0 else x_sb2
                return xbuf[:, tt * 1024 + half * 512:tt * 1024 + (half + 1) * 512]

            def _out(tt, half):
                return ao_sb[:, tt * 1024 + half * 512:tt * 1024 + (half + 1) * 512]

            ev[f"ln1v_{gg}"] = emit_ln(gg, _src, _out, f"ln1_{gg}",
                                       extra_waits=[(qal[gg % 2], ev[f"xload_{gg}"])],
                                       norm_waits=g_waits)

        def emit_n1t(gg):
            h_ev, l_ev = emit_tsplit(gg, ao_sb, f"n1t_{gg}",
                                     waits=[(ve, ev[f"ln1v_{gg}"])],
                                     banks=(4, 5, 6, 7))
            pending_fm[gg] = (h_ev, l_ev)

        def emit_kvq(gg, base_use):
            bufg = (gg % nblk) % 2
            h_ev, l_ev = pending_fm.pop(gg)
            emit_proj_fm(gg, base_use, k_sb, bufg * 2048, f"kp_{gg}",
                         h_ev, l_ev, "s")
            pump_loads(base_use + 4)
            emit_proj_tm(gg, base_use + 1, f"vp_{gg}", h_ev, l_ev, [0, 1, 2, 3])
            for ti in range(4):
                tt, nh = ti // 2, ti % 2

                @block.vector
                def _(vector, ti=ti, tt=tt, nh=nh):
                    kb.wait(vector, mm, ev[f"vp_{gg}_mm{ti}"])
                    ev[f"vp_{gg}_ev{ti}"] = ve.inc(vector.tensor_copy(
                        v_sb[:, bufg * 2048 + tt * 1024 + nh * 512:
                             bufg * 2048 + tt * 1024 + (nh + 1) * 512],
                        pb[ti][:]))
                kb.bank_set([ti], [(ve, ve.v)])
            pump_loads(base_use + 5)
            emit_proj_fm(gg, base_use + 2, q_sb, 0, f"qp_{gg}", h_ev, l_ev, "v")
            pump_loads(base_use + 6)

        @block.gpsimd
        def _(gpsimd):
            qal[0].inc(gpsimd.dma_start(out=x_sb[:, 0:1024], in_=x_d[0:128, :]), 16)
            qal[0].inc(gpsimd.dma_start(out=x_sb[:, 1024:2048],
                                        in_=x_d[128:256, :]), 16)
            ev["xload_0"] = qal[0].v
            if seqs * nblk > 1:
                qal[1].inc(gpsimd.dma_start(out=x_sb2[:, 0:1024],
                                            in_=x_d[W + 0:W + 128, :]), 16)
                qal[1].inc(gpsimd.dma_start(out=x_sb2[:, 1024:2048],
                                            in_=x_d[W + 128:W + 256, :]), 16)
                ev["xload_1"] = qal[1].v

        emit_ln1(0)
        emit_n1t(0)
        emit_prep()
        alloc_slots()
        pump_loads(3)
        emit_kvq(0, 0)

        for g in range(seqs * nblk):
            seq, j = g // nblk, g % nblk
            row0 = seq * nblk * W + j * W
            P = (g % 2) * 128
            buf = j % 2
            pbuf = 1 - buf
            last = (g == seqs * nblk - 1)
            xb_g = x_sb if g % 2 == 0 else x_sb2

            def cs_of(qsub):
                if j == 0:
                    return [2] if qsub == 0 else [2, 3]
                return [0, 1, 2] if qsub == 0 else [1, 2, 3]

            mvar = 2 if j == 0 else 0

            SL = 384

            def score_mms(u, qsub, h):
                sb_bank = 4 + (u % 2)
                cs = cs_of(qsub)
                c0, c1 = cs[0], cs[-1]
                ncols = (c1 - c0 + 1) * 128
                m = qsub + mvar
                pr = 64 * (h % 2)
                lq = q_sb[pr:pr + 64,
                          (h // 2) * 256 + qsub * 128:
                          (h // 2) * 256 + (qsub + 1) * 128]
                mms = []
                pcs = [c for c in cs if c < 2]
                ccs = [c for c in cs if c >= 2]
                if pcs:
                    rk = k_sb[pr:pr + 64,
                              pbuf * 2048 + (h // 2) * 256 + pcs[0] * 128:
                              pbuf * 2048 + (h // 2) * 256 + (pcs[-1] + 1) * 128]
                    mms.append(((pcs[0] * 128, (pcs[-1] + 1) * 128), lq, rk))
                if ccs:
                    rk = k_sb[pr:pr + 64,
                              buf * 2048 + (h // 2) * 256 + (ccs[0] - 2) * 128:
                              buf * 2048 + (h // 2) * 256 + (ccs[-1] - 1) * 128]
                    mms.append(((ccs[0] * 128, (ccs[-1] + 1) * 128), lq, rk))
                return sb_bank, (c0 * 128, (c1 + 1) * 128, MOFF[m], ncols), mms

            def pair_scores_t(u):
                specs = [score_mms(uu, *units[uu]) for uu in (u, u + 1)]

                @block.tensor
                def _(tensor, u=u, specs=specs):
                    kb.wait(tensor, ve, ev[f"qp_{g}_done"])
                    kb.wait(tensor, ac, ev[f"kp_{g}_done"])
                    kb.bank_wait(tensor, [specs[0][0], specs[1][0]])
                    for bank, (a, b, mo, ncols), _mms in specs:
                        tensor.matmul(pb[bank][:, a:b], eyebf[:],
                                      maskc[:, mo:mo + ncols],
                                      start=True, stop=False)
                    nmax = max(len(sp[2]) for sp in specs)
                    last = [len(specs[0][2]) - 1, len(specs[1][2]) - 1]
                    ins = None
                    for i in range(nmax):
                        for si, (bank, _mk, mms) in enumerate(specs):
                            if i < len(mms):
                                (a, b), lq, rk = mms[i]
                                ins = tensor.matmul(pb[bank][:, a:b], lq, rk,
                                                    start=False,
                                                    stop=(i == last[si]))
                    ev[f"sc_{g}_{u}"] = mm.inc(ins)
                    ev[f"sc_{g}_{u + 1}"] = ev[f"sc_{g}_{u}"]

            def unit_scores_va(u, qsub, h):
                sb_bank = 4 + (u % 2)
                up = u % 4
                cs = cs_of(qsub)
                c0, c1 = cs[0], cs[-1]

            def pair_va(u):
                specs = []
                for uu in (u, u + 1):
                    qsub, h = units[uu]
                    cs = cs_of(qsub)
                    specs.append((uu, uu % 2, uu % 4, qsub, h, cs[0], cs[-1]))

                @block.vector
                def _(vector, u=u, specs=specs):
                    kb.wait(vector, mm, ev[f"sc_{g}_{u}"])
                    for uu, ub, up, qsub, h, c0, c1 in specs:
                        if uu - 4 >= 0:
                            kb.wait(vector, mm, ev[f"tp_{g}_{uu - 4}"])
                        elif g >= 1:
                            kb.wait(vector, mm, ev[f"tp_{g - 1}_{28 + uu}"])
                        kept = pb[4 + ub][:, c0 * 128:(c1 + 1) * 128]
                        ev[f"mx_{g}_{uu}"] = ve.inc(vector.reduce_max(
                            stats[:, P + up:P + up + 1], kept, axis=AX.X))
                        kb.wait(vector, ve, ev[f"mx_{g}_{uu}"])
                        ev[f"ss_{g}_{uu}"] = ve.inc(vector.tensor_scalar(
                            attn_sb[:, up * SL:up * SL + (c1 - c0 + 1) * 128],
                            kept, stats[:, P + up:P + up + 1], 0.125,
                            op0=ALU.subtract, op1=ALU.mult))
                kb.bank_set([4], [(ve, ev[f"ss_{g}_{u}"])])
                kb.bank_set([5], [(ve, ev[f"ss_{g}_{u + 1}"])])

                @block.scalar
                def _(scalar, u=u, specs=specs):
                    for uu, ub, up, qsub, h, c0, c1 in specs:
                        kb.wait(scalar, ve, ev[f"ss_{g}_{uu}"])
                        ev[f"ex_{g}_{uu}"] = ac.inc(scalar.activation(
                            attn_sb[:, up * SL:up * SL + (c1 - c0 + 1) * 128],
                            attn_sb[:, up * SL:up * SL + (c1 - c0 + 1) * 128],
                            AF.Exp,
                            accum_out=stats[:, P + 8 + qsub * 16 + h:
                                            P + 9 + qsub * 16 + h]))

            def unit_tp(u, qsub, h):
                upA = u % 4
                upT = u % 3
                tp_bank = 6 + (u % 2)
                cs = cs_of(qsub)
                c0 = cs[0]

                @block.tensor
                def _(tensor, u=u, upA=upA, tp_bank=tp_bank, cs=cs, c0=c0):
                    kb.wait(tensor, ac, ev[f"ex_{g}_{u}"])
                    kb.bank_wait(tensor, [tp_bank])
                    ins = None
                    for c in cs:
                        ins = tensor.transpose(
                            pb[tp_bank][:, (c - c0) * 128:(c - c0 + 1) * 128],
                            attn_sb[:, upA * SL + (c - c0) * 128:
                                    upA * SL + (c - c0 + 1) * 128],
                            eyef[:])
                    ev[f"tp_{g}_{u}"] = mm.inc(ins)

                @block.vector
                def _(vector, u=u, upT=upT, tp_bank=tp_bank, cs=cs):
                    kb.wait(vector, mm, ev[f"tp_{g}_{u}"])
                    nc_ = len(cs) * 128
                    ev[f"te_{g}_{u}"] = ve.inc(vector.tensor_copy(
                        attnT_sb[:, upT * SL:upT * SL + nc_],
                        pb[tp_bank][:, 0:nc_]))
                kb.bank_set([tp_bank], [(ve, ve.v)])

            def unit_av(u, qsub, h):
                up = u % 3
                avb = [0, 1] if qsub == 0 else [2, 3]
                b_av = avb[h // 8]
                cs = cs_of(qsub)
                c0 = cs[0]

                @block.tensor
                def _(tensor, u=u, up=up, qsub=qsub, h=h, b_av=b_av, cs=cs, c0=c0):
                    kb.wait(tensor, ve, ev[f"te_{g}_{u}"])
                    kb.wait(tensor, ve, ev[f"vp_{g}_ev3"])
                    if h % 8 == 0:
                        kb.bank_wait(tensor, [b_av])
                    first_in_bank = (h % 8 == 0)
                    last_in_bank = (h % 8 == 7)
                    ins = None
                    for ci, c in enumerate(cs):
                        lhsT = attnT_sb[:, up * SL + (c - c0) * 128:
                                        up * SL + (c - c0 + 1) * 128]
                        vb, vtt = (pbuf, c) if c < 2 else (buf, c - 2)
                        rhs = v_sb[:, vb * 2048 + vtt * 1024 + h * DH:
                                   vb * 2048 + vtt * 1024 + (h + 1) * DH]
                        ins = tensor.matmul(
                            pb[b_av][:, (h % 8) * DH:(h % 8 + 1) * DH], lhsT, rhs,
                            start=(first_in_bank and ci == 0),
                            stop=(last_in_bank and ci == len(cs) - 1))
                    ev[f"av_{g}_{u}"] = mm.inc(ins)

            units = [(qsub, h) for qsub in range(2) for h in range(H)]
            nu = len(units)
            for u in range(nu + 5):
                if 0 <= u - 5 < nu:
                    unit_av(u - 5, *units[u - 5])
                if 0 <= u - 3 < nu:
                    unit_tp(u - 3, *units[u - 3])
                if u < nu and u % 2 == 0:
                    pair_scores_t(u)
                    pair_va(u)
                if pending_tail[0] is not None:
                    while pending_tail[0] and pending_tail[0][0][0] == u:
                        pending_tail[0].pop(0)[1]()
                    if not pending_tail[0]:
                        pending_tail[0] = None

            for qsub in range(2):
                avb = [0, 1] if qsub == 0 else [2, 3]
                u_last = qsub * 16 + 15

                @block.vector
                def _(vector, qsub=qsub, u_last=u_last):
                    kb.wait(vector, ac, ev[f"ex_{g}_{u_last}"])
                    kb.wait(vector, mm, ev[f"av_{g}_{u_last}"])
                    ev[f"rc_{g}_{qsub}"] = ve.inc(vector.reciprocal(
                        stats[:, P + 40 + qsub * 16:P + 56 + qsub * 16],
                        stats[:, P + 8 + qsub * 16:P + 24 + qsub * 16]))
                    kb.wait(vector, ve, ev[f"rc_{g}_{qsub}"])
                    ins = None
                    for h in range(H):
                        b_av = avb[h // 8]
                        rcol = P + 40 + qsub * 16 + h
                        ins = vector.tensor_scalar_mul(
                            ao_sb[:, qsub * 1024 + h * DH:qsub * 1024 + (h + 1) * DH],
                            pb[b_av][:, (h % 8) * DH:(h % 8 + 1) * DH],
                            stats[:, rcol:rcol + 1])
                    ev[f"ao_{g}_{qsub}"] = ve.inc(ins)
                kb.bank_set(avb, [(ve, ev[f"ao_{g}_{qsub}"])])

            emit_tsplit_tt(g, 0, ao_sb, f"aot0_{g}",
                           waits=[(ve, ev[f"ao_{g}_0"])], banks=(0, 1),
                           dsth=fmh2, dstl=fml2)
            h_ev2, l_ev2 = emit_tsplit_tt(g, 1, ao_sb, f"aot1_{g}",
                                          waits=[(ve, ev[f"ao_{g}_1"])], banks=(2, 3),
                                          dsth=fmh2, dstl=fml2)


            if not last:
                emit_ln1(g + 1, g_waits=[(mm, ev[f"aot1_{g}_tp1"])])

            o_use = 3 + 6 * g
            emit_proj_tm(g, o_use, f"op_{g}", h_ev2, l_ev2, [0, 1, 2, 3],
                         extra_waits=[(ac, ev[f"aot0_{g}_hi"]),
                                      (ve, ev[f"aot0_{g}_lo"])],
                         srch=fmh2, srcl=fml2)
            pump_loads(o_use + 4)

            if not last:
                emit_n1t(g + 1)

            @block.vector
            def _(vector):
                ins = None
                for ti in range(4):
                    kb.wait(vector, mm, ev[f"op_{g}_mm{ti}"])
                    ins = vector.tensor_copy(
                        sc2[:, ti * 512:(ti + 1) * 512], pb[ti][:])
                ev[f"oev_{g}"] = ve.inc(ins)
            kb.bank_set([0, 1, 2, 3], [(ve, ev[f"oev_{g}"])])

            def _sc2_ap(tt, half):
                return sc2[:, tt * 1024 + half * 512:tt * 1024 + (half + 1) * 512]

            ln2v = emit_ln(g, _sc2_ap, _sc2_ap, f"ln2_{g}",
                           extra_waits=[(ve, ev[f"oev_{g}"])])

            @block.vector
            def _(vector):
                kb.wait(vector, ve, ln2v)
                ins = None
                for tt in range(2):
                    ins = vector.tensor_add(
                        xb_g[:, tt * 1024:(tt + 1) * 1024],
                        xb_g[:, tt * 1024:(tt + 1) * 1024],
                        sc2[:, tt * 1024:(tt + 1) * 1024])
                ev[f"x2_{g}"] = ve.inc(ins)

            def _src_x2(tt, half):
                return xb_g[:, tt * 1024 + half * 512:tt * 1024 + (half + 1) * 512]

            def _out_sc1(tt, half):
                return sc1[:, tt * 1024 + half * 512:tt * 1024 + (half + 1) * 512]

            ln3v = emit_ln(g, _src_x2, _out_sc1, f"ln3_{g}",
                           extra_waits=[(ve, ev[f"x2_{g}"])],
                           norm_waits=([(st, ev[f"store_{g - 1}"])]
                                       if g >= 1 else []))

            if not last:
                emit_kvq(g + 1, 4 + 6 * g)
            else:
                for uid in range(4 + 6 * g, 7 + 6 * g):
                    ev[f"use_mm_{uid}"] = mm.v
                pump_loads(10 + 6 * g)

            h_ev3, l_ev3 = emit_tsplit(g, sc1, f"n3t_{g}", waits=[(ve, ln3v)],
                                       banks=(4, 5, 6, 7),
                                       hi_waits=[(mm, ev[f"op_{g}_mm3"])],
                                       dsth=fmh2, dstl=fml2)

            cos_use = 7 + 6 * g
            emit_proj_tm(g, cos_use, f"re_{g}", h_ev3, l_ev3, [0, 1, 2, 3],
                         srch=fmh2, srcl=fml2)
            pump_loads(cos_use + 4)

            @block.scalar
            def _(scalar):
                ins = None
                for ti in range(4):
                    kb.wait(scalar, mm, ev[f"re_{g}_mm{ti}"])
                    if not last:
                        kb.wait(scalar, mm, ev[f"n1t_{g + 1}_tp{KC - 2}"])
                    ins = scalar.activation(ao_sb[:, ti * 512:(ti + 1) * 512],
                                            pb[ti][:], AF.Square)
                ev[f"sqre_{g}"] = ac.inc(ins)
            kb.bank_set([0, 1, 2, 3], [(ac, ev[f"sqre_{g}"])])

            emit_proj_tm(g, cos_use + 1, f"im_{g}", h_ev3, l_ev3, [0, 1, 2, 3],
                         srch=fmh2, srcl=fml2)
            pump_loads(cos_use + 5)

            @block.scalar
            def _(scalar):
                ins = None
                for ti in range(4):
                    kb.wait(scalar, mm, ev[f"im_{g}_mm{ti}"])
                    ins = scalar.activation(sc2[:, ti * 512:(ti + 1) * 512],
                                            pb[ti][:], AF.Square)
                ev[f"sq_{g}"] = ac.inc(ins)
            kb.bank_set([0, 1, 2, 3], [(ac, ev[f"sq_{g}"])])

            Pt = (g % 2) * 128
            bns0, bns1 = Pt + 72, Pt + 84
            agg = Pt + 96
            stdc, rstdc, nmrc, vepc = Pt + 100, Pt + 102, Pt + 104, Pt + 106

            def tail_a(g=g):
                @block.vector
                def _(vector):
                    kb.wait(vector, ac, ev[f"sq_{g}"])
                    ev[f"m2_{g}"] = ve.inc(
                        vector.tensor_add(sc1[:], ao_sb[:], sc2[:]))

                @block.scalar
                def _(scalar):
                    kb.wait(scalar, ve, ev[f"m2_{g}"])
                    ev[f"mag_{g}"] = ac.inc(
                        scalar.activation(ao_sb[:], sc1[:], AF.Sqrt))

            def tail_b(g=g):
                @block.vector
                def _(vector):
                    kb.wait(vector, ac, ev[f"mag_{g}"])
                    vector.bn_stats(stats[:, bns0:bns0 + 6], ao_sb[:, 0:512])
                    vector.bn_stats(stats[:, bns0 + 6:bns0 + 12],
                                    ao_sb[:, 512:1024])
                    vector.bn_stats(stats[:, bns1:bns1 + 6], ao_sb[:, 1024:1536])
                    ev[f"ln4_{g}_bns"] = ve.inc(vector.bn_stats(
                        stats[:, bns1 + 6:bns1 + 12], ao_sb[:, 1536:2048]))
                    kb.wait(vector, ve, ev[f"ln4_{g}_bns"])
                    vector.bn_aggr(stats[:, agg:agg + 2], stats[:, bns0:bns0 + 12])
                    ev[f"ln4_{g}_agg"] = ve.inc(vector.bn_aggr(
                        stats[:, agg + 2:agg + 4], stats[:, bns1:bns1 + 12]))
                    kb.wait(vector, ve, ev[f"ln4_{g}_agg"])
                    vector.tensor_scalar_add(stats[:, vepc:vepc + 1],
                                             stats[:, agg + 1:agg + 2], EPS)
                    ev[f"ln4_{g}_vep"] = ve.inc(vector.tensor_scalar_add(
                        stats[:, vepc + 1:vepc + 2], stats[:, agg + 3:agg + 4],
                        EPS))

                @block.scalar
                def _(scalar):
                    kb.wait(scalar, ve, ev[f"ln4_{g}_vep"])
                    ev[f"ln4_{g}_std"] = ac.inc(scalar.activation(
                        stats[:, stdc:stdc + 2], stats[:, vepc:vepc + 2],
                        AF.Sqrt))

            def tail_c1(g=g):
                @block.vector
                def _(vector):
                    kb.wait(vector, ac, ev[f"ln4_{g}_std"])
                    ev[f"ln4_{g}_rstd"] = ve.inc(vector.reciprocal(
                        stats[:, rstdc:rstdc + 2], stats[:, stdc:stdc + 2]))
                    kb.wait(vector, ve, ev[f"ln4_{g}_rstd"])
                    vector.scalar_tensor_tensor(
                        stats[:, nmrc:nmrc + 1], stats[:, agg:agg + 1], -1.0,
                        stats[:, rstdc:rstdc + 1], op0=ALU.mult, op1=ALU.mult)
                    ev[f"ln4_{g}_nmr"] = ve.inc(vector.scalar_tensor_tensor(
                        stats[:, nmrc + 1:nmrc + 2], stats[:, agg + 2:agg + 3],
                        -1.0, stats[:, rstdc + 1:rstdc + 2],
                        op0=ALU.mult, op1=ALU.mult))
                    kb.wait(vector, ve, ev[f"ln4_{g}_nmr"])
                    ins = None
                    for tt in range(2):
                        for half in range(2):
                            ins = vector.tensor_scalar(
                                sc2[:, tt * 1024 + half * 512:
                                    tt * 1024 + (half + 1) * 512],
                                ao_sb[:, tt * 1024 + half * 512:
                                      tt * 1024 + (half + 1) * 512],
                                stats[:, rstdc + tt:rstdc + tt + 1],
                                stats[:, nmrc + tt:nmrc + tt + 1],
                                op0=ALU.mult, op1=ALU.add)
                    ev[f"ln4_{g}_norm"] = ve.inc(ins)

            def tail_c2(g=g, row0=row0, xb_g=xb_g):
                @block.vector
                def _(vector):
                    kb.wait(vector, ve, ev[f"ln4_{g}_norm"])
                    ins = None
                    for tt in range(2):
                        ins = vector.tensor_add(
                            sc1[:, tt * 1024:(tt + 1) * 1024],
                            xb_g[:, tt * 1024:(tt + 1) * 1024],
                            sc2[:, tt * 1024:(tt + 1) * 1024])
                    ev[f"outtt_{g}"] = ve.inc(ins)

                @block.gpsimd
                def _(gpsimd):
                    kb.wait(gpsimd, ve, ev[f"outtt_{g}"])
                    for tt in range(2):
                        st.inc(gpsimd.dma_start(
                            out=y_d[row0 + tt * 128:row0 + (tt + 1) * 128, :],
                            in_=sc1[:, tt * 1024:(tt + 1) * 1024]), 16)
                    ev[f"store_{g}"] = st.v

                if g + 2 < seqs * nblk:
                    nseq, nj = (g + 2) // nblk, (g + 2) % nblk
                    nrow0 = nseq * nblk * W + nj * W

                    @block.gpsimd
                    def _(gpsimd):
                        kb.wait(gpsimd, ve, ev[f"outtt_{g}"])
                        for tt in range(2):
                            qal[g % 2].inc(gpsimd.dma_start(
                                out=xb_g[:, tt * 1024:(tt + 1) * 1024],
                                in_=x_d[nrow0 + tt * 128:
                                        nrow0 + (tt + 1) * 128, :]), 16)
                        ev[f"xload_{g + 2}"] = qal[g % 2].v

            pending_tail[0] = [(2, tail_a), (6, tail_b), (10, tail_c1),
                               (12, tail_c2)]

        if pending_tail[0] is not None:
            for _, _fn in pending_tail[0]:
                _fn()
            pending_tail[0] = None

        @block.scalar
        def _(scalar):
            scalar.wait_ge(st.sem, st.v)

    return nc


def _host_consts():
    qi = np.arange(128)[:, None]
    ki = np.arange(512)[None, :]
    spans = {0: (0, 384), 1: (128, 512), 2: (256, 384), 3: (256, 512)}
    cols = []
    for variant in range(4):
        qsub, is_j0 = variant % 2, variant // 2
        gq = qsub * 128 + qi
        band = (ki >= gq + 1) & (ki <= gq + W)
        if is_j0:
            band = band & (ki >= 256)
        a, b = spans[variant]
        cols.append(np.where(band, 0.0, NEG)[:, a:b])
    maskc = np.concatenate(cols, axis=1).astype(ml_dtypes.bfloat16)
    assert maskc.shape == (128, 1152)
    eyebf = np.eye(128).astype(ml_dtypes.bfloat16)
    eyef = np.eye(128, dtype=np.float32)
    consts = np.zeros((128, 8), dtype=np.float32)
    consts[:, 0] = EPS
    consts[:, 1] = np.pi / 2
    consts[:, 3] = 1.0
    return maskc, eyebf, eyef, consts


def make_in_map(x_shard, wq, wk, wv, wo, tha, thb, pha, phb):
    maskc, eyebf, eyef, consts = _host_consts()
    return {
        "x": np.ascontiguousarray(x_shard.reshape(-1, D)),
        "wqT": np.ascontiguousarray(wq.T),
        "wkT": np.ascontiguousarray(wk.T),
        "wvT": np.ascontiguousarray(wv.T),
        "woT": np.ascontiguousarray(wo.T),
        "tha": np.ascontiguousarray(tha),
        "thb": np.ascontiguousarray(thb),
        "pha": np.ascontiguousarray(pha),
        "phb": np.ascontiguousarray(phb),
        "maskc": maskc, "eyebf": eyebf, "eyef": eyef, "consts": consts,
    }


def kernel(L0_attn_x, L0_attn_wq, L0_attn_wk, L0_attn_wv, L0_attn_wo,
           L0_ffn_theta, L0_ffn_phi, L0_ffn_theta_noise, L0_ffn_phi_noise):
    x = np.asarray(L0_attn_x, dtype=np.float32)
    wq = np.asarray(L0_attn_wq, dtype=np.float32)
    wk = np.asarray(L0_attn_wk, dtype=np.float32)
    wv = np.asarray(L0_attn_wv, dtype=np.float32)
    wo = np.asarray(L0_attn_wo, dtype=np.float32)
    tha = np.asarray(L0_ffn_theta, dtype=np.float32)
    thb = np.asarray(L0_ffn_theta_noise, dtype=np.float32)
    pha = np.asarray(L0_ffn_phi, dtype=np.float32)
    phb = np.asarray(L0_ffn_phi_noise, dtype=np.float32)

    n_cores, seqs, nblk = 8, 2, 8
    nc = build_nc(seqs=seqs, nblk=nblk)
    in_maps = [make_in_map(x[c * seqs:(c + 1) * seqs], wq, wk, wv, wo,
                           tha, thb, pha, phb)
               for c in range(n_cores)]
    res = run_bass_kernel_spmd(nc, in_maps, core_ids=list(range(n_cores)))
    y = np.concatenate(
        [np.asarray(r["y"]).reshape(seqs, nblk * W, D) for r in res.results], axis=0)
    return y


# version 23
# speedup vs baseline: 1.0254x; 1.0190x over previous
import contextlib
import numpy as np
import ml_dtypes
import concourse.bass as bass
import concourse.mybir as mybir
from concourse.bass_utils import run_bass_kernel_spmd

F32 = mybir.dt.float32
BF16 = mybir.dt.bfloat16
AF = mybir.ActivationFunctionType
ALU = mybir.AluOpType
AX = mybir.AxisListType

D = 1024
H = 16
DH = 64
W = 256
KC = 8
NEG = -1.0e9
EPS = 1e-5

W_Q, W_K, W_V, W_O, W_COS, W_SIN = 0, 1, 2, 3, 4, 5
_USE_CYCLE = [W_O, W_K, W_V, W_Q, W_COS, W_SIN]


def use_widx(use_id):
    if use_id < 3:
        return [W_K, W_V, W_Q][use_id]
    return _USE_CYCLE[(use_id - 3) % 6]


class Ctr:

    def __init__(self, sem, name):
        self.sem = sem
        self.name = name
        self.v = 0

    def inc(self, instr, n=1):
        instr.then_inc(self.sem, n)
        self.v += n
        return self.v


class KB:
    def __init__(self):
        self.ev = {}
        self.guards = {}
        self.waited = {}

    def wait(self, eng, ctr, val):
        if val <= 0:
            return
        key = (id(eng), ctr.name)
        if self.waited.get(key, 0) >= val:
            return
        eng.wait_ge(ctr.sem, val)
        self.waited[key] = val

    def bank_wait(self, eng, banks):
        for b in banks:
            for (c, v) in self.guards.get(b, []):
                self.wait(eng, c, v)

    def bank_set(self, banks, entries):
        for b in banks:
            self.guards[b] = list(entries)


def build_nc(seqs=2, nblk=8):
    S = nblk * W
    T = seqs * S
    nc = bass.Bass("TRN2", target_bir_lowering=False)

    x_d = nc.dram_tensor("x", [T, D], F32, kind="ExternalInput").ap()
    y_d = nc.dram_tensor("y", [T, D], F32, kind="ExternalOutput").ap()
    wsrc = [nc.dram_tensor(n, [D, D], F32, kind="ExternalInput").ap()
            for n in ["wqT", "wkT", "wvT", "woT"]]
    tha = nc.dram_tensor("tha", [D, D], F32, kind="ExternalInput").ap()
    thb = nc.dram_tensor("thb", [D, D], F32, kind="ExternalInput").ap()
    pha = nc.dram_tensor("pha", [D, D], F32, kind="ExternalInput").ap()
    phb = nc.dram_tensor("phb", [D, D], F32, kind="ExternalInput").ap()
    maskc_d = nc.dram_tensor("maskc", [128, 1152], BF16, kind="ExternalInput").ap()
    eyebf_d = nc.dram_tensor("eyebf", [128, 128], BF16, kind="ExternalInput").ap()
    eyef_d = nc.dram_tensor("eyef", [128, 128], F32, kind="ExternalInput").ap()
    consts_d = nc.dram_tensor("consts", [128, 8], F32, kind="ExternalInput").ap()
    wsc = nc.dram_tensor("wsc", [6, 2, 128, KC * D], BF16).ap()

    kb = KB()
    ev = kb.ev

    with contextlib.ExitStack() as ctx:
        E = ctx.enter_context
        sb = nc.sbuf_tensor
        x_sb = E(sb("x_sb", [128, 2048], F32))
        x_sb2 = E(sb("x_sb2", [128, 2048], F32))
        fmh2 = E(sb("fmh2", [128, 2048], BF16))
        fml2 = E(sb("fml2", [128, 2048], BF16))
        sc1 = E(sb("sc1", [128, 2048], F32))
        sc2 = E(sb("sc2", [128, 2048], F32))
        fmh = E(sb("fmh", [128, 2048], BF16))
        fml = E(sb("fml", [128, 2048], BF16))
        q_sb = E(sb("q_sb", [128, 2048], F32))
        k_sb = E(sb("k_sb", [128, 4096], F32))
        v_sb = E(sb("v_sb", [128, 4096], F32))
        attn_sb = E(sb("attn_sb", [128, 1536], F32))
        attnT_sb = E(sb("attnT_sb", [128, 1152], F32))
        ao_sb = E(sb("ao_sb", [128, 2048], F32))
        stats = E(sb("stats", [128, 256], F32))
        maskc = E(sb("maskc_s", [128, 1152], BF16))
        MOFF = {0: 0, 1: 384, 2: 768, 3: 896}
        eyebf = E(sb("eyebf_s", [128, 128], BF16))
        eyef = E(sb("eyef_s", [128, 128], F32))
        cst = E(sb("cst_s", [128, 8], F32))
        pb = [E(nc.psum_tensor(f"pb{i}", [128, 512], F32)) for i in range(8)]

        qsl = [Ctr(E(nc.semaphore("qs0")), "qs0"),
               Ctr(E(nc.semaphore("qs1")), "qs1")]
        qc = Ctr(E(nc.semaphore("qc")), "qc")
        qw = [Ctr(E(nc.semaphore("qw0")), "qw0"),
              Ctr(E(nc.semaphore("qw1")), "qw1")]
        qv = [Ctr(E(nc.semaphore("qv0")), "qv0"),
              Ctr(E(nc.semaphore("qv1")), "qv1")]
        qal = [Ctr(E(nc.semaphore("qa0")), "qa0"),
               Ctr(E(nc.semaphore("qa1")), "qa1")]
        st = Ctr(E(nc.semaphore("st")), "st")
        mm = Ctr(E(nc.semaphore("mm")), "mm")
        ve = Ctr(E(nc.semaphore("ve")), "ve")
        ac = Ctr(E(nc.semaphore("ac")), "ac")
        block = E(nc.Block())

        @block.sync
        def _(sync):
            qc.inc(sync.dma_start(out=maskc[:], in_=maskc_d), 16)
            qc.inc(sync.dma_start(out=eyebf[:], in_=eyebf_d), 16)
            qc.inc(sync.dma_start(out=eyef[:], in_=eyef_d), 16)
            qc.inc(sync.dma_start(out=cst[:], in_=consts_d), 16)
        ev["cst"] = qc.v

        qp = [[Ctr(E(nc.semaphore(f"qp{i}_{p}")), f"qp{i}_{p}") for p in range(2)]
              for i in range(6)]
        slot_h, slot_l = [], []

        def alloc_slots():
            for i in range(3):
                slot_h.append(E(sb(f"slot{i}h", [128, KC * D], BF16)))
                slot_l.append(E(sb(f"slot{i}l", [128, KC * D], BF16)))

        def emit_prep():
            CW = 2 * D
            with contextlib.ExitStack() as pctx:
                PE_ = pctx.enter_context
                pa = [PE_(sb(f"pa{i}", [128, CW], F32)) for i in range(2)]
                pbf = [PE_(sb(f"pbf{i}", [128, CW], F32)) for i in range(2)]
                pf = [PE_(sb(f"pf{i}", [128, CW], F32)) for i in range(2)]
                ph = [PE_(sb(f"ph{i}", [128, CW], BF16)) for i in range(2)]
                pl = [PE_(sb(f"pl{i}", [128, CW], BF16)) for i in range(2)]

                for wpos, widx in enumerate([W_K, W_V, W_Q, W_O, W_COS, W_SIN]):
                    for ci, c in enumerate(range(0, KC, 2)):
                        gi = wpos * 4 + ci
                        p = ci % 2
                        trig = widx >= 4
                        if not trig:
                            srcs = [(pa[p], wsrc[widx])]
                        else:
                            srcs = [(pa[p], tha if widx == 4 else pha),
                                    (pbf[p], thb if widx == 4 else phb)]

                        @block.sync
                        def _(sync, p=p, c=c, gi=gi, widx=widx, srcs=srcs, trig=trig):
                            kb.wait(sync, ve, ev.get(f"plog_{gi - 2}", 0))
                            kb.wait(sync, ac, ev.get(f"phig_{gi - 2}", 0))
                            if trig:
                                kb.wait(sync, ac, ev.get(f"pactg_{gi - 2}", 0))
                            for dst, srcd in srcs:
                                src2 = srcd[c * 128:(c + 2) * 128, :].rearrange(
                                    "(a p) d -> p a d", p=128)
                                dst2 = dst[:, :].rearrange("p (a d) -> p a d", d=D)
                                qsl[p].inc(sync.dma_start(out=dst2, in_=src2), 16)
                            ev[f"pld_{widx}_{c}"] = qsl[p].v

                        if trig:
                            @block.vector
                            def _(vector, p=p, c=c, gi=gi, widx=widx):
                                kb.wait(vector, qsl[p], ev[f"pld_{widx}_{c}"])
                                kb.wait(vector, ac, ev.get(f"pactg_{gi - 2}", 0))
                                if widx == 4:
                                    r = float(np.float64(np.pi / 2)
                                              - np.float64(np.float32(np.pi / 2)))
                                    ev[f"padd_{widx}_{c}"] = ve.inc(
                                        vector.scalar_tensor_tensor(
                                            pf[p][:], pa[p][:], r, pbf[p][:],
                                            op0=ALU.add, op1=ALU.add))
                                else:
                                    ev[f"padd_{widx}_{c}"] = ve.inc(vector.tensor_add(
                                        pf[p][:], pa[p][:], pbf[p][:]))

                            @block.scalar
                            def _(scalar, p=p, c=c, gi=gi, widx=widx):
                                kb.wait(scalar, ve, ev[f"padd_{widx}_{c}"])
                                kb.wait(scalar, qc, ev["cst"])
                                if widx == 4:
                                    ins = scalar.activation(pa[p][:], pf[p][:], AF.Sin,
                                                            bias=cst[:, 1:2], scale=1.0)
                                else:
                                    ins = scalar.activation(pa[p][:], pf[p][:], AF.Sin)
                                ev[f"pact_{widx}_{c}"] = ac.inc(ins)
                                ev[f"pactg_{gi}"] = ev[f"pact_{widx}_{c}"]

                        @block.scalar
                        def _(scalar, p=p, c=c, gi=gi, widx=widx, trig=trig):
                            if trig:
                                kb.wait(scalar, ac, ev[f"pact_{widx}_{c}"])
                            else:
                                kb.wait(scalar, qsl[p], ev[f"pld_{widx}_{c}"])
                            pg = ev.get(f"pstg_{gi - 2}")
                            if pg is not None:
                                kb.wait(scalar, pg[0], pg[1])
                            ev[f"phi_{widx}_{c}"] = ac.inc(
                                scalar.activation(ph[p][:], pa[p][:], AF.Copy))
                            ev[f"phig_{gi}"] = ev[f"phi_{widx}_{c}"]

                        @block.vector
                        def _(vector, p=p, c=c, gi=gi, widx=widx):
                            kb.wait(vector, ac, ev[f"phi_{widx}_{c}"])
                            ev[f"plo_{widx}_{c}"] = ve.inc(
                                vector.scalar_tensor_tensor(
                                    pl[p][:], pa[p][:], 1.0, ph[p][:],
                                    op0=ALU.mult, op1=ALU.subtract))
                            ev[f"plog_{gi}"] = ev[f"plo_{widx}_{c}"]

                        @block.gpsimd
                        def _(gpsimd, p=p, c=c, gi=gi, widx=widx):
                            kb.wait(gpsimd, ac, ev[f"phi_{widx}_{c}"])
                            kb.wait(gpsimd, ve, ev[f"plo_{widx}_{c}"])
                            qp[widx][p].inc(gpsimd.dma_start(
                                out=wsc[widx, 0, :, c * D:(c + 2) * D], in_=ph[p][:]), 16)
                            qp[widx][p].inc(gpsimd.dma_start(
                                out=wsc[widx, 1, :, c * D:(c + 2) * D], in_=pl[p][:]), 16)
                            ev[f"pstg_{gi}"] = (qp[widx][p], qp[widx][p].v)
                    ev[f"wsc_ready_{widx}"] = (qp[widx][0].v, qp[widx][1].v)

        total_uses = 3 + 6 * seqs * nblk
        loads_emitted = [0]

        def load_slot(use_id):
            si = use_id % 3
            qq = qw[use_id % 2]
            qq2 = qv[use_id % 2]
            widx = use_widx(use_id)

            @block.sync
            def _(sync):
                for wi in range(6):
                    for pp in range(2):
                        kb.wait(sync, qp[wi][pp], ev[f"wsc_ready_{wi}"][pp])
                if use_id - 2 >= 0:
                    kb.wait(sync, qq, ev[f"use_ld_{use_id - 2}"])
                if use_id - 3 >= 0:
                    kb.wait(sync, mm, ev[f"use_mm_{use_id - 3}"])
                qq.inc(sync.dma_start(out=slot_h[si][:], in_=wsc[widx, 0]), 16)
                ev[f"use_ld_{use_id}"] = qq.v

            @block.gpsimd
            def _(gpsimd):
                for wi in range(6):
                    for pp in range(2):
                        kb.wait(gpsimd, qp[wi][pp], ev[f"wsc_ready_{wi}"][pp])
                if use_id - 2 >= 0:
                    kb.wait(gpsimd, qv[use_id % 2], ev[f"use_lo_{use_id - 2}"])
                if use_id - 3 >= 0:
                    kb.wait(gpsimd, mm, ev[f"use_mm_{use_id - 3}"])
                qq2.inc(gpsimd.dma_start(out=slot_l[si][:], in_=wsc[widx, 1]), 16)
                ev[f"use_lo_{use_id}"] = qq2.v

        def pump_loads(upto):
            while loads_emitted[0] < min(upto, total_uses):
                load_slot(loads_emitted[0])
                loads_emitted[0] += 1

        def emit_ln(g, src_ap_fn, out_ap_fn, tag, extra_waits=(),
                    norm_waits=()):
            P = (g % 2) * 128
            bns0, bns1 = P + 72, P + 84
            agg = P + 96
            stdc, rstdc, nmrc, vepc = P + 100, P + 102, P + 104, P + 106

            @block.vector
            def _(vector):
                for c_, v_ in extra_waits:
                    kb.wait(vector, c_, v_)
                vector.bn_stats(stats[:, bns0:bns0 + 6], src_ap_fn(0, 0))
                vector.bn_stats(stats[:, bns0 + 6:bns0 + 12], src_ap_fn(0, 1))
                vector.bn_stats(stats[:, bns1:bns1 + 6], src_ap_fn(1, 0))
                ev[f"{tag}_bns"] = ve.inc(
                    vector.bn_stats(stats[:, bns1 + 6:bns1 + 12], src_ap_fn(1, 1)))
                kb.wait(vector, ve, ev[f"{tag}_bns"])
                vector.bn_aggr(stats[:, agg:agg + 2], stats[:, bns0:bns0 + 12])
                ev[f"{tag}_agg"] = ve.inc(
                    vector.bn_aggr(stats[:, agg + 2:agg + 4], stats[:, bns1:bns1 + 12]))
                kb.wait(vector, ve, ev[f"{tag}_agg"])
                vector.tensor_scalar_add(stats[:, vepc:vepc + 1],
                                         stats[:, agg + 1:agg + 2], EPS)
                ev[f"{tag}_vep"] = ve.inc(vector.tensor_scalar_add(
                    stats[:, vepc + 1:vepc + 2], stats[:, agg + 3:agg + 4], EPS))

            @block.scalar
            def _(scalar):
                kb.wait(scalar, ve, ev[f"{tag}_vep"])
                ev[f"{tag}_std"] = ac.inc(scalar.activation(
                    stats[:, stdc:stdc + 2], stats[:, vepc:vepc + 2], AF.Sqrt))

            @block.vector
            def _(vector):
                kb.wait(vector, ac, ev[f"{tag}_std"])
                ev[f"{tag}_rstd"] = ve.inc(vector.reciprocal(
                    stats[:, rstdc:rstdc + 2], stats[:, stdc:stdc + 2]))
                kb.wait(vector, ve, ev[f"{tag}_rstd"])
                vector.scalar_tensor_tensor(
                    stats[:, nmrc:nmrc + 1], stats[:, agg:agg + 1], -1.0,
                    stats[:, rstdc:rstdc + 1], op0=ALU.mult, op1=ALU.mult)
                ev[f"{tag}_nmr"] = ve.inc(vector.scalar_tensor_tensor(
                    stats[:, nmrc + 1:nmrc + 2], stats[:, agg + 2:agg + 3], -1.0,
                    stats[:, rstdc + 1:rstdc + 2], op0=ALU.mult, op1=ALU.mult))
                kb.wait(vector, ve, ev[f"{tag}_nmr"])
                for c_, v_ in norm_waits:
                    kb.wait(vector, c_, v_)
                ins = None
                for tt in range(2):
                    for half in range(2):
                        ins = vector.tensor_scalar(
                            out_ap_fn(tt, half), src_ap_fn(tt, half),
                            stats[:, rstdc + tt:rstdc + tt + 1],
                            stats[:, nmrc + tt:nmrc + tt + 1],
                            op0=ALU.mult, op1=ALU.add)
                ev[f"{tag}_norm"] = ve.inc(ins)
            return ev[f"{tag}_norm"]

        def emit_tsplit_tt(g, ht, src_sb, tag, waits=(), banks=(0, 1),
                           dsth=None, dstl=None):
            for bi, b in enumerate(banks):
                @block.tensor
                def _(tensor, bi=bi, b=b):
                    kb.wait(tensor, qc, ev["cst"])
                    for c_, v_ in waits:
                        kb.wait(tensor, c_, v_)
                    kb.bank_wait(tensor, [b])
                    ins = None
                    for i in range(4):
                        kc = bi * 4 + i
                        ins = tensor.transpose(
                            pb[b][:, i * 128:(i + 1) * 128],
                            src_sb[:, ht * 1024 + kc * 128:ht * 1024 + (kc + 1) * 128],
                            eyef[:])
                    ev[f"{tag}_tp{bi}"] = mm.inc(ins)

            dsth = fmh if dsth is None else dsth
            dstl = fml if dstl is None else dstl

            def _fm_dst(t, bi):
                v3 = t[:, :].rearrange("p (kc d) -> p kc d", d=256)
                return v3[:, bi * 4:(bi + 1) * 4, ht * 128:(ht + 1) * 128]

            @block.scalar
            def _(scalar):
                for bi, b in enumerate(banks):
                    kb.wait(scalar, mm, ev[f"{tag}_tp{bi}"])
                    ev[f"{tag}_hi{bi}"] = ac.inc(scalar.activation(
                        _fm_dst(dsth, bi),
                        pb[b][:, 0:512].rearrange("p (kc d) -> p kc d", d=128),
                        AF.Copy))

            @block.vector
            def _(vector):
                for bi, b in enumerate(banks):
                    kb.wait(vector, ac, ev[f"{tag}_hi{bi}"])
                    ev[f"{tag}_lo{bi}"] = ve.inc(vector.scalar_tensor_tensor(
                        _fm_dst(dstl, bi),
                        pb[b][:, 0:512].rearrange("p (kc d) -> p kc d", d=128), 1.0,
                        _fm_dst(dsth, bi), op0=ALU.mult, op1=ALU.subtract))
            kb.bank_set(list(banks), [(ve, ve.v)])
            ev[f"{tag}_hi"] = ev[f"{tag}_hi1"]
            ev[f"{tag}_lo"] = ev[f"{tag}_lo1"]
            return f"{tag}_hi", f"{tag}_lo"

        def emit_tsplit(g, src_sb, tag, waits=(), banks=(0, 1, 2, 3),
                        hi_waits=(), dsth=None, dstl=None):
            for kcp in range(0, KC, 2):
                b = banks[kcp // 2]
                tiles = [(kcp, 0), (kcp, 1), (kcp + 1, 0), (kcp + 1, 1)]

                @block.tensor
                def _(tensor, kcp=kcp, b=b, tiles=tiles):
                    kb.wait(tensor, qc, ev["cst"])
                    for c_, v_ in waits:
                        kb.wait(tensor, c_, v_)
                    kb.bank_wait(tensor, [b])
                    ins = None
                    for i, (kc, tt) in enumerate(tiles):
                        ins = tensor.transpose(
                            pb[b][:, i * 128:(i + 1) * 128],
                            src_sb[:, tt * 1024 + kc * 128:tt * 1024 + (kc + 1) * 128],
                            eyef[:])
                    ev[f"{tag}_tp{kcp}"] = mm.inc(ins)

            @block.scalar
            def _(scalar):
                for c_, v_ in hi_waits:
                    kb.wait(scalar, c_, v_)
                dh_ = fmh if dsth is None else dsth
                for kcp in range(0, KC, 2):
                    kb.wait(scalar, mm, ev[f"{tag}_tp{kcp}"])
                    ev[f"{tag}_hi{kcp}"] = ac.inc(scalar.activation(
                        dh_[:, kcp * 256:(kcp + 2) * 256],
                        pb[banks[kcp // 2]][:], AF.Copy))

            @block.vector
            def _(vector):
                dh_ = fmh if dsth is None else dsth
                dl_ = fml if dstl is None else dstl
                for kcp in range(0, KC, 2):
                    kb.wait(vector, ac, ev[f"{tag}_hi{kcp}"])
                    ev[f"{tag}_lo{kcp}"] = ve.inc(vector.scalar_tensor_tensor(
                        dl_[:, kcp * 256:(kcp + 2) * 256], pb[banks[kcp // 2]][:], 1.0,
                        dh_[:, kcp * 256:(kcp + 2) * 256],
                        op0=ALU.mult, op1=ALU.subtract))
            kb.bank_set(list(banks), [(ve, ve.v)])
            ev[f"{tag}_hi"] = ev[f"{tag}_hi{KC - 2}"]
            ev[f"{tag}_lo"] = ev[f"{tag}_lo{KC - 2}"]
            return f"{tag}_hi", f"{tag}_lo"

        def emit_proj_fm(g, use_id, dst_sb, dst_col0, tag, hi_ev, lo_ev, evac_eng,
                         evac_waits=(), srch=None, srcl=None):
            si = use_id % 3
            for m in range(KC):
                b = m % 4

                @block.tensor
                def _(tensor, m=m, b=b):
                    kb.wait(tensor, qw[use_id % 2], ev[f"use_ld_{use_id}"])
                    kb.wait(tensor, qv[use_id % 2], ev[f"use_lo_{use_id}"])
                    kb.wait(tensor, ac, ev[hi_ev])
                    kb.wait(tensor, ve, ev[lo_ev])
                    kb.bank_wait(tensor, [b])
                    sh_ = fmh if srch is None else srch
                    sl_ = fml if srcl is None else srcl
                    n_mm = 0
                    ins = None
                    for kc in range(KC):
                        wh = slot_h[si][:, kc * D + m * 128:kc * D + (m + 1) * 128]
                        wl = slot_l[si][:, kc * D + m * 128:kc * D + (m + 1) * 128]
                        ah = sh_[:, kc * 256:(kc + 1) * 256]
                        al = sl_[:, kc * 256:(kc + 1) * 256]
                        for lhsT, rhs in [(wh, ah), (wh, al), (wl, ah)]:
                            ins = tensor.matmul(pb[b][:, 0:256], lhsT, rhs,
                                                start=(n_mm == 0), stop=(n_mm == 23))
                            n_mm += 1
                    ev[f"{tag}_mm{m}"] = mm.inc(ins)

                eng_sec = block.vector if evac_eng == "v" else block.scalar
                ctr = ve if evac_eng == "v" else ac

                @eng_sec
                def _(eng, m=m, b=b):
                    for c_, v_ in evac_waits:
                        kb.wait(eng, c_, v_)
                    kb.wait(eng, mm, ev[f"{tag}_mm{m}"])
                    dst = dst_sb[:, dst_col0 + m * 256:dst_col0 + (m + 1) * 256]
                    if evac_eng == "v":
                        ins = eng.tensor_copy(dst, pb[b][:, 0:256])
                    else:
                        ins = eng.activation(dst, pb[b][:, 0:256], AF.Copy)
                    ev[f"{tag}_ev{m}"] = ctr.inc(ins)
                kb.bank_set([b], [(ctr, ctr.v)])
            ev[f"use_mm_{use_id}"] = ev[f"{tag}_mm{KC - 1}"]
            ev[f"{tag}_done"] = ev[f"{tag}_ev{KC - 1}"]

        def emit_proj_tm(g, use_id, tag, hi_ev, lo_ev, banks, extra_waits=(),
                         srch=None, srcl=None, tile_waits=None):
            si = use_id % 3
            for ti, b in enumerate(banks):
                tt, nh = ti // 2, ti % 2

                @block.tensor
                def _(tensor, tt=tt, nh=nh, b=b, ti=ti):
                    kb.wait(tensor, qw[use_id % 2], ev[f"use_ld_{use_id}"])
                    kb.wait(tensor, qv[use_id % 2], ev[f"use_lo_{use_id}"])
                    if tile_waits is not None:
                        for c_, v_ in tile_waits[ti]:
                            kb.wait(tensor, c_, v_)
                    else:
                        for c_, v_ in extra_waits:
                            kb.wait(tensor, c_, v_)
                        kb.wait(tensor, ac, ev[hi_ev])
                        kb.wait(tensor, ve, ev[lo_ev])
                    kb.bank_wait(tensor, [b])
                    sh_ = fmh if srch is None else srch
                    sl_ = fml if srcl is None else srcl
                    n_mm = 0
                    ins = None
                    for kc in range(KC):
                        ah = sh_[:, kc * 256 + tt * 128:kc * 256 + (tt + 1) * 128]
                        al = sl_[:, kc * 256 + tt * 128:kc * 256 + (tt + 1) * 128]
                        wh = slot_h[si][:, kc * D + nh * 512:kc * D + (nh + 1) * 512]
                        wl = slot_l[si][:, kc * D + nh * 512:kc * D + (nh + 1) * 512]
                        for lhsT, rhs in [(ah, wh), (ah, wl), (al, wh)]:
                            ins = tensor.matmul(pb[b][:], lhsT, rhs,
                                                start=(n_mm == 0), stop=(n_mm == 23))
                            n_mm += 1
                    ev[f"{tag}_mm{ti}"] = mm.inc(ins)
            ev[f"use_mm_{use_id}"] = ev[f"{tag}_mm3"]

        nuse = 0
        pending_fm = {}
        pending_tail = [None]

        def _src_x(tt, half):
            return x_sb[:, tt * 1024 + half * 512:tt * 1024 + (half + 1) * 512]

        def _out_sc1(tt, half):
            return sc1[:, tt * 1024 + half * 512:tt * 1024 + (half + 1) * 512]

        def emit_ln1(gg):
            nw = [(mm, ev[f"n3t_{gg - 1}_tp{KC - 2}"])] if gg >= 1 else []
            ev[f"ln1v_{gg}"] = emit_ln(gg, _src_x, _out_sc1, f"ln1_{gg}",
                                       extra_waits=[(qa, ev[f"xload_{gg}"])],
                                       norm_waits=nw)

        def emit_n1t(gg):
            hw_ = [(mm, ev[f"im_{gg - 1}_mm3"])] if gg >= 1 else []
            h_ev, l_ev = emit_tsplit(gg, sc1, f"n1t_{gg}",
                                     waits=[(ve, ev[f"ln1v_{gg}"])],
                                     banks=(4, 5, 6, 7), hi_waits=hw_)
            pending_fm[gg] = (h_ev, l_ev)

        def emit_ln1(gg, g_waits=()):
            def _src(tt, half):
                xbuf = x_sb if gg % 2 == 0 else x_sb2
                return xbuf[:, tt * 1024 + half * 512:tt * 1024 + (half + 1) * 512]

            def _out(tt, half):
                return ao_sb[:, tt * 1024 + half * 512:tt * 1024 + (half + 1) * 512]

            ev[f"ln1v_{gg}"] = emit_ln(gg, _src, _out, f"ln1_{gg}",
                                       extra_waits=[(qal[gg % 2], ev[f"xload_{gg}"])],
                                       norm_waits=g_waits)

        def emit_n1t(gg):
            h_ev, l_ev = emit_tsplit(gg, ao_sb, f"n1t_{gg}",
                                     waits=[(ve, ev[f"ln1v_{gg}"])],
                                     banks=(4, 5, 6, 7))
            pending_fm[gg] = (h_ev, l_ev)

        def emit_kvq(gg, base_use):
            bufg = (gg % nblk) % 2
            h_ev, l_ev = pending_fm.pop(gg)
            emit_proj_fm(gg, base_use, k_sb, bufg * 2048, f"kp_{gg}",
                         h_ev, l_ev, "s")
            pump_loads(base_use + 4)
            emit_proj_tm(gg, base_use + 1, f"vp_{gg}", h_ev, l_ev, [0, 1, 2, 3])
            for ti in range(4):
                tt, nh = ti // 2, ti % 2

                @block.vector
                def _(vector, ti=ti, tt=tt, nh=nh):
                    kb.wait(vector, mm, ev[f"vp_{gg}_mm{ti}"])
                    ev[f"vp_{gg}_ev{ti}"] = ve.inc(vector.tensor_copy(
                        v_sb[:, bufg * 2048 + tt * 1024 + nh * 512:
                             bufg * 2048 + tt * 1024 + (nh + 1) * 512],
                        pb[ti][:]))
                kb.bank_set([ti], [(ve, ve.v)])
            pump_loads(base_use + 5)
            emit_proj_fm(gg, base_use + 2, q_sb, 0, f"qp_{gg}", h_ev, l_ev, "v")
            pump_loads(base_use + 6)

        @block.gpsimd
        def _(gpsimd):
            qal[0].inc(gpsimd.dma_start(out=x_sb[:, 0:1024], in_=x_d[0:128, :]), 16)
            qal[0].inc(gpsimd.dma_start(out=x_sb[:, 1024:2048],
                                        in_=x_d[128:256, :]), 16)
            ev["xload_0"] = qal[0].v
            if seqs * nblk > 1:
                qal[1].inc(gpsimd.dma_start(out=x_sb2[:, 0:1024],
                                            in_=x_d[W + 0:W + 128, :]), 16)
                qal[1].inc(gpsimd.dma_start(out=x_sb2[:, 1024:2048],
                                            in_=x_d[W + 128:W + 256, :]), 16)
                ev["xload_1"] = qal[1].v

        emit_ln1(0)
        emit_n1t(0)
        emit_prep()
        alloc_slots()
        pump_loads(3)
        emit_kvq(0, 0)

        for g in range(seqs * nblk):
            seq, j = g // nblk, g % nblk
            row0 = seq * nblk * W + j * W
            P = (g % 2) * 128
            buf = j % 2
            pbuf = 1 - buf
            last = (g == seqs * nblk - 1)
            xb_g = x_sb if g % 2 == 0 else x_sb2

            def cs_of(qsub):
                if j == 0:
                    return [2] if qsub == 0 else [2, 3]
                return [0, 1, 2] if qsub == 0 else [1, 2, 3]

            mvar = 2 if j == 0 else 0

            SL = 384

            def score_mms(u, qsub, h):
                sb_bank = 4 + (u % 2)
                cs = cs_of(qsub)
                c0, c1 = cs[0], cs[-1]
                ncols = (c1 - c0 + 1) * 128
                m = qsub + mvar
                pr = 64 * (h % 2)
                lq = q_sb[pr:pr + 64,
                          (h // 2) * 256 + qsub * 128:
                          (h // 2) * 256 + (qsub + 1) * 128]
                mms = []
                pcs = [c for c in cs if c < 2]
                ccs = [c for c in cs if c >= 2]
                if pcs:
                    rk = k_sb[pr:pr + 64,
                              pbuf * 2048 + (h // 2) * 256 + pcs[0] * 128:
                              pbuf * 2048 + (h // 2) * 256 + (pcs[-1] + 1) * 128]
                    mms.append(((pcs[0] * 128, (pcs[-1] + 1) * 128), lq, rk))
                if ccs:
                    rk = k_sb[pr:pr + 64,
                              buf * 2048 + (h // 2) * 256 + (ccs[0] - 2) * 128:
                              buf * 2048 + (h // 2) * 256 + (ccs[-1] - 1) * 128]
                    mms.append(((ccs[0] * 128, (ccs[-1] + 1) * 128), lq, rk))
                return sb_bank, (c0 * 128, (c1 + 1) * 128, MOFF[m], ncols), mms

            def pair_scores_t(u):
                specs = [score_mms(uu, *units[uu]) for uu in (u, u + 1)]

                @block.tensor
                def _(tensor, u=u, specs=specs):
                    kb.wait(tensor, ve, ev[f"qp_{g}_done"])
                    kb.wait(tensor, ac, ev[f"kp_{g}_done"])
                    kb.bank_wait(tensor, [specs[0][0], specs[1][0]])
                    for bank, (a, b, mo, ncols), _mms in specs:
                        tensor.matmul(pb[bank][:, a:b], eyebf[:],
                                      maskc[:, mo:mo + ncols],
                                      start=True, stop=False)
                    nmax = max(len(sp[2]) for sp in specs)
                    last = [len(specs[0][2]) - 1, len(specs[1][2]) - 1]
                    ins = None
                    for i in range(nmax):
                        for si, (bank, _mk, mms) in enumerate(specs):
                            if i < len(mms):
                                (a, b), lq, rk = mms[i]
                                ins = tensor.matmul(pb[bank][:, a:b], lq, rk,
                                                    start=False,
                                                    stop=(i == last[si]))
                    ev[f"sc_{g}_{u}"] = mm.inc(ins)
                    ev[f"sc_{g}_{u + 1}"] = ev[f"sc_{g}_{u}"]

            def unit_scores_va(u, qsub, h):
                sb_bank = 4 + (u % 2)
                up = u % 4
                cs = cs_of(qsub)
                c0, c1 = cs[0], cs[-1]

            def pair_va(u):
                specs = []
                for uu in (u, u + 1):
                    qsub, h = units[uu]
                    cs = cs_of(qsub)
                    specs.append((uu, uu % 2, uu % 4, qsub, h, cs[0], cs[-1]))

                @block.vector
                def _(vector, u=u, specs=specs):
                    kb.wait(vector, mm, ev[f"sc_{g}_{u}"])
                    for uu, ub, up, qsub, h, c0, c1 in specs:
                        if uu - 4 >= 0:
                            kb.wait(vector, mm, ev[f"tp_{g}_{uu - 4}"])
                        elif g >= 1:
                            kb.wait(vector, mm, ev[f"tp_{g - 1}_{28 + uu}"])
                        kept = pb[4 + ub][:, c0 * 128:(c1 + 1) * 128]
                        ev[f"mx_{g}_{uu}"] = ve.inc(vector.reduce_max(
                            stats[:, P + up:P + up + 1], kept, axis=AX.X))
                        kb.wait(vector, ve, ev[f"mx_{g}_{uu}"])
                        ev[f"ss_{g}_{uu}"] = ve.inc(vector.tensor_scalar(
                            attn_sb[:, up * SL:up * SL + (c1 - c0 + 1) * 128],
                            kept, stats[:, P + up:P + up + 1], 0.125,
                            op0=ALU.subtract, op1=ALU.mult))
                kb.bank_set([4], [(ve, ev[f"ss_{g}_{u}"])])
                kb.bank_set([5], [(ve, ev[f"ss_{g}_{u + 1}"])])

                @block.scalar
                def _(scalar, u=u, specs=specs):
                    for uu, ub, up, qsub, h, c0, c1 in specs:
                        kb.wait(scalar, ve, ev[f"ss_{g}_{uu}"])
                        ev[f"ex_{g}_{uu}"] = ac.inc(scalar.activation(
                            attn_sb[:, up * SL:up * SL + (c1 - c0 + 1) * 128],
                            attn_sb[:, up * SL:up * SL + (c1 - c0 + 1) * 128],
                            AF.Exp,
                            accum_out=stats[:, P + 8 + qsub * 16 + h:
                                            P + 9 + qsub * 16 + h]))

            def unit_tp(u, qsub, h):
                upA = u % 4
                upT = u % 3
                tp_bank = 6 + (u % 2)
                cs = cs_of(qsub)
                c0 = cs[0]

                @block.tensor
                def _(tensor, u=u, upA=upA, tp_bank=tp_bank, cs=cs, c0=c0):
                    kb.wait(tensor, ac, ev[f"ex_{g}_{u}"])
                    kb.bank_wait(tensor, [tp_bank])
                    ins = None
                    for c in cs:
                        ins = tensor.transpose(
                            pb[tp_bank][:, (c - c0) * 128:(c - c0 + 1) * 128],
                            attn_sb[:, upA * SL + (c - c0) * 128:
                                    upA * SL + (c - c0 + 1) * 128],
                            eyef[:])
                    ev[f"tp_{g}_{u}"] = mm.inc(ins)

                @block.vector
                def _(vector, u=u, upT=upT, tp_bank=tp_bank, cs=cs):
                    kb.wait(vector, mm, ev[f"tp_{g}_{u}"])
                    nc_ = len(cs) * 128
                    ev[f"te_{g}_{u}"] = ve.inc(vector.tensor_copy(
                        attnT_sb[:, upT * SL:upT * SL + nc_],
                        pb[tp_bank][:, 0:nc_]))
                kb.bank_set([tp_bank], [(ve, ve.v)])

            def unit_av(u, qsub, h):
                up = u % 3
                avb = [0, 1] if qsub == 0 else [2, 3]
                b_av = avb[h // 8]
                cs = cs_of(qsub)
                c0 = cs[0]

                @block.tensor
                def _(tensor, u=u, up=up, qsub=qsub, h=h, b_av=b_av, cs=cs, c0=c0):
                    kb.wait(tensor, ve, ev[f"te_{g}_{u}"])
                    kb.wait(tensor, ve, ev[f"vp_{g}_ev3"])
                    if h % 8 == 0:
                        kb.bank_wait(tensor, [b_av])
                    first_in_bank = (h % 8 == 0)
                    last_in_bank = (h % 8 == 7)
                    ins = None
                    for ci, c in enumerate(cs):
                        lhsT = attnT_sb[:, up * SL + (c - c0) * 128:
                                        up * SL + (c - c0 + 1) * 128]
                        vb, vtt = (pbuf, c) if c < 2 else (buf, c - 2)
                        rhs = v_sb[:, vb * 2048 + vtt * 1024 + h * DH:
                                   vb * 2048 + vtt * 1024 + (h + 1) * DH]
                        ins = tensor.matmul(
                            pb[b_av][:, (h % 8) * DH:(h % 8 + 1) * DH], lhsT, rhs,
                            start=(first_in_bank and ci == 0),
                            stop=(last_in_bank and ci == len(cs) - 1))
                    ev[f"av_{g}_{u}"] = mm.inc(ins)

            units = [(qsub, h) for qsub in range(2) for h in range(H)]
            nu = len(units)
            for u in range(nu + 5):
                if 0 <= u - 5 < nu:
                    unit_av(u - 5, *units[u - 5])
                if 0 <= u - 3 < nu:
                    unit_tp(u - 3, *units[u - 3])
                if u < nu and u % 2 == 0:
                    pair_scores_t(u)
                    pair_va(u)
                if pending_tail[0] is not None:
                    while pending_tail[0] and pending_tail[0][0][0] == u:
                        pending_tail[0].pop(0)[1]()
                    if not pending_tail[0]:
                        pending_tail[0] = None

            for qsub in range(2):
                avb = [0, 1] if qsub == 0 else [2, 3]
                u_last = qsub * 16 + 15

                @block.vector
                def _(vector, qsub=qsub, u_last=u_last):
                    kb.wait(vector, ac, ev[f"ex_{g}_{u_last}"])
                    kb.wait(vector, mm, ev[f"av_{g}_{u_last}"])
                    ev[f"rc_{g}_{qsub}"] = ve.inc(vector.reciprocal(
                        stats[:, P + 40 + qsub * 16:P + 56 + qsub * 16],
                        stats[:, P + 8 + qsub * 16:P + 24 + qsub * 16]))
                    kb.wait(vector, ve, ev[f"rc_{g}_{qsub}"])
                    ins = None
                    for h in range(H):
                        b_av = avb[h // 8]
                        rcol = P + 40 + qsub * 16 + h
                        ins = vector.tensor_scalar_mul(
                            ao_sb[:, qsub * 1024 + h * DH:qsub * 1024 + (h + 1) * DH],
                            pb[b_av][:, (h % 8) * DH:(h % 8 + 1) * DH],
                            stats[:, rcol:rcol + 1])
                    ev[f"ao_{g}_{qsub}"] = ve.inc(ins)
                kb.bank_set(avb, [(ve, ev[f"ao_{g}_{qsub}"])])

            emit_tsplit_tt(g, 0, ao_sb, f"aot0_{g}",
                           waits=[(ve, ev[f"ao_{g}_0"])], banks=(0, 1),
                           dsth=fmh2, dstl=fml2)
            h_ev2, l_ev2 = emit_tsplit_tt(g, 1, ao_sb, f"aot1_{g}",
                                          waits=[(ve, ev[f"ao_{g}_1"])], banks=(2, 3),
                                          dsth=fmh2, dstl=fml2)


            if not last:
                emit_ln1(g + 1, g_waits=[(mm, ev[f"aot1_{g}_tp1"])])

            o_use = 3 + 6 * g
            _ow0 = [(ac, ev[f"aot0_{g}_hi"]), (ve, ev[f"aot0_{g}_lo"])]
            _ow1 = [(ac, ev[f"aot1_{g}_hi"]), (ve, ev[f"aot1_{g}_lo"])]
            emit_proj_tm(g, o_use, f"op_{g}", h_ev2, l_ev2, [0, 1, 2, 3],
                         tile_waits=[_ow0, _ow0, _ow1, _ow1],
                         srch=fmh2, srcl=fml2)
            pump_loads(o_use + 4)

            if not last:
                emit_n1t(g + 1)

            @block.vector
            def _(vector):
                ins = None
                for ti in range(4):
                    kb.wait(vector, mm, ev[f"op_{g}_mm{ti}"])
                    ins = vector.tensor_copy(
                        sc2[:, ti * 512:(ti + 1) * 512], pb[ti][:])
                ev[f"oev_{g}"] = ve.inc(ins)
            kb.bank_set([0, 1, 2, 3], [(ve, ev[f"oev_{g}"])])

            def _sc2_ap(tt, half):
                return sc2[:, tt * 1024 + half * 512:tt * 1024 + (half + 1) * 512]

            ln2v = emit_ln(g, _sc2_ap, _sc2_ap, f"ln2_{g}",
                           extra_waits=[(ve, ev[f"oev_{g}"])])

            @block.vector
            def _(vector):
                kb.wait(vector, ve, ln2v)
                ins = None
                for tt in range(2):
                    ins = vector.tensor_add(
                        xb_g[:, tt * 1024:(tt + 1) * 1024],
                        xb_g[:, tt * 1024:(tt + 1) * 1024],
                        sc2[:, tt * 1024:(tt + 1) * 1024])
                ev[f"x2_{g}"] = ve.inc(ins)

            def _src_x2(tt, half):
                return xb_g[:, tt * 1024 + half * 512:tt * 1024 + (half + 1) * 512]

            def _out_sc1(tt, half):
                return sc1[:, tt * 1024 + half * 512:tt * 1024 + (half + 1) * 512]

            ln3v = emit_ln(g, _src_x2, _out_sc1, f"ln3_{g}",
                           extra_waits=[(ve, ev[f"x2_{g}"])],
                           norm_waits=([(st, ev[f"store_{g - 1}"])]
                                       if g >= 1 else []))

            if not last:
                emit_kvq(g + 1, 4 + 6 * g)
            else:
                for uid in range(4 + 6 * g, 7 + 6 * g):
                    ev[f"use_mm_{uid}"] = mm.v
                pump_loads(10 + 6 * g)

            h_ev3, l_ev3 = emit_tsplit(g, sc1, f"n3t_{g}", waits=[(ve, ln3v)],
                                       banks=(4, 5, 6, 7),
                                       hi_waits=[(mm, ev[f"op_{g}_mm3"])],
                                       dsth=fmh2, dstl=fml2)

            cos_use = 7 + 6 * g
            emit_proj_tm(g, cos_use, f"re_{g}", h_ev3, l_ev3, [0, 1, 2, 3],
                         srch=fmh2, srcl=fml2)
            pump_loads(cos_use + 4)

            @block.scalar
            def _(scalar):
                ins = None
                for ti in range(4):
                    kb.wait(scalar, mm, ev[f"re_{g}_mm{ti}"])
                    if not last:
                        kb.wait(scalar, mm, ev[f"n1t_{g + 1}_tp{KC - 2}"])
                    ins = scalar.activation(ao_sb[:, ti * 512:(ti + 1) * 512],
                                            pb[ti][:], AF.Square)
                ev[f"sqre_{g}"] = ac.inc(ins)
            kb.bank_set([0, 1, 2, 3], [(ac, ev[f"sqre_{g}"])])

            emit_proj_tm(g, cos_use + 1, f"im_{g}", h_ev3, l_ev3, [0, 1, 2, 3],
                         srch=fmh2, srcl=fml2)
            pump_loads(cos_use + 5)

            @block.scalar
            def _(scalar):
                ins = None
                for ti in range(4):
                    kb.wait(scalar, mm, ev[f"im_{g}_mm{ti}"])
                    ins = scalar.activation(sc2[:, ti * 512:(ti + 1) * 512],
                                            pb[ti][:], AF.Square)
                ev[f"sq_{g}"] = ac.inc(ins)
            kb.bank_set([0, 1, 2, 3], [(ac, ev[f"sq_{g}"])])

            Pt = (g % 2) * 128
            bns0, bns1 = Pt + 72, Pt + 84
            agg = Pt + 96
            stdc, rstdc, nmrc, vepc = Pt + 100, Pt + 102, Pt + 104, Pt + 106

            def tail_a(g=g):
                @block.vector
                def _(vector):
                    kb.wait(vector, ac, ev[f"sq_{g}"])
                    ev[f"m2_{g}"] = ve.inc(
                        vector.tensor_add(sc1[:], ao_sb[:], sc2[:]))

                @block.scalar
                def _(scalar):
                    kb.wait(scalar, ve, ev[f"m2_{g}"])
                    ev[f"mag_{g}"] = ac.inc(
                        scalar.activation(ao_sb[:], sc1[:], AF.Sqrt))

            def tail_b(g=g):
                @block.vector
                def _(vector):
                    kb.wait(vector, ac, ev[f"mag_{g}"])
                    vector.bn_stats(stats[:, bns0:bns0 + 6], ao_sb[:, 0:512])
                    vector.bn_stats(stats[:, bns0 + 6:bns0 + 12],
                                    ao_sb[:, 512:1024])
                    vector.bn_stats(stats[:, bns1:bns1 + 6], ao_sb[:, 1024:1536])
                    ev[f"ln4_{g}_bns"] = ve.inc(vector.bn_stats(
                        stats[:, bns1 + 6:bns1 + 12], ao_sb[:, 1536:2048]))
                    kb.wait(vector, ve, ev[f"ln4_{g}_bns"])
                    vector.bn_aggr(stats[:, agg:agg + 2], stats[:, bns0:bns0 + 12])
                    ev[f"ln4_{g}_agg"] = ve.inc(vector.bn_aggr(
                        stats[:, agg + 2:agg + 4], stats[:, bns1:bns1 + 12]))
                    kb.wait(vector, ve, ev[f"ln4_{g}_agg"])
                    vector.tensor_scalar_add(stats[:, vepc:vepc + 1],
                                             stats[:, agg + 1:agg + 2], EPS)
                    ev[f"ln4_{g}_vep"] = ve.inc(vector.tensor_scalar_add(
                        stats[:, vepc + 1:vepc + 2], stats[:, agg + 3:agg + 4],
                        EPS))

                @block.scalar
                def _(scalar):
                    kb.wait(scalar, ve, ev[f"ln4_{g}_vep"])
                    ev[f"ln4_{g}_std"] = ac.inc(scalar.activation(
                        stats[:, stdc:stdc + 2], stats[:, vepc:vepc + 2],
                        AF.Sqrt))

            def tail_c1(g=g):
                @block.vector
                def _(vector):
                    kb.wait(vector, ac, ev[f"ln4_{g}_std"])
                    ev[f"ln4_{g}_rstd"] = ve.inc(vector.reciprocal(
                        stats[:, rstdc:rstdc + 2], stats[:, stdc:stdc + 2]))
                    kb.wait(vector, ve, ev[f"ln4_{g}_rstd"])
                    vector.scalar_tensor_tensor(
                        stats[:, nmrc:nmrc + 1], stats[:, agg:agg + 1], -1.0,
                        stats[:, rstdc:rstdc + 1], op0=ALU.mult, op1=ALU.mult)
                    ev[f"ln4_{g}_nmr"] = ve.inc(vector.scalar_tensor_tensor(
                        stats[:, nmrc + 1:nmrc + 2], stats[:, agg + 2:agg + 3],
                        -1.0, stats[:, rstdc + 1:rstdc + 2],
                        op0=ALU.mult, op1=ALU.mult))
                    kb.wait(vector, ve, ev[f"ln4_{g}_nmr"])
                    ins = None
                    for tt in range(2):
                        for half in range(2):
                            ins = vector.tensor_scalar(
                                sc2[:, tt * 1024 + half * 512:
                                    tt * 1024 + (half + 1) * 512],
                                ao_sb[:, tt * 1024 + half * 512:
                                      tt * 1024 + (half + 1) * 512],
                                stats[:, rstdc + tt:rstdc + tt + 1],
                                stats[:, nmrc + tt:nmrc + tt + 1],
                                op0=ALU.mult, op1=ALU.add)
                    ev[f"ln4_{g}_norm"] = ve.inc(ins)

            def tail_c2(g=g, row0=row0, xb_g=xb_g):
                @block.vector
                def _(vector):
                    kb.wait(vector, ve, ev[f"ln4_{g}_norm"])
                    ins = None
                    for tt in range(2):
                        ins = vector.tensor_add(
                            sc1[:, tt * 1024:(tt + 1) * 1024],
                            xb_g[:, tt * 1024:(tt + 1) * 1024],
                            sc2[:, tt * 1024:(tt + 1) * 1024])
                    ev[f"outtt_{g}"] = ve.inc(ins)

                @block.gpsimd
                def _(gpsimd):
                    kb.wait(gpsimd, ve, ev[f"outtt_{g}"])
                    for tt in range(2):
                        st.inc(gpsimd.dma_start(
                            out=y_d[row0 + tt * 128:row0 + (tt + 1) * 128, :],
                            in_=sc1[:, tt * 1024:(tt + 1) * 1024]), 16)
                    ev[f"store_{g}"] = st.v

                if g + 2 < seqs * nblk:
                    nseq, nj = (g + 2) // nblk, (g + 2) % nblk
                    nrow0 = nseq * nblk * W + nj * W

                    @block.gpsimd
                    def _(gpsimd):
                        kb.wait(gpsimd, ve, ev[f"outtt_{g}"])
                        for tt in range(2):
                            qal[g % 2].inc(gpsimd.dma_start(
                                out=xb_g[:, tt * 1024:(tt + 1) * 1024],
                                in_=x_d[nrow0 + tt * 128:
                                        nrow0 + (tt + 1) * 128, :]), 16)
                        ev[f"xload_{g + 2}"] = qal[g % 2].v

            pending_tail[0] = [(2, tail_a), (6, tail_b), (10, tail_c1),
                               (12, tail_c2)]

        if pending_tail[0] is not None:
            for _, _fn in pending_tail[0]:
                _fn()
            pending_tail[0] = None

        @block.scalar
        def _(scalar):
            scalar.wait_ge(st.sem, st.v)

    return nc


def _host_consts():
    qi = np.arange(128)[:, None]
    ki = np.arange(512)[None, :]
    spans = {0: (0, 384), 1: (128, 512), 2: (256, 384), 3: (256, 512)}
    cols = []
    for variant in range(4):
        qsub, is_j0 = variant % 2, variant // 2
        gq = qsub * 128 + qi
        band = (ki >= gq + 1) & (ki <= gq + W)
        if is_j0:
            band = band & (ki >= 256)
        a, b = spans[variant]
        cols.append(np.where(band, 0.0, NEG)[:, a:b])
    maskc = np.concatenate(cols, axis=1).astype(ml_dtypes.bfloat16)
    assert maskc.shape == (128, 1152)
    eyebf = np.eye(128).astype(ml_dtypes.bfloat16)
    eyef = np.eye(128, dtype=np.float32)
    consts = np.zeros((128, 8), dtype=np.float32)
    consts[:, 0] = EPS
    consts[:, 1] = np.pi / 2
    consts[:, 3] = 1.0
    return maskc, eyebf, eyef, consts


def make_in_map(x_shard, wq, wk, wv, wo, tha, thb, pha, phb):
    maskc, eyebf, eyef, consts = _host_consts()
    return {
        "x": np.ascontiguousarray(x_shard.reshape(-1, D)),
        "wqT": np.ascontiguousarray(wq.T),
        "wkT": np.ascontiguousarray(wk.T),
        "wvT": np.ascontiguousarray(wv.T),
        "woT": np.ascontiguousarray(wo.T),
        "tha": np.ascontiguousarray(tha),
        "thb": np.ascontiguousarray(thb),
        "pha": np.ascontiguousarray(pha),
        "phb": np.ascontiguousarray(phb),
        "maskc": maskc, "eyebf": eyebf, "eyef": eyef, "consts": consts,
    }


def kernel(L0_attn_x, L0_attn_wq, L0_attn_wk, L0_attn_wv, L0_attn_wo,
           L0_ffn_theta, L0_ffn_phi, L0_ffn_theta_noise, L0_ffn_phi_noise):
    x = np.asarray(L0_attn_x, dtype=np.float32)
    wq = np.asarray(L0_attn_wq, dtype=np.float32)
    wk = np.asarray(L0_attn_wk, dtype=np.float32)
    wv = np.asarray(L0_attn_wv, dtype=np.float32)
    wo = np.asarray(L0_attn_wo, dtype=np.float32)
    tha = np.asarray(L0_ffn_theta, dtype=np.float32)
    thb = np.asarray(L0_ffn_theta_noise, dtype=np.float32)
    pha = np.asarray(L0_ffn_phi, dtype=np.float32)
    phb = np.asarray(L0_ffn_phi_noise, dtype=np.float32)

    n_cores, seqs, nblk = 8, 2, 8
    nc = build_nc(seqs=seqs, nblk=nblk)
    in_maps = [make_in_map(x[c * seqs:(c + 1) * seqs], wq, wk, wv, wo,
                           tha, thb, pha, phb)
               for c in range(n_cores)]
    res = run_bass_kernel_spmd(nc, in_maps, core_ids=list(range(n_cores)))
    y = np.concatenate(
        [np.asarray(r["y"]).reshape(seqs, nblk * W, D) for r in res.results], axis=0)
    return y


# version 24 (final)
# speedup vs baseline: 1.0699x; 1.0434x over previous
import contextlib
import numpy as np
import ml_dtypes
import concourse.bass as bass
import concourse.mybir as mybir
from concourse.bass_utils import run_bass_kernel_spmd

F32 = mybir.dt.float32
BF16 = mybir.dt.bfloat16
AF = mybir.ActivationFunctionType
ALU = mybir.AluOpType
AX = mybir.AxisListType

D = 1024
H = 16
DH = 64
W = 256
KC = 8
NEG = -1.0e9
EPS = 1e-5

W_Q, W_K, W_V, W_O, W_COS, W_SIN = 0, 1, 2, 3, 4, 5
_USE_CYCLE = [W_O, W_K, W_V, W_Q, W_COS, W_SIN]


def use_widx(use_id):
    if use_id < 3:
        return [W_K, W_V, W_Q][use_id]
    return _USE_CYCLE[(use_id - 3) % 6]


class Ctr:

    def __init__(self, sem, name):
        self.sem = sem
        self.name = name
        self.v = 0

    def inc(self, instr, n=1):
        instr.then_inc(self.sem, n)
        self.v += n
        return self.v


class KB:
    def __init__(self):
        self.ev = {}
        self.guards = {}
        self.waited = {}

    def wait(self, eng, ctr, val):
        if val <= 0:
            return
        key = (id(eng), ctr.name)
        if self.waited.get(key, 0) >= val:
            return
        eng.wait_ge(ctr.sem, val)
        self.waited[key] = val

    def bank_wait(self, eng, banks):
        for b in banks:
            for (c, v) in self.guards.get(b, []):
                self.wait(eng, c, v)

    def bank_set(self, banks, entries):
        for b in banks:
            self.guards[b] = list(entries)


def build_nc(seqs=2, nblk=8):
    S = nblk * W
    T = seqs * S
    nc = bass.Bass("TRN2", target_bir_lowering=False)

    x_d = nc.dram_tensor("x", [T, D], F32, kind="ExternalInput").ap()
    y_d = nc.dram_tensor("y", [T, D], F32, kind="ExternalOutput").ap()
    wsrc = [nc.dram_tensor(n, [D, D], F32, kind="ExternalInput").ap()
            for n in ["wqT", "wkT", "wvT", "woT"]]
    tha = nc.dram_tensor("tha", [D, D], F32, kind="ExternalInput").ap()
    thb = nc.dram_tensor("thb", [D, D], F32, kind="ExternalInput").ap()
    pha = nc.dram_tensor("pha", [D, D], F32, kind="ExternalInput").ap()
    phb = nc.dram_tensor("phb", [D, D], F32, kind="ExternalInput").ap()
    maskc_d = nc.dram_tensor("maskc", [128, 1152], BF16, kind="ExternalInput").ap()
    eyebf_d = nc.dram_tensor("eyebf", [128, 128], BF16, kind="ExternalInput").ap()
    eyef_d = nc.dram_tensor("eyef", [128, 128], F32, kind="ExternalInput").ap()
    consts_d = nc.dram_tensor("consts", [128, 8], F32, kind="ExternalInput").ap()
    wsc = nc.dram_tensor("wsc", [6, 2, 128, KC * D], BF16).ap()

    kb = KB()
    ev = kb.ev

    with contextlib.ExitStack() as ctx:
        E = ctx.enter_context
        sb = nc.sbuf_tensor
        x_sb = E(sb("x_sb", [128, 2048], F32))
        x_sb2 = E(sb("x_sb2", [128, 2048], F32))
        fmh2 = E(sb("fmh2", [128, 2048], BF16))
        fml2 = E(sb("fml2", [128, 2048], BF16))
        sc1 = E(sb("sc1", [128, 2048], F32))
        sc2 = E(sb("sc2", [128, 2048], F32))
        fmh = E(sb("fmh", [128, 2048], BF16))
        fml = E(sb("fml", [128, 2048], BF16))
        q_sb = E(sb("q_sb", [128, 2048], F32))
        k_sb = E(sb("k_sb", [128, 4096], F32))
        v_sb = E(sb("v_sb", [128, 4096], F32))
        attn_sb = E(sb("attn_sb", [128, 1536], F32))
        attnT_sb = E(sb("attnT_sb", [128, 1152], F32))
        ao_sb = E(sb("ao_sb", [128, 2048], F32))
        stats = E(sb("stats", [128, 256], F32))
        maskc = E(sb("maskc_s", [128, 1152], BF16))
        MOFF = {0: 0, 1: 384, 2: 768, 3: 896}
        eyebf = E(sb("eyebf_s", [128, 128], BF16))
        eyef = E(sb("eyef_s", [128, 128], F32))
        cst = E(sb("cst_s", [128, 8], F32))
        pb = [E(nc.psum_tensor(f"pb{i}", [128, 512], F32)) for i in range(8)]

        qsl = [Ctr(E(nc.semaphore("qs0")), "qs0"),
               Ctr(E(nc.semaphore("qs1")), "qs1")]
        qc = Ctr(E(nc.semaphore("qc")), "qc")
        qw = [Ctr(E(nc.semaphore("qw0")), "qw0"),
              Ctr(E(nc.semaphore("qw1")), "qw1")]
        qv = [Ctr(E(nc.semaphore("qv0")), "qv0"),
              Ctr(E(nc.semaphore("qv1")), "qv1")]
        qal = [Ctr(E(nc.semaphore("qa0")), "qa0"),
               Ctr(E(nc.semaphore("qa1")), "qa1")]
        st = Ctr(E(nc.semaphore("st")), "st")
        mm = Ctr(E(nc.semaphore("mm")), "mm")
        ve = Ctr(E(nc.semaphore("ve")), "ve")
        ac = Ctr(E(nc.semaphore("ac")), "ac")
        block = E(nc.Block())

        @block.sync
        def _(sync):
            qc.inc(sync.dma_start(out=maskc[:], in_=maskc_d), 16)
            qc.inc(sync.dma_start(out=eyebf[:], in_=eyebf_d), 16)
            qc.inc(sync.dma_start(out=eyef[:], in_=eyef_d), 16)
            qc.inc(sync.dma_start(out=cst[:], in_=consts_d), 16)
        ev["cst"] = qc.v

        qp = [[Ctr(E(nc.semaphore(f"qp{i}_{p}")), f"qp{i}_{p}") for p in range(2)]
              for i in range(6)]
        slot_h, slot_l = [], []

        def alloc_slots():
            for i in range(3):
                slot_h.append(E(sb(f"slot{i}h", [128, KC * D], BF16)))
                slot_l.append(E(sb(f"slot{i}l", [128, KC * D], BF16)))

        def emit_prep():
            CW = 2 * D
            with contextlib.ExitStack() as pctx:
                PE_ = pctx.enter_context
                pa = [PE_(sb(f"pa{i}", [128, CW], F32)) for i in range(2)]
                pbf = [PE_(sb(f"pbf{i}", [128, CW], F32)) for i in range(2)]
                pf = [PE_(sb(f"pf{i}", [128, CW], F32)) for i in range(2)]
                ph = [PE_(sb(f"ph{i}", [128, CW], BF16)) for i in range(2)]
                pl = [PE_(sb(f"pl{i}", [128, CW], BF16)) for i in range(2)]

                for wpos, widx in enumerate([W_K, W_V, W_Q, W_O, W_COS, W_SIN]):
                    for ci, c in enumerate(range(0, KC, 2)):
                        gi = wpos * 4 + ci
                        p = ci % 2
                        trig = widx >= 4
                        if not trig:
                            srcs = [(pa[p], wsrc[widx])]
                        else:
                            srcs = [(pa[p], tha if widx == 4 else pha),
                                    (pbf[p], thb if widx == 4 else phb)]

                        @block.sync
                        def _(sync, p=p, c=c, gi=gi, widx=widx, srcs=srcs, trig=trig):
                            kb.wait(sync, ve, ev.get(f"plog_{gi - 2}", 0))
                            kb.wait(sync, ac, ev.get(f"phig_{gi - 2}", 0))
                            if trig:
                                kb.wait(sync, ac, ev.get(f"pactg_{gi - 2}", 0))
                            for dst, srcd in srcs:
                                src2 = srcd[c * 128:(c + 2) * 128, :].rearrange(
                                    "(a p) d -> p a d", p=128)
                                dst2 = dst[:, :].rearrange("p (a d) -> p a d", d=D)
                                qsl[p].inc(sync.dma_start(out=dst2, in_=src2), 16)
                            ev[f"pld_{widx}_{c}"] = qsl[p].v

                        if trig:
                            @block.vector
                            def _(vector, p=p, c=c, gi=gi, widx=widx):
                                kb.wait(vector, qsl[p], ev[f"pld_{widx}_{c}"])
                                kb.wait(vector, ac, ev.get(f"pactg_{gi - 2}", 0))
                                if widx == 4:
                                    r = float(np.float64(np.pi / 2)
                                              - np.float64(np.float32(np.pi / 2)))
                                    ev[f"padd_{widx}_{c}"] = ve.inc(
                                        vector.scalar_tensor_tensor(
                                            pf[p][:], pa[p][:], r, pbf[p][:],
                                            op0=ALU.add, op1=ALU.add))
                                else:
                                    ev[f"padd_{widx}_{c}"] = ve.inc(vector.tensor_add(
                                        pf[p][:], pa[p][:], pbf[p][:]))

                            @block.scalar
                            def _(scalar, p=p, c=c, gi=gi, widx=widx):
                                kb.wait(scalar, ve, ev[f"padd_{widx}_{c}"])
                                kb.wait(scalar, qc, ev["cst"])
                                if widx == 4:
                                    ins = scalar.activation(pa[p][:], pf[p][:], AF.Sin,
                                                            bias=cst[:, 1:2], scale=1.0)
                                else:
                                    ins = scalar.activation(pa[p][:], pf[p][:], AF.Sin)
                                ev[f"pact_{widx}_{c}"] = ac.inc(ins)
                                ev[f"pactg_{gi}"] = ev[f"pact_{widx}_{c}"]

                        @block.scalar
                        def _(scalar, p=p, c=c, gi=gi, widx=widx, trig=trig):
                            if trig:
                                kb.wait(scalar, ac, ev[f"pact_{widx}_{c}"])
                            else:
                                kb.wait(scalar, qsl[p], ev[f"pld_{widx}_{c}"])
                            pg = ev.get(f"pstg_{gi - 2}")
                            if pg is not None:
                                kb.wait(scalar, pg[0], pg[1])
                            ev[f"phi_{widx}_{c}"] = ac.inc(
                                scalar.activation(ph[p][:], pa[p][:], AF.Copy))
                            ev[f"phig_{gi}"] = ev[f"phi_{widx}_{c}"]

                        @block.vector
                        def _(vector, p=p, c=c, gi=gi, widx=widx):
                            kb.wait(vector, ac, ev[f"phi_{widx}_{c}"])
                            ev[f"plo_{widx}_{c}"] = ve.inc(
                                vector.scalar_tensor_tensor(
                                    pl[p][:], pa[p][:], 1.0, ph[p][:],
                                    op0=ALU.mult, op1=ALU.subtract))
                            ev[f"plog_{gi}"] = ev[f"plo_{widx}_{c}"]

                        @block.gpsimd
                        def _(gpsimd, p=p, c=c, gi=gi, widx=widx):
                            kb.wait(gpsimd, ac, ev[f"phi_{widx}_{c}"])
                            kb.wait(gpsimd, ve, ev[f"plo_{widx}_{c}"])
                            qp[widx][p].inc(gpsimd.dma_start(
                                out=wsc[widx, 0, :, c * D:(c + 2) * D], in_=ph[p][:]), 16)
                            qp[widx][p].inc(gpsimd.dma_start(
                                out=wsc[widx, 1, :, c * D:(c + 2) * D], in_=pl[p][:]), 16)
                            ev[f"pstg_{gi}"] = (qp[widx][p], qp[widx][p].v)
                    ev[f"wsc_ready_{widx}"] = (qp[widx][0].v, qp[widx][1].v)

        total_uses = 3 + 6 * seqs * nblk
        loads_emitted = [0]

        def load_slot(use_id):
            si = use_id % 3
            qq = qw[use_id % 2]
            qq2 = qv[use_id % 2]
            widx = use_widx(use_id)

            @block.sync
            def _(sync):
                for wi in range(6):
                    for pp in range(2):
                        kb.wait(sync, qp[wi][pp], ev[f"wsc_ready_{wi}"][pp])
                if use_id - 2 >= 0:
                    kb.wait(sync, qq, ev[f"use_ld_{use_id - 2}"])
                if use_id - 3 >= 0:
                    kb.wait(sync, mm, ev[f"use_mm_{use_id - 3}"])
                qq.inc(sync.dma_start(out=slot_h[si][:], in_=wsc[widx, 0]), 16)
                ev[f"use_ld_{use_id}"] = qq.v

            @block.gpsimd
            def _(gpsimd):
                for wi in range(6):
                    for pp in range(2):
                        kb.wait(gpsimd, qp[wi][pp], ev[f"wsc_ready_{wi}"][pp])
                if use_id - 2 >= 0:
                    kb.wait(gpsimd, qv[use_id % 2], ev[f"use_lo_{use_id - 2}"])
                if use_id - 3 >= 0:
                    kb.wait(gpsimd, mm, ev[f"use_mm_{use_id - 3}"])
                qq2.inc(gpsimd.dma_start(out=slot_l[si][:], in_=wsc[widx, 1]), 16)
                ev[f"use_lo_{use_id}"] = qq2.v

        def pump_loads(upto):
            while loads_emitted[0] < min(upto, total_uses):
                load_slot(loads_emitted[0])
                loads_emitted[0] += 1

        def emit_ln(g, src_ap_fn, out_ap_fn, tag, extra_waits=(),
                    norm_waits=()):
            P = (g % 2) * 128
            bns0, bns1 = P + 72, P + 84
            agg = P + 96
            stdc, rstdc, nmrc, vepc = P + 100, P + 102, P + 104, P + 106

            @block.vector
            def _(vector):
                for c_, v_ in extra_waits:
                    kb.wait(vector, c_, v_)
                vector.bn_stats(stats[:, bns0:bns0 + 6], src_ap_fn(0, 0))
                vector.bn_stats(stats[:, bns0 + 6:bns0 + 12], src_ap_fn(0, 1))
                vector.bn_stats(stats[:, bns1:bns1 + 6], src_ap_fn(1, 0))
                ev[f"{tag}_bns"] = ve.inc(
                    vector.bn_stats(stats[:, bns1 + 6:bns1 + 12], src_ap_fn(1, 1)))
                kb.wait(vector, ve, ev[f"{tag}_bns"])
                vector.bn_aggr(stats[:, agg:agg + 2], stats[:, bns0:bns0 + 12])
                ev[f"{tag}_agg"] = ve.inc(
                    vector.bn_aggr(stats[:, agg + 2:agg + 4], stats[:, bns1:bns1 + 12]))
                kb.wait(vector, ve, ev[f"{tag}_agg"])
                vector.tensor_scalar_add(stats[:, vepc:vepc + 1],
                                         stats[:, agg + 1:agg + 2], EPS)
                ev[f"{tag}_vep"] = ve.inc(vector.tensor_scalar_add(
                    stats[:, vepc + 1:vepc + 2], stats[:, agg + 3:agg + 4], EPS))

            @block.scalar
            def _(scalar):
                kb.wait(scalar, ve, ev[f"{tag}_vep"])
                ev[f"{tag}_std"] = ac.inc(scalar.activation(
                    stats[:, stdc:stdc + 2], stats[:, vepc:vepc + 2], AF.Sqrt))

            @block.vector
            def _(vector):
                kb.wait(vector, ac, ev[f"{tag}_std"])
                ev[f"{tag}_rstd"] = ve.inc(vector.reciprocal(
                    stats[:, rstdc:rstdc + 2], stats[:, stdc:stdc + 2]))
                kb.wait(vector, ve, ev[f"{tag}_rstd"])
                vector.scalar_tensor_tensor(
                    stats[:, nmrc:nmrc + 1], stats[:, agg:agg + 1], -1.0,
                    stats[:, rstdc:rstdc + 1], op0=ALU.mult, op1=ALU.mult)
                ev[f"{tag}_nmr"] = ve.inc(vector.scalar_tensor_tensor(
                    stats[:, nmrc + 1:nmrc + 2], stats[:, agg + 2:agg + 3], -1.0,
                    stats[:, rstdc + 1:rstdc + 2], op0=ALU.mult, op1=ALU.mult))
                kb.wait(vector, ve, ev[f"{tag}_nmr"])
                for c_, v_ in norm_waits:
                    kb.wait(vector, c_, v_)
                ins = None
                for tt in range(2):
                    for half in range(2):
                        ins = vector.tensor_scalar(
                            out_ap_fn(tt, half), src_ap_fn(tt, half),
                            stats[:, rstdc + tt:rstdc + tt + 1],
                            stats[:, nmrc + tt:nmrc + tt + 1],
                            op0=ALU.mult, op1=ALU.add)
                ev[f"{tag}_norm"] = ve.inc(ins)
            return ev[f"{tag}_norm"]

        def emit_tsplit_tt(g, ht, src_sb, tag, waits=(), banks=(0, 1),
                           dsth=None, dstl=None):
            for bi, b in enumerate(banks):
                @block.tensor
                def _(tensor, bi=bi, b=b):
                    kb.wait(tensor, qc, ev["cst"])
                    for c_, v_ in waits:
                        kb.wait(tensor, c_, v_)
                    kb.bank_wait(tensor, [b])
                    ins = None
                    for i in range(4):
                        kc = bi * 4 + i
                        ins = tensor.transpose(
                            pb[b][:, i * 128:(i + 1) * 128],
                            src_sb[:, ht * 1024 + kc * 128:ht * 1024 + (kc + 1) * 128],
                            eyef[:])
                    ev[f"{tag}_tp{bi}"] = mm.inc(ins)

            dsth = fmh if dsth is None else dsth
            dstl = fml if dstl is None else dstl

            def _fm_dst(t, bi):
                v3 = t[:, :].rearrange("p (kc d) -> p kc d", d=256)
                return v3[:, bi * 4:(bi + 1) * 4, ht * 128:(ht + 1) * 128]

            @block.scalar
            def _(scalar):
                for bi, b in enumerate(banks):
                    kb.wait(scalar, mm, ev[f"{tag}_tp{bi}"])
                    ev[f"{tag}_hi{bi}"] = ac.inc(scalar.activation(
                        _fm_dst(dsth, bi),
                        pb[b][:, 0:512].rearrange("p (kc d) -> p kc d", d=128),
                        AF.Copy))

            @block.vector
            def _(vector):
                for bi, b in enumerate(banks):
                    kb.wait(vector, ac, ev[f"{tag}_hi{bi}"])
                    ev[f"{tag}_lo{bi}"] = ve.inc(vector.scalar_tensor_tensor(
                        _fm_dst(dstl, bi),
                        pb[b][:, 0:512].rearrange("p (kc d) -> p kc d", d=128), 1.0,
                        _fm_dst(dsth, bi), op0=ALU.mult, op1=ALU.subtract))
            kb.bank_set(list(banks), [(ve, ve.v)])
            ev[f"{tag}_hi"] = ev[f"{tag}_hi1"]
            ev[f"{tag}_lo"] = ev[f"{tag}_lo1"]
            return f"{tag}_hi", f"{tag}_lo"

        def emit_tsplit(g, src_sb, tag, waits=(), banks=(0, 1, 2, 3),
                        hi_waits=(), dsth=None, dstl=None):
            for kcp in range(0, KC, 2):
                b = banks[kcp // 2]
                tiles = [(kcp, 0), (kcp, 1), (kcp + 1, 0), (kcp + 1, 1)]

                @block.tensor
                def _(tensor, kcp=kcp, b=b, tiles=tiles):
                    kb.wait(tensor, qc, ev["cst"])
                    for c_, v_ in waits:
                        kb.wait(tensor, c_, v_)
                    kb.bank_wait(tensor, [b])
                    ins = None
                    for i, (kc, tt) in enumerate(tiles):
                        ins = tensor.transpose(
                            pb[b][:, i * 128:(i + 1) * 128],
                            src_sb[:, tt * 1024 + kc * 128:tt * 1024 + (kc + 1) * 128],
                            eyef[:])
                    ev[f"{tag}_tp{kcp}"] = mm.inc(ins)

            @block.scalar
            def _(scalar):
                for c_, v_ in hi_waits:
                    kb.wait(scalar, c_, v_)
                dh_ = fmh if dsth is None else dsth
                for kcp in range(0, KC, 2):
                    kb.wait(scalar, mm, ev[f"{tag}_tp{kcp}"])
                    ev[f"{tag}_hi{kcp}"] = ac.inc(scalar.activation(
                        dh_[:, kcp * 256:(kcp + 2) * 256],
                        pb[banks[kcp // 2]][:], AF.Copy))

            @block.vector
            def _(vector):
                dh_ = fmh if dsth is None else dsth
                dl_ = fml if dstl is None else dstl
                for kcp in range(0, KC, 2):
                    kb.wait(vector, ac, ev[f"{tag}_hi{kcp}"])
                    ev[f"{tag}_lo{kcp}"] = ve.inc(vector.scalar_tensor_tensor(
                        dl_[:, kcp * 256:(kcp + 2) * 256], pb[banks[kcp // 2]][:], 1.0,
                        dh_[:, kcp * 256:(kcp + 2) * 256],
                        op0=ALU.mult, op1=ALU.subtract))
            kb.bank_set(list(banks), [(ve, ve.v)])
            ev[f"{tag}_hi"] = ev[f"{tag}_hi{KC - 2}"]
            ev[f"{tag}_lo"] = ev[f"{tag}_lo{KC - 2}"]
            return f"{tag}_hi", f"{tag}_lo"

        def emit_proj_fm(g, use_id, dst_sb, dst_col0, tag, hi_ev, lo_ev, evac_eng,
                         evac_waits=(), srch=None, srcl=None, nbanks=4,
                         kc_tag=None):
            si = use_id % 3
            for m in range(KC):
                b = m % nbanks

                @block.tensor
                def _(tensor, m=m, b=b):
                    kb.wait(tensor, qw[use_id % 2], ev[f"use_ld_{use_id}"])
                    kb.wait(tensor, qv[use_id % 2], ev[f"use_lo_{use_id}"])
                    if kc_tag is None:
                        kb.wait(tensor, ac, ev[hi_ev])
                        kb.wait(tensor, ve, ev[lo_ev])
                    kb.bank_wait(tensor, [b])
                    sh_ = fmh if srch is None else srch
                    sl_ = fml if srcl is None else srcl
                    n_mm = 0
                    ins = None
                    for kc in range(KC):
                        if kc_tag is not None:
                            kb.wait(tensor, ac, ev[f"{kc_tag}_hi{kc & ~1}"])
                            kb.wait(tensor, ve, ev[f"{kc_tag}_lo{kc & ~1}"])
                        wh = slot_h[si][:, kc * D + m * 128:kc * D + (m + 1) * 128]
                        wl = slot_l[si][:, kc * D + m * 128:kc * D + (m + 1) * 128]
                        ah = sh_[:, kc * 256:(kc + 1) * 256]
                        al = sl_[:, kc * 256:(kc + 1) * 256]
                        for lhsT, rhs in [(wh, ah), (wh, al), (wl, ah)]:
                            ins = tensor.matmul(pb[b][:, 0:256], lhsT, rhs,
                                                start=(n_mm == 0), stop=(n_mm == 23))
                            n_mm += 1
                    ev[f"{tag}_mm{m}"] = mm.inc(ins)

                eng_sec = block.vector if evac_eng == "v" else block.scalar
                ctr = ve if evac_eng == "v" else ac

                @eng_sec
                def _(eng, m=m, b=b):
                    for c_, v_ in evac_waits:
                        kb.wait(eng, c_, v_)
                    kb.wait(eng, mm, ev[f"{tag}_mm{m}"])
                    dst = dst_sb[:, dst_col0 + m * 256:dst_col0 + (m + 1) * 256]
                    if evac_eng == "v":
                        ins = eng.tensor_copy(dst, pb[b][:, 0:256])
                    else:
                        ins = eng.activation(dst, pb[b][:, 0:256], AF.Copy)
                    ev[f"{tag}_ev{m}"] = ctr.inc(ins)
                kb.bank_set([b], [(ctr, ctr.v)])
            ev[f"use_mm_{use_id}"] = ev[f"{tag}_mm{KC - 1}"]
            ev[f"{tag}_done"] = ev[f"{tag}_ev{KC - 1}"]

        def emit_proj_tm(g, use_id, tag, hi_ev, lo_ev, banks, extra_waits=(),
                         srch=None, srcl=None, tile_waits=None, kc_tag=None):
            si = use_id % 3
            for ti, b in enumerate(banks):
                tt, nh = ti // 2, ti % 2

                @block.tensor
                def _(tensor, tt=tt, nh=nh, b=b, ti=ti):
                    kb.wait(tensor, qw[use_id % 2], ev[f"use_ld_{use_id}"])
                    kb.wait(tensor, qv[use_id % 2], ev[f"use_lo_{use_id}"])
                    if tile_waits is not None:
                        for c_, v_ in tile_waits[ti]:
                            kb.wait(tensor, c_, v_)
                    elif kc_tag is None:
                        for c_, v_ in extra_waits:
                            kb.wait(tensor, c_, v_)
                        kb.wait(tensor, ac, ev[hi_ev])
                        kb.wait(tensor, ve, ev[lo_ev])
                    kb.bank_wait(tensor, [b])
                    sh_ = fmh if srch is None else srch
                    sl_ = fml if srcl is None else srcl
                    n_mm = 0
                    ins = None
                    for kc in range(KC):
                        if kc_tag is not None:
                            kb.wait(tensor, ac, ev[f"{kc_tag}_hi{kc & ~1}"])
                            kb.wait(tensor, ve, ev[f"{kc_tag}_lo{kc & ~1}"])
                        ah = sh_[:, kc * 256 + tt * 128:kc * 256 + (tt + 1) * 128]
                        al = sl_[:, kc * 256 + tt * 128:kc * 256 + (tt + 1) * 128]
                        wh = slot_h[si][:, kc * D + nh * 512:kc * D + (nh + 1) * 512]
                        wl = slot_l[si][:, kc * D + nh * 512:kc * D + (nh + 1) * 512]
                        for lhsT, rhs in [(ah, wh), (ah, wl), (al, wh)]:
                            ins = tensor.matmul(pb[b][:], lhsT, rhs,
                                                start=(n_mm == 0), stop=(n_mm == 23))
                            n_mm += 1
                    ev[f"{tag}_mm{ti}"] = mm.inc(ins)
            ev[f"use_mm_{use_id}"] = ev[f"{tag}_mm3"]

        nuse = 0
        pending_fm = {}
        pending_tail = [None]

        def _src_x(tt, half):
            return x_sb[:, tt * 1024 + half * 512:tt * 1024 + (half + 1) * 512]

        def _out_sc1(tt, half):
            return sc1[:, tt * 1024 + half * 512:tt * 1024 + (half + 1) * 512]

        def emit_ln1(gg):
            nw = [(mm, ev[f"n3t_{gg - 1}_tp{KC - 2}"])] if gg >= 1 else []
            ev[f"ln1v_{gg}"] = emit_ln(gg, _src_x, _out_sc1, f"ln1_{gg}",
                                       extra_waits=[(qa, ev[f"xload_{gg}"])],
                                       norm_waits=nw)

        def emit_n1t(gg):
            hw_ = [(mm, ev[f"im_{gg - 1}_mm3"])] if gg >= 1 else []
            h_ev, l_ev = emit_tsplit(gg, sc1, f"n1t_{gg}",
                                     waits=[(ve, ev[f"ln1v_{gg}"])],
                                     banks=(4, 5, 6, 7), hi_waits=hw_)
            pending_fm[gg] = (h_ev, l_ev)

        def emit_ln1(gg, g_waits=()):
            def _src(tt, half):
                xbuf = x_sb if gg % 2 == 0 else x_sb2
                return xbuf[:, tt * 1024 + half * 512:tt * 1024 + (half + 1) * 512]

            def _out(tt, half):
                return ao_sb[:, tt * 1024 + half * 512:tt * 1024 + (half + 1) * 512]

            ev[f"ln1v_{gg}"] = emit_ln(gg, _src, _out, f"ln1_{gg}",
                                       extra_waits=[(qal[gg % 2], ev[f"xload_{gg}"])],
                                       norm_waits=g_waits)

        def emit_n1t(gg):
            h_ev, l_ev = emit_tsplit(gg, ao_sb, f"n1t_{gg}",
                                     waits=[(ve, ev[f"ln1v_{gg}"])],
                                     banks=(4, 5, 6, 7))
            pending_fm[gg] = (h_ev, l_ev)

        def emit_kvq(gg, base_use):
            bufg = (gg % nblk) % 2
            h_ev, l_ev = pending_fm.pop(gg)
            emit_proj_fm(gg, base_use, k_sb, bufg * 2048, f"kp_{gg}",
                         h_ev, l_ev, "s", nbanks=8, kc_tag=f"n1t_{gg}")
            pump_loads(base_use + 4)
            emit_proj_tm(gg, base_use + 1, f"vp_{gg}", h_ev, l_ev, [0, 1, 2, 3],
                         kc_tag=f"n1t_{gg}")
            for ti in range(4):
                tt, nh = ti // 2, ti % 2

                @block.vector
                def _(vector, ti=ti, tt=tt, nh=nh):
                    kb.wait(vector, mm, ev[f"vp_{gg}_mm{ti}"])
                    ev[f"vp_{gg}_ev{ti}"] = ve.inc(vector.tensor_copy(
                        v_sb[:, bufg * 2048 + tt * 1024 + nh * 512:
                             bufg * 2048 + tt * 1024 + (nh + 1) * 512],
                        pb[ti][:]))
                kb.bank_set([ti], [(ve, ve.v)])
            pump_loads(base_use + 5)
            emit_proj_fm(gg, base_use + 2, q_sb, 0, f"qp_{gg}", h_ev, l_ev, "v",
                         nbanks=8, kc_tag=f"n1t_{gg}")
            pump_loads(base_use + 6)

        @block.gpsimd
        def _(gpsimd):
            qal[0].inc(gpsimd.dma_start(out=x_sb[:, 0:1024], in_=x_d[0:128, :]), 16)
            qal[0].inc(gpsimd.dma_start(out=x_sb[:, 1024:2048],
                                        in_=x_d[128:256, :]), 16)
            ev["xload_0"] = qal[0].v
            if seqs * nblk > 1:
                qal[1].inc(gpsimd.dma_start(out=x_sb2[:, 0:1024],
                                            in_=x_d[W + 0:W + 128, :]), 16)
                qal[1].inc(gpsimd.dma_start(out=x_sb2[:, 1024:2048],
                                            in_=x_d[W + 128:W + 256, :]), 16)
                ev["xload_1"] = qal[1].v

        emit_ln1(0)
        emit_n1t(0)
        emit_prep()
        alloc_slots()
        pump_loads(3)
        emit_kvq(0, 0)

        for g in range(seqs * nblk):
            seq, j = g // nblk, g % nblk
            row0 = seq * nblk * W + j * W
            P = (g % 2) * 128
            buf = j % 2
            pbuf = 1 - buf
            last = (g == seqs * nblk - 1)
            xb_g = x_sb if g % 2 == 0 else x_sb2

            def cs_of(qsub):
                if j == 0:
                    return [2] if qsub == 0 else [2, 3]
                return [0, 1, 2] if qsub == 0 else [1, 2, 3]

            mvar = 2 if j == 0 else 0

            SL = 384

            def score_mms(u, qsub, h):
                sb_bank = 4 + (u % 2)
                cs = cs_of(qsub)
                c0, c1 = cs[0], cs[-1]
                ncols = (c1 - c0 + 1) * 128
                m = qsub + mvar
                pr = 64 * (h % 2)
                lq = q_sb[pr:pr + 64,
                          (h // 2) * 256 + qsub * 128:
                          (h // 2) * 256 + (qsub + 1) * 128]
                mms = []
                pcs = [c for c in cs if c < 2]
                ccs = [c for c in cs if c >= 2]
                if pcs:
                    rk = k_sb[pr:pr + 64,
                              pbuf * 2048 + (h // 2) * 256 + pcs[0] * 128:
                              pbuf * 2048 + (h // 2) * 256 + (pcs[-1] + 1) * 128]
                    mms.append(((pcs[0] * 128, (pcs[-1] + 1) * 128), lq, rk))
                if ccs:
                    rk = k_sb[pr:pr + 64,
                              buf * 2048 + (h // 2) * 256 + (ccs[0] - 2) * 128:
                              buf * 2048 + (h // 2) * 256 + (ccs[-1] - 1) * 128]
                    mms.append(((ccs[0] * 128, (ccs[-1] + 1) * 128), lq, rk))
                return sb_bank, (c0 * 128, (c1 + 1) * 128, MOFF[m], ncols), mms

            def pair_scores_t(u):
                specs = [score_mms(uu, *units[uu]) for uu in (u, u + 1)]

                @block.tensor
                def _(tensor, u=u, specs=specs):
                    kb.wait(tensor, ve, ev[f"qp_{g}_done"])
                    kb.wait(tensor, ac, ev[f"kp_{g}_done"])
                    kb.bank_wait(tensor, [specs[0][0], specs[1][0]])
                    for bank, (a, b, mo, ncols), _mms in specs:
                        tensor.matmul(pb[bank][:, a:b], eyebf[:],
                                      maskc[:, mo:mo + ncols],
                                      start=True, stop=False)
                    nmax = max(len(sp[2]) for sp in specs)
                    last = [len(specs[0][2]) - 1, len(specs[1][2]) - 1]
                    ins = None
                    for i in range(nmax):
                        for si, (bank, _mk, mms) in enumerate(specs):
                            if i < len(mms):
                                (a, b), lq, rk = mms[i]
                                ins = tensor.matmul(pb[bank][:, a:b], lq, rk,
                                                    start=False,
                                                    stop=(i == last[si]))
                    ev[f"sc_{g}_{u}"] = mm.inc(ins)
                    ev[f"sc_{g}_{u + 1}"] = ev[f"sc_{g}_{u}"]

            def unit_scores_va(u, qsub, h):
                sb_bank = 4 + (u % 2)
                up = u % 4
                cs = cs_of(qsub)
                c0, c1 = cs[0], cs[-1]

            def pair_va(u):
                specs = []
                for uu in (u, u + 1):
                    qsub, h = units[uu]
                    cs = cs_of(qsub)
                    specs.append((uu, uu % 2, uu % 4, qsub, h, cs[0], cs[-1]))

                @block.vector
                def _(vector, u=u, specs=specs):
                    kb.wait(vector, mm, ev[f"sc_{g}_{u}"])
                    for uu, ub, up, qsub, h, c0, c1 in specs:
                        if uu - 4 >= 0:
                            kb.wait(vector, mm, ev[f"tp_{g}_{uu - 4}"])
                        elif g >= 1:
                            kb.wait(vector, mm, ev[f"tp_{g - 1}_{28 + uu}"])
                        kept = pb[4 + ub][:, c0 * 128:(c1 + 1) * 128]
                        ev[f"mx_{g}_{uu}"] = ve.inc(vector.reduce_max(
                            stats[:, P + up:P + up + 1], kept, axis=AX.X))
                        kb.wait(vector, ve, ev[f"mx_{g}_{uu}"])
                        ev[f"ss_{g}_{uu}"] = ve.inc(vector.tensor_scalar(
                            attn_sb[:, up * SL:up * SL + (c1 - c0 + 1) * 128],
                            kept, stats[:, P + up:P + up + 1], 0.125,
                            op0=ALU.subtract, op1=ALU.mult))
                kb.bank_set([4], [(ve, ev[f"ss_{g}_{u}"])])
                kb.bank_set([5], [(ve, ev[f"ss_{g}_{u + 1}"])])

                @block.scalar
                def _(scalar, u=u, specs=specs):
                    for uu, ub, up, qsub, h, c0, c1 in specs:
                        kb.wait(scalar, ve, ev[f"ss_{g}_{uu}"])
                        ev[f"ex_{g}_{uu}"] = ac.inc(scalar.activation(
                            attn_sb[:, up * SL:up * SL + (c1 - c0 + 1) * 128],
                            attn_sb[:, up * SL:up * SL + (c1 - c0 + 1) * 128],
                            AF.Exp,
                            accum_out=stats[:, P + 8 + qsub * 16 + h:
                                            P + 9 + qsub * 16 + h]))

            def unit_tp(u, qsub, h):
                upA = u % 4
                upT = u % 3
                tp_bank = 6 + (u % 2)
                cs = cs_of(qsub)
                c0 = cs[0]

                @block.tensor
                def _(tensor, u=u, upA=upA, tp_bank=tp_bank, cs=cs, c0=c0):
                    kb.wait(tensor, ac, ev[f"ex_{g}_{u}"])
                    kb.bank_wait(tensor, [tp_bank])
                    ins = None
                    for c in cs:
                        ins = tensor.transpose(
                            pb[tp_bank][:, (c - c0) * 128:(c - c0 + 1) * 128],
                            attn_sb[:, upA * SL + (c - c0) * 128:
                                    upA * SL + (c - c0 + 1) * 128],
                            eyef[:])
                    ev[f"tp_{g}_{u}"] = mm.inc(ins)

                @block.vector
                def _(vector, u=u, upT=upT, tp_bank=tp_bank, cs=cs):
                    kb.wait(vector, mm, ev[f"tp_{g}_{u}"])
                    nc_ = len(cs) * 128
                    ev[f"te_{g}_{u}"] = ve.inc(vector.tensor_copy(
                        attnT_sb[:, upT * SL:upT * SL + nc_],
                        pb[tp_bank][:, 0:nc_]))
                kb.bank_set([tp_bank], [(ve, ve.v)])

            def unit_av(u, qsub, h):
                up = u % 3
                avb = [0, 1] if qsub == 0 else [2, 3]
                b_av = avb[h // 8]
                cs = cs_of(qsub)
                c0 = cs[0]

                @block.tensor
                def _(tensor, u=u, up=up, qsub=qsub, h=h, b_av=b_av, cs=cs, c0=c0):
                    kb.wait(tensor, ve, ev[f"te_{g}_{u}"])
                    kb.wait(tensor, ve, ev[f"vp_{g}_ev3"])
                    if h % 8 == 0:
                        kb.bank_wait(tensor, [b_av])
                    first_in_bank = (h % 8 == 0)
                    last_in_bank = (h % 8 == 7)
                    ins = None
                    for ci, c in enumerate(cs):
                        lhsT = attnT_sb[:, up * SL + (c - c0) * 128:
                                        up * SL + (c - c0 + 1) * 128]
                        vb, vtt = (pbuf, c) if c < 2 else (buf, c - 2)
                        rhs = v_sb[:, vb * 2048 + vtt * 1024 + h * DH:
                                   vb * 2048 + vtt * 1024 + (h + 1) * DH]
                        ins = tensor.matmul(
                            pb[b_av][:, (h % 8) * DH:(h % 8 + 1) * DH], lhsT, rhs,
                            start=(first_in_bank and ci == 0),
                            stop=(last_in_bank and ci == len(cs) - 1))
                    ev[f"av_{g}_{u}"] = mm.inc(ins)

            units = [(qsub, h) for qsub in range(2) for h in range(H)]
            nu = len(units)
            for u in range(nu + 5):
                if 0 <= u - 5 < nu:
                    unit_av(u - 5, *units[u - 5])
                if 0 <= u - 3 < nu:
                    unit_tp(u - 3, *units[u - 3])
                if u < nu and u % 2 == 0:
                    pair_scores_t(u)
                    pair_va(u)
                if pending_tail[0] is not None:
                    while pending_tail[0] and pending_tail[0][0][0] == u:
                        pending_tail[0].pop(0)[1]()
                    if not pending_tail[0]:
                        pending_tail[0] = None

            for qsub in range(2):
                avb = [0, 1] if qsub == 0 else [2, 3]
                u_last = qsub * 16 + 15

                @block.vector
                def _(vector, qsub=qsub, u_last=u_last):
                    kb.wait(vector, ac, ev[f"ex_{g}_{u_last}"])
                    kb.wait(vector, mm, ev[f"av_{g}_{u_last}"])
                    ev[f"rc_{g}_{qsub}"] = ve.inc(vector.reciprocal(
                        stats[:, P + 40 + qsub * 16:P + 56 + qsub * 16],
                        stats[:, P + 8 + qsub * 16:P + 24 + qsub * 16]))
                    kb.wait(vector, ve, ev[f"rc_{g}_{qsub}"])
                    ins = None
                    for h in range(H):
                        b_av = avb[h // 8]
                        rcol = P + 40 + qsub * 16 + h
                        ins = vector.tensor_scalar_mul(
                            ao_sb[:, qsub * 1024 + h * DH:qsub * 1024 + (h + 1) * DH],
                            pb[b_av][:, (h % 8) * DH:(h % 8 + 1) * DH],
                            stats[:, rcol:rcol + 1])
                    ev[f"ao_{g}_{qsub}"] = ve.inc(ins)
                kb.bank_set(avb, [(ve, ev[f"ao_{g}_{qsub}"])])

            emit_tsplit_tt(g, 0, ao_sb, f"aot0_{g}",
                           waits=[(ve, ev[f"ao_{g}_0"])], banks=(0, 1),
                           dsth=fmh2, dstl=fml2)
            h_ev2, l_ev2 = emit_tsplit_tt(g, 1, ao_sb, f"aot1_{g}",
                                          waits=[(ve, ev[f"ao_{g}_1"])], banks=(2, 3),
                                          dsth=fmh2, dstl=fml2)


            if not last:
                emit_ln1(g + 1, g_waits=[(mm, ev[f"aot1_{g}_tp1"])])

            o_use = 3 + 6 * g
            _ow0 = [(ac, ev[f"aot0_{g}_hi"]), (ve, ev[f"aot0_{g}_lo"])]
            _ow1 = [(ac, ev[f"aot1_{g}_hi"]), (ve, ev[f"aot1_{g}_lo"])]
            emit_proj_tm(g, o_use, f"op_{g}", h_ev2, l_ev2, [0, 1, 2, 3],
                         tile_waits=[_ow0, _ow0, _ow1, _ow1],
                         srch=fmh2, srcl=fml2)
            pump_loads(o_use + 4)

            if not last:
                emit_n1t(g + 1)

            @block.vector
            def _(vector):
                ins = None
                for ti in range(4):
                    kb.wait(vector, mm, ev[f"op_{g}_mm{ti}"])
                    ins = vector.tensor_copy(
                        sc2[:, ti * 512:(ti + 1) * 512], pb[ti][:])
                ev[f"oev_{g}"] = ve.inc(ins)
            kb.bank_set([0, 1, 2, 3], [(ve, ev[f"oev_{g}"])])

            def _sc2_ap(tt, half):
                return sc2[:, tt * 1024 + half * 512:tt * 1024 + (half + 1) * 512]

            ln2v = emit_ln(g, _sc2_ap, _sc2_ap, f"ln2_{g}",
                           extra_waits=[(ve, ev[f"oev_{g}"])])

            @block.vector
            def _(vector):
                kb.wait(vector, ve, ln2v)
                ins = None
                for tt in range(2):
                    ins = vector.tensor_add(
                        xb_g[:, tt * 1024:(tt + 1) * 1024],
                        xb_g[:, tt * 1024:(tt + 1) * 1024],
                        sc2[:, tt * 1024:(tt + 1) * 1024])
                ev[f"x2_{g}"] = ve.inc(ins)

            def _src_x2(tt, half):
                return xb_g[:, tt * 1024 + half * 512:tt * 1024 + (half + 1) * 512]

            def _out_sc1(tt, half):
                return sc1[:, tt * 1024 + half * 512:tt * 1024 + (half + 1) * 512]

            ln3v = emit_ln(g, _src_x2, _out_sc1, f"ln3_{g}",
                           extra_waits=[(ve, ev[f"x2_{g}"])],
                           norm_waits=([(st, ev[f"store_{g - 1}"])]
                                       if g >= 1 else []))

            if not last:
                emit_kvq(g + 1, 4 + 6 * g)
            else:
                for uid in range(4 + 6 * g, 7 + 6 * g):
                    ev[f"use_mm_{uid}"] = mm.v
                pump_loads(10 + 6 * g)

            h_ev3, l_ev3 = emit_tsplit(g, sc1, f"n3t_{g}", waits=[(ve, ln3v)],
                                       banks=(4, 5, 6, 7),
                                       hi_waits=[(mm, ev[f"op_{g}_mm3"])],
                                       dsth=fmh2, dstl=fml2)

            cos_use = 7 + 6 * g
            emit_proj_tm(g, cos_use, f"re_{g}", h_ev3, l_ev3, [0, 1, 2, 3],
                         srch=fmh2, srcl=fml2, kc_tag=f"n3t_{g}")
            pump_loads(cos_use + 4)

            @block.scalar
            def _(scalar):
                ins = None
                for ti in range(4):
                    kb.wait(scalar, mm, ev[f"re_{g}_mm{ti}"])
                    if not last:
                        kb.wait(scalar, mm, ev[f"n1t_{g + 1}_tp{KC - 2}"])
                    ins = scalar.activation(ao_sb[:, ti * 512:(ti + 1) * 512],
                                            pb[ti][:], AF.Square)
                ev[f"sqre_{g}"] = ac.inc(ins)
            kb.bank_set([0, 1, 2, 3], [(ac, ev[f"sqre_{g}"])])

            emit_proj_tm(g, cos_use + 1, f"im_{g}", h_ev3, l_ev3, [0, 1, 2, 3],
                         srch=fmh2, srcl=fml2, kc_tag=f"n3t_{g}")
            pump_loads(cos_use + 5)

            @block.scalar
            def _(scalar):
                ins = None
                for ti in range(4):
                    kb.wait(scalar, mm, ev[f"im_{g}_mm{ti}"])
                    ins = scalar.activation(sc2[:, ti * 512:(ti + 1) * 512],
                                            pb[ti][:], AF.Square)
                ev[f"sq_{g}"] = ac.inc(ins)
            kb.bank_set([0, 1, 2, 3], [(ac, ev[f"sq_{g}"])])

            Pt = (g % 2) * 128
            bns0, bns1 = Pt + 72, Pt + 84
            agg = Pt + 96
            stdc, rstdc, nmrc, vepc = Pt + 100, Pt + 102, Pt + 104, Pt + 106

            def tail_a(g=g):
                @block.vector
                def _(vector):
                    kb.wait(vector, ac, ev[f"sq_{g}"])
                    ev[f"m2_{g}"] = ve.inc(
                        vector.tensor_add(sc1[:], ao_sb[:], sc2[:]))

                @block.scalar
                def _(scalar):
                    kb.wait(scalar, ve, ev[f"m2_{g}"])
                    ev[f"mag_{g}"] = ac.inc(
                        scalar.activation(ao_sb[:], sc1[:], AF.Sqrt))

            def tail_b(g=g):
                @block.vector
                def _(vector):
                    kb.wait(vector, ac, ev[f"mag_{g}"])
                    vector.bn_stats(stats[:, bns0:bns0 + 6], ao_sb[:, 0:512])
                    vector.bn_stats(stats[:, bns0 + 6:bns0 + 12],
                                    ao_sb[:, 512:1024])
                    vector.bn_stats(stats[:, bns1:bns1 + 6], ao_sb[:, 1024:1536])
                    ev[f"ln4_{g}_bns"] = ve.inc(vector.bn_stats(
                        stats[:, bns1 + 6:bns1 + 12], ao_sb[:, 1536:2048]))
                    kb.wait(vector, ve, ev[f"ln4_{g}_bns"])
                    vector.bn_aggr(stats[:, agg:agg + 2], stats[:, bns0:bns0 + 12])
                    ev[f"ln4_{g}_agg"] = ve.inc(vector.bn_aggr(
                        stats[:, agg + 2:agg + 4], stats[:, bns1:bns1 + 12]))
                    kb.wait(vector, ve, ev[f"ln4_{g}_agg"])
                    vector.tensor_scalar_add(stats[:, vepc:vepc + 1],
                                             stats[:, agg + 1:agg + 2], EPS)
                    ev[f"ln4_{g}_vep"] = ve.inc(vector.tensor_scalar_add(
                        stats[:, vepc + 1:vepc + 2], stats[:, agg + 3:agg + 4],
                        EPS))

                @block.scalar
                def _(scalar):
                    kb.wait(scalar, ve, ev[f"ln4_{g}_vep"])
                    ev[f"ln4_{g}_std"] = ac.inc(scalar.activation(
                        stats[:, stdc:stdc + 2], stats[:, vepc:vepc + 2],
                        AF.Sqrt))

            def tail_c1(g=g):
                @block.vector
                def _(vector):
                    kb.wait(vector, ac, ev[f"ln4_{g}_std"])
                    ev[f"ln4_{g}_rstd"] = ve.inc(vector.reciprocal(
                        stats[:, rstdc:rstdc + 2], stats[:, stdc:stdc + 2]))
                    kb.wait(vector, ve, ev[f"ln4_{g}_rstd"])
                    vector.scalar_tensor_tensor(
                        stats[:, nmrc:nmrc + 1], stats[:, agg:agg + 1], -1.0,
                        stats[:, rstdc:rstdc + 1], op0=ALU.mult, op1=ALU.mult)
                    ev[f"ln4_{g}_nmr"] = ve.inc(vector.scalar_tensor_tensor(
                        stats[:, nmrc + 1:nmrc + 2], stats[:, agg + 2:agg + 3],
                        -1.0, stats[:, rstdc + 1:rstdc + 2],
                        op0=ALU.mult, op1=ALU.mult))
                    kb.wait(vector, ve, ev[f"ln4_{g}_nmr"])
                    ins = None
                    for tt in range(2):
                        for half in range(2):
                            ins = vector.tensor_scalar(
                                sc2[:, tt * 1024 + half * 512:
                                    tt * 1024 + (half + 1) * 512],
                                ao_sb[:, tt * 1024 + half * 512:
                                      tt * 1024 + (half + 1) * 512],
                                stats[:, rstdc + tt:rstdc + tt + 1],
                                stats[:, nmrc + tt:nmrc + tt + 1],
                                op0=ALU.mult, op1=ALU.add)
                    ev[f"ln4_{g}_norm"] = ve.inc(ins)

            def tail_c2(g=g, row0=row0, xb_g=xb_g):
                @block.vector
                def _(vector):
                    kb.wait(vector, ve, ev[f"ln4_{g}_norm"])
                    ins = None
                    for tt in range(2):
                        ins = vector.tensor_add(
                            sc1[:, tt * 1024:(tt + 1) * 1024],
                            xb_g[:, tt * 1024:(tt + 1) * 1024],
                            sc2[:, tt * 1024:(tt + 1) * 1024])
                    ev[f"outtt_{g}"] = ve.inc(ins)

                @block.gpsimd
                def _(gpsimd):
                    kb.wait(gpsimd, ve, ev[f"outtt_{g}"])
                    for tt in range(2):
                        st.inc(gpsimd.dma_start(
                            out=y_d[row0 + tt * 128:row0 + (tt + 1) * 128, :],
                            in_=sc1[:, tt * 1024:(tt + 1) * 1024]), 16)
                    ev[f"store_{g}"] = st.v

                if g + 2 < seqs * nblk:
                    nseq, nj = (g + 2) // nblk, (g + 2) % nblk
                    nrow0 = nseq * nblk * W + nj * W

                    @block.gpsimd
                    def _(gpsimd):
                        kb.wait(gpsimd, ve, ev[f"outtt_{g}"])
                        for tt in range(2):
                            qal[g % 2].inc(gpsimd.dma_start(
                                out=xb_g[:, tt * 1024:(tt + 1) * 1024],
                                in_=x_d[nrow0 + tt * 128:
                                        nrow0 + (tt + 1) * 128, :]), 16)
                        ev[f"xload_{g + 2}"] = qal[g % 2].v

            pending_tail[0] = [(2, tail_a), (6, tail_b), (10, tail_c1),
                               (12, tail_c2)]

        if pending_tail[0] is not None:
            for _, _fn in pending_tail[0]:
                _fn()
            pending_tail[0] = None

        @block.scalar
        def _(scalar):
            scalar.wait_ge(st.sem, st.v)

    return nc


def _host_consts():
    qi = np.arange(128)[:, None]
    ki = np.arange(512)[None, :]
    spans = {0: (0, 384), 1: (128, 512), 2: (256, 384), 3: (256, 512)}
    cols = []
    for variant in range(4):
        qsub, is_j0 = variant % 2, variant // 2
        gq = qsub * 128 + qi
        band = (ki >= gq + 1) & (ki <= gq + W)
        if is_j0:
            band = band & (ki >= 256)
        a, b = spans[variant]
        cols.append(np.where(band, 0.0, NEG)[:, a:b])
    maskc = np.concatenate(cols, axis=1).astype(ml_dtypes.bfloat16)
    assert maskc.shape == (128, 1152)
    eyebf = np.eye(128).astype(ml_dtypes.bfloat16)
    eyef = np.eye(128, dtype=np.float32)
    consts = np.zeros((128, 8), dtype=np.float32)
    consts[:, 0] = EPS
    consts[:, 1] = np.pi / 2
    consts[:, 3] = 1.0
    return maskc, eyebf, eyef, consts


def make_in_map(x_shard, wq, wk, wv, wo, tha, thb, pha, phb):
    maskc, eyebf, eyef, consts = _host_consts()
    return {
        "x": np.ascontiguousarray(x_shard.reshape(-1, D)),
        "wqT": np.ascontiguousarray(wq.T),
        "wkT": np.ascontiguousarray(wk.T),
        "wvT": np.ascontiguousarray(wv.T),
        "woT": np.ascontiguousarray(wo.T),
        "tha": np.ascontiguousarray(tha),
        "thb": np.ascontiguousarray(thb),
        "pha": np.ascontiguousarray(pha),
        "phb": np.ascontiguousarray(phb),
        "maskc": maskc, "eyebf": eyebf, "eyef": eyef, "consts": consts,
    }


def kernel(L0_attn_x, L0_attn_wq, L0_attn_wk, L0_attn_wv, L0_attn_wo,
           L0_ffn_theta, L0_ffn_phi, L0_ffn_theta_noise, L0_ffn_phi_noise):
    x = np.asarray(L0_attn_x, dtype=np.float32)
    wq = np.asarray(L0_attn_wq, dtype=np.float32)
    wk = np.asarray(L0_attn_wk, dtype=np.float32)
    wv = np.asarray(L0_attn_wv, dtype=np.float32)
    wo = np.asarray(L0_attn_wo, dtype=np.float32)
    tha = np.asarray(L0_ffn_theta, dtype=np.float32)
    thb = np.asarray(L0_ffn_theta_noise, dtype=np.float32)
    pha = np.asarray(L0_ffn_phi, dtype=np.float32)
    phb = np.asarray(L0_ffn_phi_noise, dtype=np.float32)

    n_cores, seqs, nblk = 8, 2, 8
    nc = build_nc(seqs=seqs, nblk=nblk)
    in_maps = [make_in_map(x[c * seqs:(c + 1) * seqs], wq, wk, wv, wo,
                           tha, thb, pha, phb)
               for c in range(n_cores)]
    res = run_bass_kernel_spmd(nc, in_maps, core_ids=list(range(n_cores)))
    y = np.concatenate(
        [np.asarray(r["y"]).reshape(seqs, nblk * W, D) for r in res.results], axis=0)
    return y
